# version 08
# baseline (speedup 1.0000x reference)
import numpy as np
from contextlib import ExitStack

import concourse.bass as bass
import concourse.mybir as mybir
from concourse.bass_utils import run_bass_kernel_spmd

F32 = mybir.dt.float32
BF16 = mybir.dt.bfloat16
FP8 = mybir.dt.float8e4
AX = mybir.AxisListType.X
OP = mybir.AluOpType
ACT = mybir.ActivationFunctionType

D = 1024
NTOK = 8192
SEQ = 4096
NCH = 8
EPS = 1e-5

_ENG_ATTR = {"sp": "sync", "pe": "tensor", "act": "scalar", "dve": "vector",
             "pool": "gpsimd"}


class _Op:
    __slots__ = ("eng", "fn", "reads", "writes", "dma", "deps", "need_inc",
                 "inc_idx", "idx", "tag", "slot", "use")

    def __init__(self, idx, eng, fn, reads, writes, dma, tag):
        self.idx = idx
        self.eng = eng
        self.fn = fn
        self.reads = reads
        self.writes = writes
        self.dma = dma
        self.deps = set()
        self.need_inc = dma
        self.inc_idx = None
        self.tag = tag


class Build:
    def __init__(self, nc, stack):
        self.nc = nc
        self.stack = stack
        self.ops = []
        self._last_w = {}
        self._readers = {}

    def op(self, eng, fn, reads=(), writes=(), dma=False, tag=""):
        o = _Op(len(self.ops), eng, fn, tuple(reads), tuple(writes), dma, tag)
        for buf in o.reads:
            wi = self._last_w.get(buf)
            if wi is not None:
                o.deps.add(wi)
            self._readers.setdefault(buf, []).append(o.idx)
        for buf in o.writes:
            wi = self._last_w.get(buf)
            if wi is not None:
                o.deps.add(wi)
            for r in self._readers.get(buf, ()):
                o.deps.add(r)
            self._readers[buf] = []
            self._last_w[buf] = o.idx
        o.deps.discard(o.idx)
        self.ops.append(o)
        return o

    def dma(self, queue, out, in_, reads, writes, tag=""):
        def fn(eng):
            return eng.dma_start(out=out, in_=in_)
        return self.op(queue, fn, reads=reads, writes=writes, dma=True, tag=tag)

    def barrier(self, tag="bar"):
        keys = sorted(set(self._last_w) | set(self._readers))
        return self.op("dve", lambda e: e.engine_nop(), reads=keys,
                       writes=keys, tag=tag)

    def finalize(self):
        nc = self.nc
        ops = self.ops
        for o in ops:
            for dd in o.deps:
                a = ops[dd]
                if a.dma:
                    continue
                if a.eng != o.eng:
                    a.need_inc = True
        NSLOT = {"sp": 12, "pool": 8, "act": 4, "dve": 4, "pe": 4}
        dcount = {e: 0 for e in _ENG_ATTR}
        csem_n = {e: 0 for e in _ENG_ATTR}
        slot_use = {}
        for o in ops:
            if o.dma:
                i = dcount[o.eng]
                dcount[o.eng] += 1
                o.slot = (o.eng, i % NSLOT[o.eng])
                o.use = i // NSLOT[o.eng] + 1
                slot_use[o.slot] = o.use
            elif o.need_inc:
                csem_n[o.eng] += 1
                o.inc_idx = csem_n[o.eng]
        csem, dsem = {}, {}
        for e in _ENG_ATTR:
            if csem_n[e]:
                csem[e] = self.stack.enter_context(nc.semaphore(f"c_{e}"))
        for sl in slot_use:
            dsem[sl] = self.stack.enter_context(
                nc.semaphore(f"d_{sl[0]}{sl[1]}"))
        per_eng = {e: [] for e in _ENG_ATTR}
        for o in ops:
            per_eng[o.eng].append(o)
        block = self.stack.enter_context(nc.Block())

        def emit(eng_name, eng_obj):
            water = {}
            covered = -1
            prev_idx = -1

            def wait(key, val):
                if water.get(key, 0) >= val:
                    return
                sem = dsem[key[1]] if key[0] == "d" else csem[key[1]]
                eng_obj.wait_ge(sem, val)
                water[key] = val

            for o in per_eng[eng_name]:
                req = {}
                need_drain = False
                for dd in sorted(o.deps):
                    a = ops[dd]
                    if a.eng == o.eng and not a.dma:
                        if dd > covered:
                            need_drain = True
                        continue
                    key = ("d", a.slot) if a.dma else ("c", a.eng)
                    val = 16 * a.use if a.dma else a.inc_idx
                    if req.get(key, 0) < val:
                        req[key] = val
                if o.dma and o.use >= 2:
                    k = ("d", o.slot)
                    if req.get(k, 0) < 16 * (o.use - 1):
                        req[k] = max(req.get(k, 0), 16 * (o.use - 1))
                if need_drain:
                    eng_obj.drain()
                    covered = prev_idx
                for key, val in sorted(req.items(), key=str):
                    wait(key, val)
                ins = o.fn(eng_obj)
                prev_idx = o.idx
                if o.dma:
                    ins.then_inc(dsem[o.slot], 16)
                elif o.need_inc:
                    assert ins is not None, f"op {o.tag} needs inc, returned None"
                    ins.then_inc(csem[o.eng], 1)
            return water

        @block.sync
        def _(sync):
            water = emit("sp", sync)
            for sl, use in sorted(slot_use.items(), key=str):
                if water.get(("d", sl), 0) < 16 * use:
                    sync.wait_ge(dsem[sl], 16 * use)
            for e in sorted(_ENG_ATTR):
                n = csem_n[e]
                if e != "sp" and n and water.get(("c", e), 0) < n:
                    sync.wait_ge(csem[e], n)

        @block.tensor
        def _(tensor):
            emit("pe", tensor)

        @block.scalar
        def _(scalar):
            emit("act", scalar)

        @block.vector
        def _(vector):
            emit("dve", vector)

        @block.gpsimd
        def _(gpsimd):
            emit("pool", gpsimd)



class Prog:
    def __init__(self):
        self.nc = bass.Bass("TRN2", target_bir_lowering=False, debug=False)
        self.stack = ExitStack()
        self.b = Build(self.nc, self.stack)
        self.sb = {}

    def sbuf(self, name, shape, dtype=F32):
        t = self.stack.enter_context(self.nc.sbuf_tensor(name, list(shape), dtype))
        self.sb[name] = t
        return t

    def sbuf_alias(self, name, shape, dtype, offset):
        t = self.nc.alloc_sbuf_tensor_at(name, list(shape), dtype,
                                         offset=offset)
        self.sb[name] = t
        return t

    def psum(self, name, shape=(128, 512)):
        t = self.stack.enter_context(self.nc.psum_tensor(name, list(shape), F32))
        self.sb[name] = t
        return t

    def dram_in(self, name, shape, dtype=F32):
        return self.nc.dram_tensor(name, list(shape), dtype, kind="ExternalInput").ap()

    def dram_out(self, name, shape, dtype=F32):
        return self.nc.dram_tensor(name, list(shape), dtype, kind="ExternalOutput").ap()

    def dram_tmp(self, name, shape, dtype=F32):
        return self.nc.dram_tensor(name, list(shape), dtype, kind="Internal").ap()

    def cval(self, value, parts=1):
        if not hasattr(self, "_cvals"):
            self._cvals = {}
            self._cval_tile = self.sbuf("cvals", [128, 32], F32)
        if value not in self._cvals:
            j = len(self._cvals)
            assert j < 32
            self._cvals[value] = j
            sl = self._cval_tile[:, j:j + 1]
            self.b.op("dve", lambda e, sl=sl, v=value: e.memset(sl, v),
                      writes=["cvals"], tag="cval")
        j = self._cvals[value]
        return self._cval_tile[0:parts, j:j + 1]


def setup_consts(p):
    b = p.b
    ones_k = p.sbuf("ones_k", [128, 1], F32)
    ones_k16 = p.sbuf("ones_k16", [128, 1], BF16)
    ones_m = p.sbuf("ones_m", [1, 128], F32)
    ident = p.sbuf("ident", [128, 128], F32)
    tri = p.sbuf("tri", [128, 128], F32)
    iot = p.sbuf("iot", [128, 128], F32)
    b.op("dve", lambda e: e.memset(ones_k[:], 1.0), writes=["ones_k"])
    b.op("dve", lambda e: e.memset(ones_k16[:], 1.0), writes=["ones_k16"])
    b.op("dve", lambda e: e.memset(ones_m[:], 1.0), writes=["ones_m"])
    b.op("dve", lambda e: e.memset(p.sb["ones_t"][:], 1.0), writes=["ones_t"])
    b.op("pool", lambda e: e.iota(iot[:], pattern=[[-1, 128]], base=0,
                                  channel_multiplier=1,
                                  allow_small_or_imprecise_dtypes=True),
         writes=["iot"], tag="iota_ident")
    b.op("dve", lambda e: e.tensor_single_scalar(ident[:], iot[:], 0.0,
                                                 OP.is_equal),
         reads=["iot"], writes=["ident"], tag="ident")
    b.op("pool", lambda e: e.iota(iot[:], pattern=[[1, 128]], base=0,
                                  channel_multiplier=-1,
                                  allow_small_or_imprecise_dtypes=True),
         reads=["ident"], writes=["iot"], tag="iota_tri")
    b.op("dve", lambda e: e.tensor_single_scalar(tri[:], iot[:], 0.0,
                                                 OP.is_ge),
         reads=["iot"], writes=["tri"], tag="tri")



def ck(key, nch=NCH):
    return [key + "#%d" % c for c in range(nch)]


def load_fm(p, queue, dst, dst_key, src_hbm, t0, tw, nch=NCH, tag="ld",
            chunked=True):
    src = src_hbm.rearrange("(c p) t -> p c t", p=128)[:, :, t0:t0 + tw]
    d2 = dst.rearrange("p (c t) -> p c t", c=nch)
    wk = ck(dst_key, nch) if chunked else [dst_key]
    p.b.dma(queue, d2, src, reads=[], writes=wk, tag=tag)


def store_fm(p, queue, dst_hbm, dst_key, src, src_key, t0, tw, nch=NCH,
             tag="st", chunked=True):
    dst = dst_hbm.rearrange("(c p) t -> p c t", p=128)[:, :, t0:t0 + tw]
    s2 = src.rearrange("p (c t) -> p c t", c=nch)
    rk = ck(src_key, nch) if chunked else [src_key]
    p.b.dma(queue, dst, s2, reads=rk, writes=[dst_key], tag=tag)


def load_vec(p, queue, dst, dst_key, src_hbm, tag="ldv"):
    p.b.dma(queue, dst, src_hbm[:], reads=[], writes=[dst_key], tag=tag)


def load_w(p, queue, dst, dst_key, src_hbm, rows, cols, col0=0, ncols=None,
           tag="ldw"):
    if ncols is None:
        ncols = cols
    src = src_hbm[:, col0:col0 + ncols].rearrange("(c p) o -> p c o", p=128)
    d2 = dst.rearrange("p (c o) -> p c o", c=rows // 128)
    p.b.dma(queue, d2, src, reads=[], writes=[dst_key], tag=tag)



def emit_ln(p, x, xk, tw, banks, stat, statk, *, nch=NCH, nfeat=D,
            x_bf16=False, scale_ap=None, scale_key=None, sq=None, sqk=None,
            sq_slots=None, tag="ln"):
    b = p.b
    ones_k = p.sb["ones_k16"] if x_bf16 else p.sb["ones_k"]
    ones_kk = "ones_k16" if x_bf16 else "ones_k"
    ones_f = p.sb["ones_k"]
    ones_m = p.sb["ones_m"]
    if sq is None:
        sq = p.sb["sq_t"][:]
        sqk = "sq_t"

    def mm_s(eng):
        ins = None
        for c in range(nch):
            ins = eng.matmul(banks["sa"], ones_k[:], x[:, c * tw:(c + 1) * tw],
                             start=(c == 0), stop=(c == nch - 1))
        return ins

    b.op("pe", mm_s, reads=ck(xk, nch) + [ones_kk], writes=[banks["ka"]],
         tag=tag + "_ss")
    if sq_slots is None:
        sqks = [sqk] if sqk == "sq_t" else ck(sqk, nch)
        b.op("act", lambda e: e.square(sq[:, :nch * tw], x[:, :nch * tw]),
             reads=ck(xk, nch), writes=sqks, tag=tag + "_sq")

        def mm_q(eng):
            ins = None
            for c in range(nch):
                ins = eng.matmul(banks["sb"], ones_f[:],
                                 sq[:, c * tw:(c + 1) * tw],
                                 start=(c == 0), stop=(c == nch - 1))
            return ins

        b.op("pe", mm_q, reads=sqks + ["ones_k"], writes=[banks["kb"]],
             tag=tag + "_sq2")
    else:
        nsl = sq_slots
        for c in range(nch):
            sl = sq[:, (c % nsl) * tw:(c % nsl + 1) * tw]
            b.op("act", lambda e, sl=sl, c=c: e.square(
                     sl, x[:, c * tw:(c + 1) * tw]),
                 reads=[xk + "#%d" % c], writes=[sqk + "#%d" % (c % nsl)],
                 tag=tag + "_sq%d" % c)
        for c in range(nch):
            sl = sq[:, (c % nsl) * tw:(c % nsl + 1) * tw]
            b.op("pe", lambda e, sl=sl, c=c: e.matmul(
                     banks["sb"], ones_f[:], sl,
                     start=(c == 0), stop=(c == nch - 1)),
                 reads=[sqk + "#%d" % (c % nsl), "ones_k"],
                 writes=[banks["kb"]], tag=tag + "_sq2_%d" % c)
    r = stat[0:1, 0:tw]
    mur = stat[0:1, tw:2 * tw]
    mu = stat[0:1, 2 * tw:3 * tw]
    v = stat[0:1, 3 * tw:4 * tw]
    m2 = stat[0:1, 4 * tw:5 * tw]
    b.op("act", lambda e: e.mul(mu, banks["sa"], 1.0 / nfeat),
         reads=[banks["ka"]], writes=[statk], tag=tag + "_mu")
    b.op("act", lambda e: e.mul(v, banks["sb"], 1.0 / nfeat),
         reads=[banks["kb"]], writes=[statk], tag=tag + "_ex2")
    b.op("dve", lambda e: e.tensor_mul(m2, mu, mu), reads=[statk],
         writes=[statk], tag=tag + "_mu2")
    b.op("dve", lambda e: e.tensor_sub(v, v, m2), reads=[statk],
         writes=[statk], tag=tag + "_var")
    epsap = p.cval(EPS)
    b.op("act", lambda e: e.activation(m2, v, ACT.Sqrt, bias=epsap),
         reads=[statk, "cvals"], writes=[statk], tag=tag + "_sd")
    b.op("dve", lambda e: e.reciprocal(r, m2),
         reads=[statk], writes=[statk], tag=tag + "_r")
    b.op("dve", lambda e: e.tensor_mul(mur, mu, r), reads=[statk],
         writes=[statk], tag=tag + "_mur")
    if scale_ap is not None:
        rm2 = stat[0:1, 0:2 * tw]
        b.op("dve", lambda e: e.tensor_scalar(rm2, rm2, scale_ap, None,
                                              OP.mult),
             reads=[statk, scale_key], writes=[statk], tag=tag + "_fs")
    b.op("pe", lambda e: e.matmul(banks["br"], ones_m[:], r, start=True,
                                  stop=True),
         reads=[statk, "ones_m"], writes=[banks["kr"]], tag=tag + "_bcr")
    b.op("pe", lambda e: e.matmul(banks["bm"], ones_m[:], mur, start=True,
                                  stop=True),
         reads=[statk, "ones_m"], writes=[banks["km"]], tag=tag + "_bcm")


def ln_apply(p, x, xk, ln, lnk, tw, banks, nch=NCH, tag="lnap"):
    b = p.b
    for c in range(nch):
        xs = x[:, c * tw:(c + 1) * tw]
        ls = ln[:, c * tw:(c + 1) * tw]
        b.op("dve", lambda e, xs=xs, ls=ls: e.tensor_mul(ls, xs, banks["br"]),
             reads=[xk + "#%d" % c, banks["kr"]], writes=[lnk + "#%d" % c],
             tag=tag + "_m%d" % c)
    for c in range(nch):
        ls = ln[:, c * tw:(c + 1) * tw]
        b.op("dve", lambda e, ls=ls: e.tensor_sub(ls, ls, banks["bm"]),
             reads=[lnk + "#%d" % c, banks["km"]], writes=[lnk + "#%d" % c],
             tag=tag + "_s%d" % c)


def ln_apply_residual(p, y, yk, x, xk, out, outk, tw, banks, nch=NCH,
                      tag="lnres"):
    b = p.b
    for c in range(nch):
        ys = y[:, c * tw:(c + 1) * tw]
        os_ = out[:, c * tw:(c + 1) * tw]
        b.op("dve", lambda e, ys=ys, os_=os_: e.tensor_mul(os_, ys, banks["br"]),
             reads=[yk + "#%d" % c, banks["kr"]], writes=[outk + "#%d" % c],
             tag=tag + "_m%d" % c)
    for c in range(nch):
        os_ = out[:, c * tw:(c + 1) * tw]
        b.op("dve", lambda e, os_=os_: e.tensor_sub(os_, os_, banks["bm"]),
             reads=[outk + "#%d" % c, banks["km"]], writes=[outk + "#%d" % c],
             tag=tag + "_s%d" % c)
    for c in range(nch):
        xs = x[:, c * tw:(c + 1) * tw]
        os_ = out[:, c * tw:(c + 1) * tw]
        b.op("dve", lambda e, os_=os_, xs=xs: e.tensor_add(os_, os_, xs),
             reads=[outk + "#%d" % c, xk + "#%d" % c],
             writes=[outk + "#%d" % c], tag=tag + "_a%d" % c)


def std_banks(p, tw):
    ps = [p.sb[f"ps{i}"] for i in range(8)]
    return dict(sa=ps[6][0:1, 0:tw], sb=ps[7][0:1, 0:tw],
                br=ps[4][:, 0:tw], bm=ps[5][:, 0:tw],
                ka="ps6", kb="ps7", kr="ps4", km="ps5")



def _prep_weights(inputs):
    w = {}
    T = lambda a: np.ascontiguousarray(np.asarray(a).T)
    V = lambda v: np.ascontiguousarray(np.asarray(v).reshape(-1, 128).T)
    w["w_qkv_t"] = T(inputs["L0_attn_w_qkv"])
    w["b_qkv"] = V(inputs["L0_attn_b_qkv"])
    w["b_v_row"] = np.ascontiguousarray(
        np.asarray(inputs["L0_attn_b_qkv"])[2048:].reshape(1, 1024))
    w["w_g1_t"] = T(inputs["L0_attn_w_g1"])
    w["b_g1"] = np.ascontiguousarray(
        np.asarray(inputs["L0_attn_b_g1"]).reshape(64, 1))
    w["w_g2_t"] = T(inputs["L0_attn_w_g2"])
    w["b_g2"] = V(inputs["L0_attn_b_g2"])
    w["w_out_t"] = T(inputs["L0_attn_w_out"])
    w["b_out"] = V(inputs["L0_attn_b_out"])
    w["ln_g"] = V(inputs["L0_attn_ln_g"])
    w["ln_b"] = V(inputs["L0_attn_ln_b"])
    w["ffn_k"] = np.asarray(inputs["L0_ffn_kernel"])
    w["ffn_sin"] = np.asarray(inputs["L0_ffn_input_scale"])
    w["ffn_sk"] = np.asarray(inputs["L0_ffn_kernel_scale"])
    w["ffn_bias"] = V(inputs["L0_ffn_bias"])
    w["wq_t"] = T(inputs["L1_attn_Wq"])
    w["wk_t"] = T(inputs["L1_attn_Wk"])
    w["wv_t"] = T(inputs["L1_attn_Wv"])
    w["wf_t"] = T(inputs["L1_attn_Wf"])
    w["g_w"] = V(inputs["L1_attn_g_w"])
    w["wo_t"] = T(inputs["L1_attn_Wo"])
    w["wfc_t"] = T(inputs["L1_ffn_W_fc"])
    w["b_fc"] = V(inputs["L1_ffn_b_fc"])
    w["afc_t"] = T(inputs["L1_ffn_A_fc"])
    w["bfc_t"] = T(inputs["L1_ffn_B_fc"])
    w["wpr_t"] = T(inputs["L1_ffn_W_proj"])
    w["b_pr"] = V(inputs["L1_ffn_b_proj"])
    w["apr_t"] = T(inputs["L1_ffn_A_proj"])
    w["bpr_t"] = T(inputs["L1_ffn_B_proj"])
    return w


def _prep_x(inputs):
    x = np.asarray(inputs["L0_attn_x"])
    xs = []
    for c in range(8):
        xc = np.concatenate([x[2 * c], x[2 * c + 1]], axis=0)
        xs.append(np.ascontiguousarray(xc.T))
    return xs


def kernel(**inputs):
    w = _prep_weights(inputs)
    xs = _prep_x(inputs)
    p = build_program()
    in_maps = []
    for c in range(8):
        m = dict(w)
        m["x_fm"] = xs[c]
        in_maps.append(m)
    with p.stack:
        res = run_bass_kernel_spmd(p.nc, in_maps, core_ids=list(range(8)))
    outs = []
    for c in range(8):
        y = res.results[c]["y_fm"]
        outs.append(y[:, :SEQ].T)
        outs.append(y[:, SEQ:].T)
    return np.ascontiguousarray(np.stack(outs, axis=0).astype(np.float32))



def alloc_globals(p):
    for i in range(8):
        p.psum(f"ps{i}")
    p.off = {}
    rup = lambda a: (a + 31) & ~31
    for i in range(3):
        p.off[f"t{i}"] = rup(p.nc._sbuf_addr_for_side(None))
        p.sbuf(f"t{i}", [128, 4096], F32)
    p.sbuf("sq_t", [128, 4096], F32)
    p.off["wbig"] = rup(p.nc._sbuf_addr_for_side(None))
    p.sbuf("wbig", [128, 16384], F32)
    p.sbuf("u0", [128, 4096], BF16)
    p.sbuf_alias("hb", [128, 4096], BF16, p.off["wbig"])
    p.sbuf_alias("lb", [128, 4096], BF16, p.off["wbig"] + 8192)
    p.sbuf_alias("mb", [128, 4096], BF16, p.off["wbig"] + 16384)
    p.sbuf_alias("wq16", [128, 8192], BF16, p.off["wbig"] + 24576)
    p.sbuf_alias("d0w16", [128, 8192], BF16, p.off["t1"])
    p.sbuf_alias("d2h", [128, 16384], BF16, p.off["wbig"])
    p.sbuf_alias("d2l", [128, 16384], BF16, p.off["wbig"] + 32768)
    p.sbuf_alias("wq16d", [128, 8192], BF16, p.off["t0"])
    p.sbuf("kq8", [128, 8192], FP8)
    p.sbuf("q8t", [128, 4096], FP8)
    p.sbuf("stat", [1, 5 * 512], F32)
    p.sbuf("ones_t", [128, 256], F32)
    p.sbuf("c_mx", [128, 4], F32)
    p.sbuf("c_fr", [1, 256], F32)
    p.sbuf("c_sc", [128, 256], F32)
    p.sbuf("c_asb", [128, 256], F32)
    p.sbuf("c_utm", [128, 256], F32)
    setup_consts(p)


def build_program(phases=("A", "B", "C", "D")):
    p = Prog()
    alloc_globals(p)
    x_in = p.dram_in("x_fm", [D, NTOK], F32)
    y_out = p.dram_out("y_fm", [D, NTOK], F32)
    s1 = p.dram_tmp("s1", [D, NTOK], F32)
    s2 = p.dram_tmp("s2", [D, NTOK], F32)
    scr = [(s1, "s1"), (s2, "s2")]
    chain = [ph for ph in ("A", "B", "C", "D") if ph in phases]
    emits = {"A": emit_phase_a, "B": emit_phase_b, "C": emit_phase_c,
             "D": emit_phase_d}
    cur = (x_in, "x_fm")
    for i, ph in enumerate(chain):
        dst = (y_out, "y_fm") if i == len(chain) - 1 else scr[i % 2]
        emits[ph](p, cur, dst)
        cur = dst
    p.b.finalize()
    return p



def emit_phase_a(p, src, dst):
    src_hbm, _ = src
    dst_hbm, dst_key = dst
    b = p.b
    TW = 256
    nt = SEQ // TW
    nsub = TW // 128
    ps = [p.sb[f"ps{i}"] for i in range(8)]
    banks = std_banks(p, TW)
    t = [p.sb[f"t{i}"] for i in range(3)]
    sq_hi = p.sb["sq_t"][:, 2048:4096]
    ek_t, ekk = t[2][:, 0:2048], "a_ek"
    ektm, ekmk = t[2][:, 2048:4096], "a_ektm"
    v_tm, vtmk = sq_hi, "sq_hi"
    q_t, qk = t[2][:, 0:2048], "a_q"
    ao_t, aok = t[2][:, 2048:4096], "a_ao"
    lno_t, lnok = sq_hi, "sq_hi"
    stat = p.sb["stat"]
    wbig = p.sb["wbig"]
    ident = p.sb["ident"]
    ones_m = p.sb["ones_m"]

    bqkv = p.sbuf("a_bqkv", [128, 24], F32)
    load_vec(p, "sp", bqkv[:], "a_bqkv", p.dram_in("b_qkv", [128, 24], F32))
    bvrow = p.sbuf("a_bvrow", [1, D], F32)
    load_vec(p, "sp", bvrow[:], "a_bvrow", p.dram_in("b_v_row", [1, D], F32))
    wg1 = p.sbuf("a_wg1", [128, NCH * 64], F32)
    load_w(p, "sp", wg1[:], "a_wg1", p.dram_in("w_g1_t", [D, 64], F32), D, 64)
    bg1 = p.sbuf("a_bg1", [64, 1], F32)
    load_vec(p, "sp", bg1[:], "a_bg1", p.dram_in("b_g1", [64, 1], F32))
    wg2 = p.sbuf("a_wg2", [64, D], F32)
    load_vec(p, "sp", wg2[:], "a_wg2", p.dram_in("w_g2_t", [64, D], F32))
    bg2 = p.sbuf("a_bg2", [128, NCH], F32)
    load_vec(p, "sp", bg2[:], "a_bg2", p.dram_in("b_g2", [128, NCH], F32))
    bout = p.sbuf("a_bout", [128, NCH], F32)
    load_vec(p, "sp", bout[:], "a_bout", p.dram_in("b_out", [128, NCH], F32))
    lng = p.sbuf("a_lng", [128, NCH], F32)
    load_vec(p, "sp", lng[:], "a_lng", p.dram_in("ln_g", [128, NCH], F32))
    lnb = p.sbuf("a_lnb", [128, NCH], F32)
    load_vec(p, "sp", lnb[:], "a_lnb", p.dram_in("ln_b", [128, NCH], F32))
    sacc = p.sbuf("a_sacc", [128, NCH], F32)
    tacc = p.sbuf("a_tacc", [128, NCH], F32)
    kv_sb = p.sbuf("a_kv", [128, NCH * 64], F32)
    srec = p.sbuf("a_srec", [128, NCH], F32)
    gh_t = p.sbuf("a_gh", [64, TW], F32)
    gtmp = p.sbuf("a_gtmp", [128, TW], F32)
    wqkv_h = p.dram_in("w_qkv_t", [D, 3072], F32)
    w_out_h = p.dram_in("w_out_t", [D, D], F32)

    def ptile(i, it):
        par = it % 2
        return t[i][:, par * 2048:(par + 1) * 2048], "a_t%d_%d" % (i, par)

    for sq_ in range(2):
        base = sq_ * SEQ
        b.op("dve", lambda e: e.memset(sacc[:], 0.0), writes=["a_sacc"],
             tag="a_z")
        load_w(p, "sp", wbig[:, 0:NCH * D], "wbig", wqkv_h, D, 3072,
               col0=1024, ncols=D, tag="a_ldk")
        load_w(p, "sp", wbig[:, NCH * D:2 * NCH * D], "wbig", wqkv_h, D, 3072,
               col0=2048, ncols=D, tag="a_ldv")

        for it in range(nt):
            t0 = base + it * TW
            x_t, xk = ptile(0, it)
            ln_t, lnk = ptile(1, it)
            load_fm(p, "sp", x_t, xk, src_hbm, t0, TW, tag="a1_ldx%d" % it)
            emit_ln(p, x_t, xk, TW, banks, stat, "stat", tag="a1_ln%d" % it)
            ln_apply(p, x_t, xk, ln_t, lnk, TW, banks, tag="a1_lnap%d" % it)
            for o in range(NCH):
                bank = ps[o % 2]
                bkey = "ps%d" % (o % 2)

                def mm(eng, o=o, bank=bank, ln_t=ln_t):
                    ins = None
                    for c in range(NCH):
                        col = c * D + o * 128
                        ins = eng.matmul(bank[:, 0:TW], wbig[:, col:col + 128],
                                         ln_t[:, c * TW:(c + 1) * TW],
                                         start=(c == 0), stop=(c == NCH - 1))
                    return ins

                b.op("pe", mm, reads=["wbig"] + ck(lnk), writes=[bkey],
                     tag="a1_kmm%d_%d" % (it, o))
                eksl = ek_t[:, o * TW:(o + 1) * TW]
                b.op("act", lambda e, eksl=eksl, bank=bank, o=o: e.activation(
                         eksl, bank[:, 0:TW], ACT.Exp,
                         bias=bqkv[:, 8 + o:9 + o],
                         accum_out=tacc[:, o:o + 1]),
                     reads=[bkey, "a_bqkv"],
                     writes=[ekk + "#%d" % o, "a_tacc"],
                     tag="a1_exp%d_%d" % (it, o))
                b.op("dve", lambda e, o=o: e.tensor_add(
                         sacc[:, o:o + 1], sacc[:, o:o + 1], tacc[:, o:o + 1]),
                     reads=["a_tacc", "a_sacc"], writes=["a_sacc"],
                     tag="a1_sacc%d_%d" % (it, o))
            for half_ in range(2):
                for ts in range(nsub):
                    bank = ps[half_]
                    bkey = "ps%d" % half_

                    def mmv(eng, half_=half_, ts=ts, bank=bank, ln_t=ln_t):
                        ins = None
                        for c in range(NCH):
                            ins = eng.matmul(
                                bank[:, 0:512],
                                ln_t[:, c * TW + ts * 128:
                                     c * TW + ts * 128 + 128],
                                wbig[:, NCH * D + c * D + half_ * 512:
                                     NCH * D + c * D + (half_ + 1) * 512],
                                start=(c == 0), stop=False)
                        ins = eng.matmul(
                            bank[:, 0:512], ones_m[:],
                            bvrow[0:1, half_ * 512:(half_ + 1) * 512],
                            start=False, stop=True)
                        return ins

                    b.op("pe", mmv,
                         reads=["wbig", "ones_m", "a_bvrow"] + ck(lnk),
                         writes=[bkey], tag="a1_vmm%d_%d_%d" % (it, half_, ts))
                    b.op("act", lambda e, half_=half_, ts=ts, bank=bank: e.copy(
                             v_tm[:, ts * D + half_ * 512:
                                  ts * D + (half_ + 1) * 512],
                             bank[:, 0:512]),
                         reads=[bkey], writes=[vtmk + "#%d" % (ts * 2 + half_)],
                         tag="a1_vcp%d_%d_%d" % (it, half_, ts))
            for o in range(NCH):
                bank = ps[o % 2]
                bkey = "ps%d" % (o % 2)

                def mmt(eng, o=o, bank=bank):
                    ins = None
                    for ts in range(nsub):
                        ins = eng.matmul(bank[:, ts * 128:(ts + 1) * 128],
                                         ek_t[:, o * TW + ts * 128:
                                              o * TW + (ts + 1) * 128],
                                         ident[:], is_transpose=True,
                                         start=(ts == 0), stop=(ts == nsub - 1))
                    return ins

                b.op("pe", mmt, reads=[ekk + "#%d" % o, "ident"],
                     writes=[bkey], tag="a1_ekT%d_%d" % (it, o))

                def cpt(eng, o=o, bank=bank):
                    ins = None
                    for ts in range(nsub):
                        ins = eng.tensor_copy(
                            ektm[:, ts * D + o * 128: ts * D + o * 128 + 128],
                            bank[:, ts * 128:(ts + 1) * 128])
                    return ins

                b.op("dve", cpt, reads=[bkey], writes=[ekmk + "#%d" % o],
                     tag="a1_ekTc%d_%d" % (it, o))
            for hh in range(NCH):
                bankkv = ps[2 + hh // 4]
                bkey = "ps%d" % (2 + hh // 4)
                off = (hh % 4) * 128

                def mkv(eng, hh=hh, bankkv=bankkv, off=off, it=it):
                    ins = None
                    for ts in range(nsub):
                        first = (it == 0 and hh % 4 == 0 and ts == 0)
                        last = (it == nt - 1 and hh % 4 == 3 and ts == nsub - 1)
                        ins = eng.matmul(
                            bankkv[:, off:off + 128],
                            ektm[:, ts * D + hh * 128: ts * D + hh * 128 + 128],
                            v_tm[:, ts * D + hh * 128: ts * D + hh * 128 + 128],
                            start=first, stop=last)
                    return ins

                vu = [vtmk + "#%d" % (ts * 2 + hh // 4) for ts in range(2)]
                b.op("pe", mkv, reads=[ekmk + "#%d" % hh] + vu, writes=[bkey],
                     tag="a1_kv%d_%d" % (it, hh))
        b.op("dve", lambda e: e.reciprocal(srec[:], sacc[:]),
             reads=["a_sacc"], writes=["a_srec"], tag="a_srec")
        for hh in range(NCH):
            bankkv = ps[2 + hh // 4]
            bkey = "ps%d" % (2 + hh // 4)
            off = (hh % 4) * 128
            b.op("dve", lambda e, hh=hh, bankkv=bankkv, off=off: e.tensor_scalar(
                     kv_sb[0:64, hh * 64:(hh + 1) * 64],
                     bankkv[0:64, off:off + 64], srec[0:64, hh:hh + 1], None,
                     OP.mult),
                 reads=[bkey, "a_srec"], writes=["a_kv"], tag="a_kvn%d_0" % hh)
            b.op("dve", lambda e, hh=hh, bankkv=bankkv, off=off: e.tensor_scalar(
                     kv_sb[64:128, hh * 64:(hh + 1) * 64],
                     bankkv[64:128, off + 64:off + 128], srec[64:128, hh:hh + 1],
                     None, OP.mult),
                 reads=[bkey, "a_srec"], writes=["a_kv"], tag="a_kvn%d_1" % hh)

        load_w(p, "sp", wbig[:, 0:NCH * D], "wbig", wqkv_h, D, 3072, col0=0,
               ncols=D, tag="a_ldq")
        load_w(p, "sp", wbig[:, NCH * D:2 * NCH * D], "wbig", w_out_h, D, D,
               tag="a_ldwo")

        for it in range(nt):
            t0 = base + it * TW
            x_t, xk = ptile(0, it)
            ln_t, lnk = ptile(1, it)
            xo_t, xok = ln_t, lnk
            load_fm(p, "sp", x_t, xk, src_hbm, t0, TW, tag="a2_ldx%d" % it)
            emit_ln(p, x_t, xk, TW, banks, stat, "stat", tag="a2_ln%d" % it)
            ln_apply(p, x_t, xk, ln_t, lnk, TW, banks, tag="a2_lnap%d" % it)
            for o in range(NCH):
                bank = ps[o % 2]
                bkey = "ps%d" % (o % 2)

                def mm(eng, o=o, bank=bank, ln_t=ln_t):
                    ins = None
                    for c in range(NCH):
                        col = c * D + o * 128
                        ins = eng.matmul(bank[:, 0:TW], wbig[:, col:col + 128],
                                         ln_t[:, c * TW:(c + 1) * TW],
                                         start=(c == 0), stop=(c == NCH - 1))
                    return ins

                b.op("pe", mm, reads=["wbig"] + ck(lnk), writes=[bkey],
                     tag="a2_qmm%d_%d" % (it, o))
                qsl = q_t[:, o * TW:(o + 1) * TW]
                b.op("act", lambda e, qsl=qsl, bank=bank, o=o: e.activation(
                         qsl, bank[:, 0:TW], ACT.Silu, bias=bqkv[:, o:o + 1]),
                     reads=[bkey, "a_bqkv"], writes=[qk + "#%d" % o],
                     tag="a2_silu%d_%d" % (it, o))
            for hh in range(NCH):
                bank = ps[2 + (hh % 2)]
                bkey = "ps%d" % (2 + (hh % 2))

                def mma(eng, hh=hh, bank=bank):
                    ins = None
                    for sub in range(2):
                        rb = sub * 64
                        ins = eng.matmul(
                            bank[rb:rb + 64, 0:TW],
                            kv_sb[rb:rb + 64, hh * 64:(hh + 1) * 64],
                            q_t[rb:rb + 64, hh * TW:(hh + 1) * TW],
                            start=True, stop=True)
                    return ins

                b.op("pe", mma, reads=["a_kv", qk + "#%d" % hh],
                     writes=[bkey], tag="a2_ao%d_%d" % (it, hh))
                b.op("act", lambda e, hh=hh, bank=bank: e.copy(
                         ao_t[:, hh * TW:(hh + 1) * TW], bank[:, 0:TW]),
                     reads=[bkey], writes=[aok + "#%d" % hh],
                     tag="a2_aocp%d_%d" % (it, hh))
            emit_ln(p, ao_t, aok, TW, banks, stat, "stat", tag="a2_lno%d" % it)
            for c in range(NCH):
                asl = ao_t[:, c * TW:(c + 1) * TW]
                lsl = lno_t[:, c * TW:(c + 1) * TW]
                b.op("dve", lambda e, asl=asl, lsl=lsl: e.tensor_mul(
                         lsl, asl, banks["br"]),
                     reads=[aok + "#%d" % c, banks["kr"]],
                     writes=[lnok + "#%d" % c],
                     tag="a2_lnoap%d_m%d" % (it, c))
            for c in range(NCH):
                lsl = lno_t[:, c * TW:(c + 1) * TW]
                b.op("dve", lambda e, lsl=lsl: e.tensor_sub(lsl, lsl,
                                                            banks["bm"]),
                     reads=[lnok + "#%d" % c, banks["km"]],
                     writes=[lnok + "#%d" % c],
                     tag="a2_lnoap%d_s%d" % (it, c))
            for c in range(NCH):
                lsl = lno_t[:, c * TW:(c + 1) * TW]
                b.op("dve", lambda e, lsl=lsl, c=c: e.tensor_scalar(
                         lsl, lsl, lng[:, c:c + 1], lnb[:, c:c + 1],
                         OP.mult, OP.add),
                     reads=[lnok + "#%d" % c, "a_lng", "a_lnb"],
                     writes=[lnok + "#%d" % c],
                     tag="a2_lnogb%d_%d" % (it, c))

            def mmg1(eng, ln_t=ln_t):
                ins = None
                for c in range(NCH):
                    ins = eng.matmul(ps[3][0:64, 0:TW],
                                     wg1[:, c * 64:(c + 1) * 64],
                                     ln_t[:, c * TW:(c + 1) * TW],
                                     start=(c == 0), stop=(c == NCH - 1))
                return ins

            b.op("pe", mmg1, reads=["a_wg1"] + ck(lnk), writes=["ps3"],
                 tag="a2_g1%d" % it)
            b.op("act", lambda e: e.activation(gh_t[:], ps[3][0:64, 0:TW],
                                               ACT.Identity, bias=bg1[:, 0:1]),
                 reads=["ps3", "a_bg1"], writes=["a_gh"], tag="a2_g1b%d" % it)
            for o in range(NCH):
                bank = ps[o % 2]
                bkey = "ps%d" % (o % 2)
                b.op("pe", lambda e, o=o, bank=bank: e.matmul(
                         bank[:, 0:TW], wg2[:, o * 128:(o + 1) * 128], gh_t[:],
                         start=True, stop=True),
                     reads=["a_wg2", "a_gh"], writes=[bkey],
                     tag="a2_g2%d_%d" % (it, o))
                b.op("act", lambda e, bank=bank, o=o: e.activation(
                         gtmp[:], bank[:, 0:TW], ACT.Sigmoid,
                         bias=bg2[:, o:o + 1]),
                     reads=[bkey, "a_bg2"], writes=["a_gtmp"],
                     tag="a2_sig%d_%d" % (it, o))
                lsl = lno_t[:, o * TW:(o + 1) * TW]
                b.op("dve", lambda e, lsl=lsl: e.tensor_mul(lsl, gtmp[:], lsl),
                     reads=["a_gtmp", lnok + "#%d" % o],
                     writes=[lnok + "#%d" % o],
                     tag="a2_gmul%d_%d" % (it, o))
            for o in range(NCH):
                bank = ps[2 + (o % 2)]
                bkey = "ps%d" % (2 + (o % 2))

                def mmo(eng, o=o, bank=bank):
                    ins = None
                    for c in range(NCH):
                        col = NCH * D + c * D + o * 128
                        ins = eng.matmul(bank[:, 0:TW], wbig[:, col:col + 128],
                                         lno_t[:, c * TW:(c + 1) * TW],
                                         start=(c == 0), stop=(c == NCH - 1))
                    return ins

                b.op("pe", mmo, reads=["wbig"] + ck(lnok), writes=[bkey],
                     tag="a2_wo%d_%d" % (it, o))
                b.op("act", lambda e, o=o, bank=bank: e.activation(
                         ao_t[:, o * TW:(o + 1) * TW], bank[:, 0:TW],
                         ACT.Identity, bias=bout[:, o:o + 1]),
                     reads=[bkey, "a_bout"], writes=[aok + "#%d" % o],
                     tag="a2_wob%d_%d" % (it, o))
            emit_ln(p, ao_t, aok, TW, banks, stat, "stat",
                    tag="a2_ln2_%d" % it)
            ln_apply_residual(p, ao_t, aok, x_t, xk, xo_t, xok, TW, banks,
                              tag="a2_res%d" % it)
            store_fm(p, "sp", dst_hbm, dst_key, xo_t, xok, t0, TW,
                     tag="a2_st%d" % it)



def emit_phase_b(p, src, dst):
    src_hbm, _ = src
    dst_hbm, dst_key = dst
    b = p.b
    TW = 256
    nt = NTOK // TW
    ps = [p.sb[f"ps{i}"] for i in range(8)]
    banks = std_banks(p, TW)
    t = [p.sb[f"t{i}"] for i in range(3)]
    stat = p.sb["stat"]
    u0 = p.sb["u0"]
    kq8 = p.sb["kq8"]
    q8t = p.sb["q8t"]
    ones_m = p.sb["ones_m"]
    b.barrier("b_bar")

    bias16 = p.sbuf("b_bias16", [128, NCH], BF16)
    load_vec(p, "sp", bias16[:], "b_bias16",
             p.dram_in("ffn_bias", [128, D // 128], BF16))
    bias32 = p.sbuf("b_bias32", [128, NCH], F32)
    b.op("dve", lambda e: e.tensor_copy(bias32[:], bias16[:]),
         reads=["b_bias16"], writes=["b_bias32"], tag="b_b32")
    scal = p.sbuf("b_scal", [1, 4], F32)
    b.op("dve", lambda e: e.memset(scal[:], 0.0), writes=["b_scal"], tag="b_z")
    sin_h = p.dram_in("ffn_sin", [1], F32)
    sk_h = p.dram_in("ffn_sk", [1], F32)
    b.dma("sp", scal[0:1, 0:1], sin_h[:], reads=["b_scal"], writes=["b_scal"])
    b.dma("sp", scal[0:1, 3:4], sk_h[:], reads=["b_scal"], writes=["b_scal"])
    b.op("dve", lambda e: e.tensor_mul(scal[0:1, 1:2], scal[0:1, 0:1],
                                       scal[0:1, 3:4]),
         reads=["b_scal"], writes=["b_scal"], tag="sprod")
    scalb = p.sbuf("b_scalb", [128, 2], F32)
    skb = p.sbuf("b_skb", [128, 1], F32)
    b.op("pe", lambda e: e.matmul(ps[3][:, 0:4], ones_m[:], scal[0:1, 0:4],
                                  start=True, stop=True),
         reads=["b_scal", "ones_m"], writes=["ps3"], tag="bc_scal_mm")
    b.op("dve", lambda e: e.tensor_copy(scalb[:], ps[3][:, 0:2]),
         reads=["ps3"], writes=["b_scalb"], tag="bc_scal_cp")
    b.op("dve", lambda e: e.tensor_copy(skb[:], ps[3][:, 3:4]),
         reads=["ps3"], writes=["b_skb"], tag="bc_sk_cp")
    rsin = p.sbuf("b_rsin", [128, 1], F32)
    rsk = p.sbuf("b_rsk", [128, 1], F32)
    b.op("dve", lambda e: e.reciprocal(rsin[:], scalb[:, 0:1]),
         reads=["b_scalb"], writes=["b_rsin"], tag="rsin")
    b.op("dve", lambda e: e.reciprocal(rsk[:], skb[:]),
         reads=["b_skb"], writes=["b_rsk"], tag="rsk")
    ffnk_h = p.dram_in("ffn_k", [D, D], BF16)
    for hf in range(2):
        load_w(p, "sp", u0[:, 0:NCH * 512], "u0", ffnk_h, D, D,
               col0=hf * 512, ncols=512, tag="b_ldk%d" % hf)
        for c in range(NCH):
            sl_in = u0[:, c * 512:(c + 1) * 512]
            sl_out = kq8[:, c * D + hf * 512: c * D + (hf + 1) * 512]
            b.op("dve", lambda e, a=sl_out, x=sl_in: e.tensor_scalar(
                     a, x, rsk[:, 0:1], None, OP.mult),
                 reads=["u0", "b_rsk"], writes=["kq8#%d_%d" % (hf, c)],
                 tag="b_wq8_%d_%d" % (hf, c))

    for it in range(nt):
        t0_ = it * TW
        par = it % 2
        off = par * NCH * TW
        x_t, xk = t[0][:, off:off + NCH * TW], "b_x%d" % par
        ln_t, lnk = t[1][:, off:off + NCH * TW], "b_ln%d" % par
        xo_t, xok = t[2][:, off:off + NCH * TW], "b_xo%d" % par
        xb = u0[:, off:off + NCH * TW]
        xbk = "b_u%d" % par
        q8 = q8t[:, off:off + NCH * TW]
        q8k = "b_q8%d" % par
        gb = [ps[2 * par], ps[2 * par + 1]]
        gbk = ["ps%d" % (2 * par), "ps%d" % (2 * par + 1)]
        load_fm(p, "sp", x_t, xk, src_hbm, t0_, TW, tag="b_ldx%d" % it)
        emit_ln(p, x_t, xk, TW, banks, stat, "stat", tag="b_ln%d" % it)
        ln_apply(p, x_t, xk, ln_t, lnk, TW, banks, tag="b_lnap%d" % it)
        b.op("dve", lambda e, xb=xb, ln_t=ln_t: e.tensor_copy(xb, ln_t),
             reads=ck(lnk), writes=ck(xbk), tag="b_bf%d" % it)
        b.op("dve", lambda e, q8=q8, xb=xb: e.tensor_scalar(
                 q8, xb, rsin[:, 0:1], None, OP.mult),
             reads=ck(xbk) + ["b_rsin"], writes=ck(q8k),
             tag="b_q8_%d" % it)
        for o in range(NCH):
            bank = gb[o % 2]
            bkey = gbk[o % 2]

            def mm(eng, o=o, bank=bank, q8=q8):
                ins = None
                for c in range(NCH):
                    ins = eng.matmul(bank[:, 0:TW],
                                     kq8[:, c * D + o * 128: c * D + o * 128 + 128],
                                     q8[:, c * TW:(c + 1) * TW],
                                     start=(c == 0), stop=(c == NCH - 1))
                return ins

            b.op("pe", mm,
                 reads=["kq8#%d_%d" % (o // 4, c) for c in range(NCH)]
                 + ck(q8k), writes=[bkey], tag="b_mm%d_%d" % (it, o))
            ysl = xb[:, o * TW:(o + 1) * TW]
            b.op("act", lambda e, ysl=ysl, bank=bank: e.mul(
                     ysl, bank[:, 0:TW], scalb[:, 1:2]),
                 reads=[bkey, "b_scalb"], writes=[xbk + "#%d" % o],
                 tag="b_deq%d_%d" % (it, o))
        for o in range(NCH):
            ysl = xb[:, o * TW:(o + 1) * TW]
            b.op("dve", lambda e, ysl=ysl, o=o: e.tensor_scalar(
                     ysl, ysl, bias32[:, o:o + 1], None, OP.add),
                 reads=[xbk + "#%d" % o, "b_bias32"],
                 writes=[xbk + "#%d" % o],
                 tag="b_bias%d_%d" % (it, o))
        emit_ln(p, xb, xbk, TW, banks, stat, "stat", x_bf16=True,
                tag="b_ln2_%d" % it)
        ln_apply_residual(p, xb, xbk, x_t, xk, xo_t, xok, TW, banks,
                          tag="b_res%d" % it)
        store_fm(p, "sp", dst_hbm, dst_key, xo_t, xok, t0_, TW,
                 tag="b_st%d" % it)




def emit_phase_c(p, src, dst):
    src_hbm, src_key = src
    dst_hbm, dst_key = dst
    b = p.b
    ps = [p.sb[f"ps{i}"] for i in range(8)]
    t = [p.sb[f"t{i}"] for i in range(3)]
    stat = p.sb["stat"]
    wbig = p.sb["wbig"]
    ident = p.sb["ident"]
    tri = p.sb["tri"]
    ones_m = p.sb["ones_m"]
    ones_t = p.sb["ones_t"]
    QS = p.dram_tmp("qs", [D, NTOK], F32)
    KS = p.dram_tmp("ks", [D, NTOOK := NTOK], F32)
    VS = p.dram_tmp("vs", [D, NTOK], F32)
    FS = p.dram_tmp("fs", [D, NTOK], F32)
    OS = p.dram_tmp("os", [D, NTOK], F32)
    wq_h = p.dram_in("wq_t", [D, D], F32)
    wk_h = p.dram_in("wk_t", [D, D], F32)
    wv_h = p.dram_in("wv_t", [D, D], F32)
    wf_h = p.dram_in("wf_t", [D, D], F32)
    wo_h = p.dram_in("wo_t", [D, D], F32)
    gw = p.sbuf("c_gw", [128, NCH], F32)
    load_vec(p, "sp", gw[:], "c_gw", p.dram_in("g_w", [128, NCH], F32))
    z128 = p.cval(0.0, 128)
    n1_128 = p.cval(-1.0, 128)
    one128 = p.cval(1.0, 128)
    eps128 = p.cval(EPS, 128)

    TW = min(256, NTOK)
    nt = NTOK // TW
    banks = std_banks(p, TW)
    b.barrier("c1a_bar")
    load_w(p, "sp", wbig[:, 0:NCH * D], "wbig", wq_h, D, D, tag="c_ldwq")
    load_w(p, "sp", wbig[:, NCH * D:2 * NCH * D], "wbig", wk_h, D, D,
           tag="c_ldwk")

    def c1tile(i, it):
        par = it % 2
        return t[i][:, par * NCH * TW:(par + 1) * NCH * TW], "c1_t%d_%d" % (i, par)

    def proj_pass(pref, w_off, act_kind, out_scratch, it, ln_t, lnk, oslot):
        o_t = t[2][:, oslot * NCH * TW:(oslot + 1) * NCH * TW]
        ok_ = "c1_o%d" % oslot
        for o in range(NCH):
            bank = ps[o % 2]
            bkey = "ps%d" % (o % 2)

            def mm(eng, o=o, bank=bank, w_off=w_off, ln_t=ln_t, TW=TW):
                ins = None
                for c in range(NCH):
                    col = w_off + c * D + o * 128
                    ins = eng.matmul(bank[:, 0:TW], wbig[:, col:col + 128],
                                     ln_t[:, c * TW:(c + 1) * TW],
                                     start=(c == 0), stop=(c == NCH - 1))
                return ins

            b.op("pe", mm, reads=["wbig"] + ck(lnk), writes=[bkey],
                 tag="%s_mm%d_%d" % (pref, it, o))
            osl = o_t[:, o * TW:(o + 1) * TW]
            if act_kind == "silu":
                b.op("act", lambda e, osl=osl, bank=bank, TW=TW: e.activation(
                         osl, bank[:, 0:TW], ACT.Silu, bias=z128),
                     reads=[bkey, "cvals"], writes=[ok_ + "#%d" % o],
                     tag="%s_silu%d_%d" % (pref, it, o))
            else:
                ssl = p.sb["sq_t"][:, 2048 + o * TW:2048 + (o + 1) * TW]
                b.op("act", lambda e, ssl=ssl, bank=bank, TW=TW: e.activation(
                         ssl, bank[:, 0:TW], ACT.Exp, bias=z128,
                         scale=n1_128),
                     reads=[bkey, "cvals"], writes=["sq_hi#%d" % o],
                     tag="%s_e%d_%d" % (pref, it, o))
                b.op("act", lambda e, osl=osl, ssl=ssl: e.activation(
                         osl, ssl, ACT.Ln, bias=one128),
                     reads=["sq_hi#%d" % o, "cvals"],
                     writes=[ok_ + "#%d" % o],
                     tag="%s_l%d_%d" % (pref, it, o))
                b.op("act", lambda e, osl=osl: e.mul(osl, osl, -0.125),
                     reads=[ok_ + "#%d" % o], writes=[ok_ + "#%d" % o],
                     tag="%s_s%d_%d" % (pref, it, o))
        store_fm(p, "sp", out_scratch, pref + "_scr", o_t, ok_, it * TW, TW,
                 tag="%s_st%d" % (pref, it))

    for it in range(nt):
        x_t, xk = c1tile(0, it)
        ln_t, lnk = c1tile(1, it)
        load_fm(p, "sp", x_t, xk, src_hbm, it * TW, TW, tag="c1a_ldx%d" % it)
        emit_ln(p, x_t, xk, TW, banks, stat, "stat", tag="c1a_ln%d" % it)
        ln_apply(p, x_t, xk, ln_t, lnk, TW, banks, tag="c1a_lnap%d" % it)
        proj_pass("cq", 0, "silu", QS, it, ln_t, lnk, 0)
        proj_pass("ck", NCH * D, "silu", KS, it, ln_t, lnk, 1)
    b.barrier("c1b_bar")
    load_w(p, "sp", wbig[:, 0:NCH * D], "wbig", wv_h, D, D, tag="c_ldwv")
    load_w(p, "sp", wbig[:, NCH * D:2 * NCH * D], "wbig", wf_h, D, D,
           tag="c_ldwf")
    for it in range(nt):
        x_t, xk = c1tile(0, it)
        ln_t, lnk = c1tile(1, it)
        load_fm(p, "sp", x_t, xk, src_hbm, it * TW, TW, tag="c1b_ldx%d" % it)
        emit_ln(p, x_t, xk, TW, banks, stat, "stat", tag="c1b_ln%d" % it)
        ln_apply(p, x_t, xk, ln_t, lnk, TW, banks, tag="c1b_lnap%d" % it)
        proj_pass("cv", 0, "silu", VS, it, ln_t, lnk, 0)
        proj_pass("cf", NCH * D, "logsig", FS, it, ln_t, lnk, 1)

    b.barrier("c2_bar")
    C = 128

    def sub(i, lo, n):
        return t[i][:, lo:lo + n]

    q2 = sub(0, 0, 256); k2 = sub(0, 256, 256); v2 = sub(0, 512, 256)
    f2 = sub(0, 768, 256); gam = sub(0, 1024, 256); expg = sub(0, 1280, 256)
    expng = sub(0, 1536, 256); s2v = sub(0, 1792, 256)
    ufm = sub(0, 2048, 256); ktm = sub(0, 2304, 256); utm = sub(0, 2560, 256)
    egtm = sub(0, 2816, 256); qvtm = sub(0, 3072, 256); qvfm = sub(0, 3328, 256)
    qtfm = sub(0, 3584, 256); vtm = sub(0, 3840, 256)
    ovtm = sub(1, 0, 256)
    asb = p.sb["c_asb"][:, 0:128]
    a2sb = p.sb["c_asb"][:, 128:256]
    ofm = sub(1, 512, 1024)
    osil = sub(1, 1536, 1024)
    osq = sub(1, 2560, 1024)
    K0 = {"q2": "c_t0", "k2": "c_t0", "v2": "c_t0", "f2": "c_t0"}
    t0k, t1k, t2k = "t0", "t1", "t2"

    def hload(scr, scrk, dstap, dk, h, tt0, tag):
        src2 = scr.rearrange("(c p) t -> p c t", p=128)[:, 2 * h:2 * h + 2,
                                                        tt0:tt0 + C]
        d2 = dstap.rearrange("p (c t) -> p c t", c=2)
        b.dma("sp", d2, src2, reads=[scrk], writes=[dk], tag=tag)

    for s_ in range(2):
        b.op("dve", lambda e: e.memset(t[2][:], 0.0),
             writes=["c_s1_%d" % hh for hh in range(4)]
             + ["c_s2_%d" % hh for hh in range(4)],
             tag="c2_z%d" % s_)
        for c_ in range(SEQ // C):
            tt0 = s_ * SEQ + c_ * C
            for h in range(4):
                S1 = t[2][:, h * 512: h * 512 + 512]
                S2 = t[2][:, 2048 + h * 512: 2048 + h * 512 + 512]
                tg = "c2_%d_%d_%d" % (s_, c_, h)
                hload(QS, "cq_scr", q2, "c_q2", h, tt0, tg + "_lq")
                hload(KS, "ck_scr", k2, "c_k2", h, tt0, tg + "_lk")
                hload(VS, "cv_scr", v2, "c_v2", h, tt0, tg + "_lv")
                hload(FS, "cf_scr", f2, "c_f2", h, tt0, tg + "_lf")
                def scan(eng):
                    ins = None
                    for blk in range(2):
                        ins = eng.tensor_tensor_scan(
                            gam[:, blk * C:(blk + 1) * C],
                            ones_t[:, 0:C], f2[:, blk * C:(blk + 1) * C],
                            0.0, OP.mult, OP.add)
                    return ins
                b.op("dve", scan, reads=["c_f2", "ones_t"],
                     writes=["c_gam"], tag=tg + "_scan")
                b.op("act", lambda e: e.activation(expg[:], gam[:], ACT.Exp,
                                                   bias=z128),
                     reads=["c_gam", "cvals"], writes=["c_eg"],
                     tag=tg + "_eg")
                b.op("act", lambda e: e.activation(expng[:], gam[:], ACT.Exp,
                                                   bias=z128, scale=n1_128),
                     reads=["c_gam", "cvals"], writes=["c_eng"],
                     tag=tg + "_eng")
                b.op("act", lambda e: e.activation(s2v[:], f2[:], ACT.Exp,
                                                   bias=z128),
                     reads=["c_f2", "cvals"], writes=["c_s2v"],
                     tag=tg + "_ef")
                b.op("dve", lambda e: e.scalar_tensor_tensor(
                         s2v[:], s2v[:], -1.0, ones_t[:, 0:256],
                         OP.mult, OP.add),
                     reads=["c_s2v", "ones_t"], writes=["c_s2v"],
                     tag=tg + "_s")
                b.op("dve", lambda e: e.tensor_mul(ufm[:], s2v[:], expng[:]),
                     reads=["c_s2v", "c_eng"], writes=["c_ufm"],
                     tag=tg + "_u")
                def tr1(eng):
                    ins = None
                    for blk in range(2):
                        ins = eng.matmul(ps[2][:, blk * 128:(blk + 1) * 128],
                                         k2[:, blk * 128:(blk + 1) * 128],
                                         ident[:], is_transpose=True,
                                         start=(blk == 0), stop=False)
                    for blk in range(2):
                        ins = eng.matmul(ps[2][:, 256 + blk * 128:
                                               256 + (blk + 1) * 128],
                                         ufm[:, blk * 128:(blk + 1) * 128],
                                         ident[:], is_transpose=True,
                                         start=False, stop=(blk == 1))
                    return ins
                b.op("pe", tr1, reads=["c_k2", "c_ufm", "ident"],
                     writes=["ps2"], tag=tg + "_tr1")
                b.op("dve", lambda e: e.tensor_copy(ktm[:], ps[2][:, 0:256]),
                     reads=["ps2"], writes=["c_ktm"], tag=tg + "_ktmc")
                b.op("dve", lambda e: e.tensor_copy(utm[:], ps[2][:, 256:512]),
                     reads=["ps2"], writes=["c_utm"], tag=tg + "_utmc")
                def tr2(eng):
                    ins = None
                    for blk in range(2):
                        ins = eng.matmul(ps[3][:, blk * 128:(blk + 1) * 128],
                                         expg[:, blk * 128:(blk + 1) * 128],
                                         ident[:], is_transpose=True,
                                         start=(blk == 0), stop=False)
                    for blk in range(2):
                        ins = eng.matmul(ps[3][:, 256 + blk * 128:
                                               256 + (blk + 1) * 128],
                                         v2[:, blk * 128:(blk + 1) * 128],
                                         ident[:], is_transpose=True,
                                         start=False, stop=(blk == 1))
                    return ins
                b.op("pe", tr2, reads=["c_eg", "c_v2", "ident"],
                     writes=["ps3"], tag=tg + "_tr2")
                b.op("dve", lambda e: e.tensor_copy(egtm[:], ps[3][:, 0:256]),
                     reads=["ps3"], writes=["c_egtm"], tag=tg + "_egtmc")
                b.op("dve", lambda e: e.tensor_copy(vtm[:], ps[3][:, 256:512]),
                     reads=["ps3"], writes=["c_vtm"], tag=tg + "_vtmc")
                def mmA(eng):
                    ins = None
                    for blk in range(2):
                        ins = eng.matmul(ps[0][:, 0:128],
                                         k2[:, blk * 128:(blk + 1) * 128],
                                         q2[:, blk * 128:(blk + 1) * 128],
                                         start=(blk == 0), stop=(blk == 1))
                    return ins
                b.op("pe", mmA, reads=["c_k2", "c_q2"], writes=["ps0a"],
                     tag=tg + "_A")
                b.op("dve", lambda e: e.tensor_mul(asb[:], ps[0][:, 0:128],
                                                   tri[:]),
                     reads=["ps0a", "tri"], writes=["c_as"],
                     tag=tg + "_Am")
                def mmok(eng):
                    return eng.matmul(ps[1][:, 0:256], asb[:], utm[:],
                                      start=True, stop=True)
                b.op("pe", mmok, reads=["c_as", "c_utm"],
                     writes=["ps1a"], tag=tg + "_ok")
                def mmoki(eng, S1=S1):
                    ins = None
                    for blk in range(2):
                        ins = eng.matmul(ps[6][:, 0:256],
                                         q2[:, blk * 128:(blk + 1) * 128],
                                         S1[:, blk * 256:(blk + 1) * 256],
                                         start=(blk == 0), stop=(blk == 1))
                    return ins
                b.op("pe", mmoki, reads=["c_q2", "c_s1_%d" % h],
                     writes=["ps6a"], tag=tg + "_oki")
                b.op("dve", lambda e: e.tensor_mul(qvtm[:], ps[1][:, 0:256],
                                                   egtm[:]),
                     reads=["ps1a", "c_egtm"], writes=["c_qvtm"],
                     tag=tg + "_oksc")
                b.op("dve", lambda e: e.tensor_mul(p.sb["c_sc"][:, 0:256],
                                                   ps[6][:, 0:256], egtm[:]),
                     reads=["ps6a", "c_egtm"], writes=["c_sc"],
                     tag=tg + "_oksc2")
                b.op("dve", lambda e: e.tensor_add(qvtm[:], qvtm[:],
                                                   p.sb["c_sc"][:, 0:256]),
                     reads=["c_qvtm", "c_sc"], writes=["c_qvtm"],
                     tag=tg + "_okadd")
                b.op("dve", lambda e: e.tensor_reduce(
                         p.sb["c_mx"][:, 0:1], qvtm[:], AX, OP.max,
                         negate=True),
                     reads=["c_qvtm"], writes=["c_mx"], tag=tg + "_mx")
                b.op("act", lambda e: e.activation(
                         qvtm[:], qvtm[:], ACT.Exp,
                         bias=p.sb["c_mx"][:, 0:1],
                         accum_out=p.sb["c_mx"][:, 1:2]),
                     reads=["c_qvtm", "c_mx"], writes=["c_qvtm", "c_mx"],
                     tag=tg + "_sme")
                b.op("dve", lambda e: e.reciprocal(p.sb["c_mx"][:, 2:3],
                                                   p.sb["c_mx"][:, 1:2]),
                     reads=["c_mx"], writes=["c_mx"], tag=tg + "_smr")
                b.op("dve", lambda e: e.tensor_scalar(
                         qvtm[:], qvtm[:], p.sb["c_mx"][:, 2:3], None,
                         OP.mult),
                     reads=["c_qvtm", "c_mx"], writes=["c_qvtm"],
                     tag=tg + "_smn")
                def tr3(eng):
                    ins = None
                    for blk in range(2):
                        ins = eng.matmul(ps[7][:, blk * 128:(blk + 1) * 128],
                                         qvtm[:, blk * 128:(blk + 1) * 128],
                                         ident[:], is_transpose=True,
                                         start=(blk == 0), stop=(blk == 1))
                    return ins
                b.op("pe", tr3, reads=["c_qvtm", "ident"], writes=["ps7a"],
                     tag=tg + "_tr3")
                b.op("dve", lambda e: e.tensor_copy(qvfm[:], ps[7][:, 0:256]),
                     reads=["ps7a"], writes=["c_qvfm"], tag=tg + "_qvfmc")
                b.op("dve", lambda e: e.tensor_mul(qtfm[:], qvfm[:], expg[:]),
                     reads=["c_qvfm", "c_eg"], writes=["c_qtfm"],
                     tag=tg + "_qt")
                def mmA2(eng):
                    ins = None
                    for blk in range(2):
                        ins = eng.matmul(ps[0][:, 128:256],
                                         ufm[:, blk * 128:(blk + 1) * 128],
                                         qtfm[:, blk * 128:(blk + 1) * 128],
                                         start=(blk == 0), stop=(blk == 1))
                    return ins
                b.op("pe", mmA2, reads=["c_ufm", "c_qtfm"], writes=["ps0b"],
                     tag=tg + "_A2")
                b.op("dve", lambda e: e.tensor_mul(a2sb[:], ps[0][:, 128:256],
                                                   tri[:]),
                     reads=["ps0b", "tri"], writes=["c_a2s"],
                     tag=tg + "_A2m")
                def mmov(eng):
                    return eng.matmul(ps[1][:, 256:512], a2sb[:], vtm[:],
                                      start=True, stop=True)
                b.op("pe", mmov, reads=["c_a2s", "c_vtm"], writes=["ps1b"],
                     tag=tg + "_ov")
                def mmovi(eng, S2=S2):
                    ins = None
                    for blk in range(2):
                        ins = eng.matmul(ps[6][:, 256:512],
                                         qtfm[:, blk * 128:(blk + 1) * 128],
                                         S2[:, blk * 256:(blk + 1) * 256],
                                         start=(blk == 0), stop=(blk == 1))
                    return ins
                b.op("pe", mmovi, reads=["c_qtfm", "c_s2_%d" % h],
                     writes=["ps6b"], tag=tg + "_ovi")
                b.op("dve", lambda e: e.tensor_copy(ovtm[:], ps[1][:, 256:512]),
                     reads=["ps1b"], writes=["c_ovtm"], tag=tg + "_ovc")
                b.op("dve", lambda e: e.tensor_add(ovtm[:], ovtm[:],
                                                   ps[6][:, 256:512]),
                     reads=["ps6b", "c_ovtm"], writes=["c_ovtm"],
                     tag=tg + "_ovc2")
                def mmku(eng):
                    ins = None
                    for blk in range(2):
                        ins = eng.matmul(ps[5][:, blk * 256:(blk + 1) * 256],
                                         ktm[:, blk * 128:(blk + 1) * 128],
                                         utm[:], start=True, stop=True)
                    return ins
                b.op("pe", mmku, reads=["c_ktm", "c_utm"], writes=["ps5"],
                     tag=tg + "_ku")
                b.dma("sp", p.sb["c_fr"][0:1, 0:256], egtm[127:128, 0:256],
                      reads=["c_egtm"], writes=["c_fr"], tag=tg + "_frd")
                b.op("pe", lambda e: e.matmul(ps[0][:, 256:512], ones_m[:],
                                              p.sb["c_fr"][0:1, 0:256],
                                              start=True, stop=True),
                     reads=["ones_m", "c_fr"], writes=["ps0f"],
                     tag=tg + "_fb")
                for blk in range(2):
                    S1b = S1[:, blk * 256:(blk + 1) * 256]
                    b.op("dve", lambda e, S1b=S1b, blk=blk: e.tensor_add(
                             S1b, S1b, ps[5][:, blk * 256:(blk + 1) * 256]),
                         reads=["c_s1_%d" % h, "ps5"],
                         writes=["c_s1_%d" % h],
                         tag=tg + "_s1a%d" % blk)
                    b.op("dve", lambda e, S1b=S1b: e.tensor_mul(
                             S1b, S1b, ps[0][:, 256:512]),
                         reads=["c_s1_%d" % h, "ps0f"],
                         writes=["c_s1_%d" % h],
                         tag=tg + "_s1m%d" % blk)
                def mmku2(eng):
                    ins = None
                    for blk in range(2):
                        ins = eng.matmul(ps[4][:, blk * 256:(blk + 1) * 256],
                                         utm[:, blk * 128:(blk + 1) * 128],
                                         vtm[:], start=True, stop=True)
                    return ins
                b.op("pe", mmku2, reads=["c_utm", "c_vtm"], writes=["ps4"],
                     tag=tg + "_ku2")
                for blk in range(2):
                    S2b = S2[:, blk * 256:(blk + 1) * 256]
                    egcol = expg[:, blk * 128 + 127: blk * 128 + 128]
                    b.op("dve", lambda e, S2b=S2b, blk=blk: e.tensor_add(
                             S2b, S2b, ps[4][:, blk * 256:(blk + 1) * 256]),
                         reads=["c_s2_%d" % h, "ps4"],
                         writes=["c_s2_%d" % h],
                         tag=tg + "_s2a%d" % blk)
                    b.op("dve", lambda e, S2b=S2b, egcol=egcol: e.tensor_scalar(
                             S2b, S2b, egcol, None, OP.mult),
                         reads=["c_s2_%d" % h, "c_eg"],
                         writes=["c_s2_%d" % h],
                         tag=tg + "_s2m%d" % blk)
                def tr4(eng):
                    ins = None
                    for blk in range(2):
                        ins = eng.matmul(ps[7][:, 256 + blk * 128:
                                               256 + (blk + 1) * 128],
                                         ovtm[:, blk * 128:(blk + 1) * 128],
                                         ident[:], is_transpose=True,
                                         start=(blk == 0), stop=(blk == 1))
                    return ins
                b.op("pe", tr4, reads=["c_ovtm", "ident"], writes=["ps7b"],
                     tag=tg + "_tr4")
                b.op("dve", lambda e, h=h: e.tensor_copy(
                         ofm[:, h * 256:(h + 1) * 256], ps[7][:, 256:512]),
                     reads=["ps7b"], writes=["c_ofm#%d" % h],
                     tag=tg + "_ofmc")
            tgc = "c2r_%d_%d" % (s_, c_)
            b.op("act", lambda e: e.activation(osil[:], ofm[:], ACT.Silu,
                                               bias=z128),
                 reads=["c_ofm#%d" % hh for hh in range(4)] + ["cvals"],
                 writes=["c_osil"], tag=tgc + "_sil")
            b.op("act", lambda e: e.square(osq[:], osil[:]),
                 reads=["c_osil"], writes=["c_osq"], tag=tgc + "_sq")
            def mmrs(eng):
                ins = None
                for c2 in range(8):
                    ins = eng.matmul(ps[6][0:1, 0:128], p.sb["ones_k"][:],
                                     osq[:, c2 * 128:(c2 + 1) * 128],
                                     start=(c2 == 0), stop=(c2 == 7))
                return ins
            b.op("pe", mmrs, reads=["c_osq", "ones_k"], writes=["ps6a"],
                 tag=tgc + "_ss")
            rr = stat[0:1, 0:128]
            b.op("act", lambda e: e.mul(rr, ps[6][0:1, 0:128], 1.0 / D),
                 reads=["ps6a"], writes=["stat"], tag=tgc + "_m")
            b.op("act", lambda e: e.activation(rr, rr, ACT.Sqrt,
                                               bias=p.cval(EPS)),
                 reads=["stat", "cvals"], writes=["stat"], tag=tgc + "_sd")
            b.op("dve", lambda e: e.reciprocal(rr, rr),
                 reads=["stat"], writes=["stat"], tag=tgc + "_r")
            b.op("pe", lambda e: e.matmul(ps[7][:, 0:128], ones_m[:], rr,
                                          start=True, stop=True),
                 reads=["stat", "ones_m"], writes=["ps7a"],
                 tag=tgc + "_bc")
            for c2 in range(8):
                sl = osil[:, c2 * 128:(c2 + 1) * 128]
                b.op("dve", lambda e, sl=sl: e.tensor_mul(sl, sl,
                                                          ps[7][:, 0:128]),
                     reads=["c_osil", "ps7a"], writes=["c_osil"],
                     tag=tgc + "_rm%d" % c2)
                b.op("dve", lambda e, sl=sl, c2=c2: e.tensor_scalar(
                         sl, sl, gw[:, c2:c2 + 1], None, OP.mult),
                     reads=["c_osil", "c_gw"], writes=["c_osil"],
                     tag=tgc + "_gw%d" % c2)
            store_fm(p, "sp", OS, "o_scr", osil, "c_osil", tt0, C,
                     tag=tgc + "_st", chunked=False)

    b.barrier("c3_bar")
    TW3 = min(256, NTOK)
    banks3 = std_banks(p, TW3)
    load_w(p, "sp", wbig[:, 0:NCH * D], "wbig", wo_h, D, D, tag="c_ldwo")
    for it in range(NTOK // TW3):
        t0_ = it * TW3
        par = it % 2
        x_t = t[0][:, par * NCH * TW3:(par + 1) * NCH * TW3]
        xk = "c3_x%d" % par
        o_t = t[1][:, par * NCH * TW3:(par + 1) * NCH * TW3]
        ok_ = "c3_o%d" % par
        y_t = t[2][:, par * NCH * TW3:(par + 1) * NCH * TW3]
        yk = "c3_y%d" % par
        xo_t, xok = o_t, ok_
        load_fm(p, "sp", x_t, xk, src_hbm, t0_, TW3, tag="c3_ldx%d" % it)
        load_fm(p, "sp", o_t, ok_, OS, t0_, TW3, tag="c3_ldo%d" % it)
        gb = [ps[2 * par], ps[2 * par + 1]]
        gbk = ["ps%d" % (2 * par), "ps%d" % (2 * par + 1)]
        for o in range(NCH):
            bank = gb[o % 2]
            bkey = gbk[o % 2]

            def mm(eng, o=o, bank=bank, o_t=o_t, TW3=TW3):
                ins = None
                for c in range(NCH):
                    col = c * D + o * 128
                    ins = eng.matmul(bank[:, 0:TW3], wbig[:, col:col + 128],
                                     o_t[:, c * TW3:(c + 1) * TW3],
                                     start=(c == 0), stop=(c == NCH - 1))
                return ins

            b.op("pe", mm, reads=["wbig"] + ck(ok_), writes=[bkey],
                 tag="c3_mm%d_%d" % (it, o))
            b.op("act", lambda e, o=o, bank=bank, y_t=y_t, TW3=TW3: e.copy(
                     y_t[:, o * TW3:(o + 1) * TW3], bank[:, 0:TW3]),
                 reads=[bkey], writes=[yk + "#%d" % o],
                 tag="c3_cp%d_%d" % (it, o))
        emit_ln(p, y_t, yk, TW3, banks3, stat, "stat", tag="c3_ln%d" % it)
        ln_apply_residual(p, y_t, yk, x_t, xk, xo_t, xok, TW3, banks3,
                          tag="c3_res%d" % it)
        store_fm(p, "sp", dst_hbm, dst_key, xo_t, xok, t0_, TW3,
                 tag="c3_st%d" % it)

def emit_phase_d(p, src, dst):
    src_hbm, src_key = src
    dst_hbm, dst_key = dst
    b = p.b
    ps = [p.sb[f"ps{i}"] for i in range(8)]
    t = [p.sb[f"t{i}"] for i in range(3)]
    stat = p.sb["stat"]
    wbig = p.sb["wbig"]
    ones_m = p.sb["ones_m"]
    ident = p.sb["ident"]
    hb = p.sb["hb"]
    lb = p.sb["lb"]
    mb = p.sb["mb"]
    u0 = p.sb["u0"]
    wq16 = p.sb["wq16"]
    HS = p.dram_tmp("hs", [2 * 4096, NTOK], BF16)
    WQF = p.dram_tmp("wqf", [D, 4096], BF16)
    WQP = p.dram_tmp("wqp", [4096, D], BF16)
    wfc_h = p.dram_in("wfc_t", [D, 4096], F32)
    wpr_h = p.dram_in("wpr_t", [4096, D], F32)
    bfc_v = p.sbuf("d_bfcv", [128, 32], F32)
    load_vec(p, "sp", bfc_v[:], "d_bfcv", p.dram_in("b_fc", [128, 32], F32))
    bpr_v = p.sbuf("d_bprv", [128, 8], F32)
    load_vec(p, "sp", bpr_v[:], "d_bprv", p.dram_in("b_pr", [128, 8], F32))
    sqt = p.sb["sq_t"]
    afc = t[1][:, 1152:1152 + NCH * 16]
    bfc_h = p.dram_in("bfc_t", [16, 4096], F32)
    bfc_sl = sqt[0:16, 2560:3072]
    apr = t[1][:, 640:640 + 32 * 16]
    bpr_h = p.dram_in("bpr_t", [16, D], F32)
    bpr_sl = p.sbuf("d_bprs", [16, 256], F32)[:]
    z_sb = p.sbuf("d_z2", [16, 512], F32)[:]
    dsc = p.sbuf("d_sc", [1, 8], F32)
    dscb = p.sbuf("d_scb", [128, 4], F32)
    mx1 = p.sbuf("d_mx", [128, 2], F32)
    rowt = sqt[0:1, 3328:3456]
    MAGIC = float(3 * 2 ** 22)
    cmag = p.cval(MAGIC, 128)
    c127 = p.cval(127.0, 128)
    cn128 = p.cval(-128.0, 128)

    b.barrier("d0_bar")
    for wi, (wh, wq_scr, wq_key, rows, cols, sc_i) in enumerate([
            (wfc_h, WQF, "wqf", D, 4096, 0),
            (wpr_h, WQP, "wqp", 4096, D, 2)]):
        nld = (rows // 128) * cols // 16384
        b.op("dve", lambda e: e.memset(mx1[:, 0:1], 0.0), writes=["d_mx"],
             tag="d0_z%d" % wi)
        ncol_per = 16384 // (rows // 128)
        for j in range(nld):
            load_w(p, "sp", wbig[:], "wbig", wh, rows, cols,
                   col0=j * ncol_per, ncols=ncol_per, tag="d0_ld%d_%d" % (wi, j))
            b.op("dve", lambda e: e.tensor_reduce(mx1[:, 1:2], wbig[:], AX,
                                                  OP.max,
                                                  apply_absolute_value=True),
                 reads=["wbig"], writes=["d_mx"], tag="d0_mx%d_%d" % (wi, j))
            b.op("dve", lambda e: e.tensor_max(mx1[:, 0:1], mx1[:, 0:1],
                                               mx1[:, 1:2]),
                 reads=["d_mx"], writes=["d_mx"], tag="d0_acc%d_%d" % (wi, j))
        b.op("pe", lambda e: e.matmul(ps[3][0:1, 0:128], mx1[:, 0:1], ident[:],
                                      is_transpose=True, start=True, stop=True),
             reads=["d_mx", "ident"], writes=["ps3"], tag="d0_tr%d" % wi)
        b.op("dve", lambda e: e.tensor_copy(rowt, ps[3][0:1, 0:128]),
             reads=["ps3"], writes=["d_row"], tag="d0_rc%d" % wi)
        b.op("dve", lambda e, sc_i=sc_i: e.tensor_reduce(
                 dsc[0:1, sc_i:sc_i + 1], rowt, AX, OP.max,
                 apply_absolute_value=True),
             reads=["d_row"], writes=["d_sc"], tag="d0_rm%d" % wi)
        b.op("act", lambda e, sc_i=sc_i: e.mul(dsc[0:1, sc_i:sc_i + 1],
                                               dsc[0:1, sc_i:sc_i + 1],
                                               1.0 / 127.0),
             reads=["d_sc"], writes=["d_sc"], tag="d0_s%d" % wi)
        b.op("dve", lambda e, sc_i=sc_i: e.reciprocal(
                 dsc[0:1, sc_i + 1:sc_i + 2], dsc[0:1, sc_i:sc_i + 1]),
             reads=["d_sc"], writes=["d_sc"], tag="d0_r%d" % wi)
    b.op("pe", lambda e: e.matmul(ps[3][:, 0:4], ones_m[:], dsc[0:1, 0:4],
                                  start=True, stop=True),
         reads=["d_sc", "ones_m"], writes=["ps3"], tag="d0_bc")
    b.op("dve", lambda e: e.tensor_copy(dscb[:], ps[3][:, 0:4]),
         reads=["ps3"], writes=["d_scb"], tag="d0_bcc")
    for wi, (wh, wq_scr, wq_key, rows, cols, sc_i) in enumerate([
            (wfc_h, WQF, "wqf", D, 4096, 0),
            (wpr_h, WQP, "wqp", 4096, D, 2)]):
        nld = (rows // 128) * cols // 16384
        ncol_per = 16384 // (rows // 128)
        for j in range(nld):
            load_w(p, "sp", wbig[:], "wbig", wh, rows, cols,
                   col0=j * ncol_per, ncols=ncol_per, tag="d0q_ld%d_%d" % (wi, j))
            b.op("dve", lambda e, sc_i=sc_i: e.tensor_scalar(
                     wbig[:], wbig[:], dscb[:, sc_i + 1:sc_i + 2], None,
                     OP.mult),
                 reads=["wbig", "d_scb"], writes=["wbig"],
                 tag="d0q_m%d_%d" % (wi, j))
            b.op("dve", lambda e: e.tensor_scalar(wbig[:], wbig[:], cmag,
                                                  None, OP.add),
                 reads=["wbig", "cvals"], writes=["wbig"],
                 tag="d0q_a%d_%d" % (wi, j))
            b.op("dve", lambda e: e.tensor_scalar(wbig[:], wbig[:], cmag,
                                                  None, OP.subtract),
                 reads=["wbig", "cvals"], writes=["wbig"],
                 tag="d0q_s%d_%d" % (wi, j))
            b.op("dve", lambda e: e.tensor_scalar(wbig[:], wbig[:], c127,
                                                  None, OP.min),
                 reads=["wbig", "cvals"], writes=["wbig"],
                 tag="d0q_c1%d_%d" % (wi, j))
            b.op("dve", lambda e: e.tensor_scalar(wbig[:], wbig[:], cn128,
                                                  None, OP.max),
                 reads=["wbig", "cvals"], writes=["wbig"],
                 tag="d0q_c2%d_%d" % (wi, j))
            nchj = rows // 256
            for hf in range(2):
                sl = wbig[:, hf * 8192:(hf + 1) * 8192]
                w16 = p.sb["d0w16"]
                b.op("dve", lambda e, sl=sl, w16=w16: e.tensor_copy(w16[:], sl),
                     reads=["wbig"], writes=["d0_w16"],
                     tag="d0q_cv%d_%d_%d" % (wi, j, hf))
                dstw = wq_scr.rearrange("(c p) o -> p c o", p=128)[
                    :, hf * nchj:(hf + 1) * nchj,
                    j * ncol_per:(j + 1) * ncol_per]
                srcw = w16[:].rearrange("p (c o) -> p c o", c=nchj)
                b.dma("sp", dstw, srcw, reads=["d0_w16"],
                      writes=["%s#%d_%d" % (wq_key, j, hf)],
                      tag="d0q_st%d_%d_%d" % (wi, j, hf))
    b.barrier("d0e_bar")
    load_w(p, "sp", afc, "d_afc", p.dram_in("afc_t", [D, 16], F32), D, 16)
    load_w(p, "sp", apr, "d_apr", p.dram_in("apr_t", [4096, 16], F32),
           4096, 16)
    afcs = t[1][:, 0:NCH * 16]
    afch = mb[:, 0:NCH * 16]
    afcl = mb[:, NCH * 16:2 * NCH * 16]
    b.op("dve", lambda e: e.tensor_scalar(afcs, afc, dscb[:, 1:2],
                                          None, OP.mult),
         reads=["d_afc", "d_scb"], writes=["d0_afcs"], tag="d0_afs")
    b.op("dve", lambda e: e.tensor_copy(afch, afcs),
         reads=["d0_afcs"], writes=["d_afch"], tag="d0_afh")
    b.op("dve", lambda e: e.tensor_tensor(afcl, afcs, afch,
                                          OP.subtract),
         reads=["d0_afcs", "d_afch"], writes=["d_afcl"], tag="d0_afl")
    aprs = t[1][:, 128:128 + 32 * 16]
    aprh = p.sbuf("d_aprh", [128, 32 * 16], BF16)
    aprl = p.sbuf("d_aprl", [128, 32 * 16], BF16)
    
    b.op("dve", lambda e: e.tensor_scalar(aprs, apr, dscb[:, 3:4],
                                          None, OP.mult),
         reads=["d_apr", "d_scb"], writes=["d0_aprs"], tag="d0_aps")
    b.op("dve", lambda e: e.tensor_copy(aprh[:], aprs),
         reads=["d0_aprs"], writes=["d_aprh"], tag="d0_aph")
    b.op("dve", lambda e: e.tensor_tensor(aprl[:], aprs, aprh[:],
                                          OP.subtract),
         reads=["d0_aprs", "d_aprh"], writes=["d_aprl"], tag="d0_apl")

    b.barrier("d1_bar")
    TW = 512
    nt = NTOK // TW
    banks = std_banks(p, TW)
    x_t, xk = t[0][:, :NCH * TW], "d1_x"
    hikeys = ["d1_hi#%d" % c for c in range(NCH)]
    lokeys = ["d1_lo#%d" % c for c in range(NCH)]
    HSr = HS.rearrange("(c p) t -> p c t", p=128)
    for it in range(nt):
        t0_ = it * TW
        load_fm(p, "sp", x_t, xk, src_hbm, t0_, TW, tag="d1_ldx%d" % it)
        emit_ln(p, x_t, xk, TW, banks, stat, "stat", tag="d1_ln%d" % it,
                scale_ap=dsc[0:1, 0:1], scale_key="d_sc",
                sq=t[1][:], sqk="d1_t32")
        for c in range(NCH):
            xs_ = x_t[:, c * TW:(c + 1) * TW]
            tmp = t[1][:, c * TW:(c + 1) * TW]
            hic = hb[:, c * TW:(c + 1) * TW]
            loc = lb[:, c * TW:(c + 1) * TW]
            tk = "d1_t32#%d" % c
            b.op("dve", lambda e, tmp=tmp, xs_=xs_: e.tensor_mul(
                     tmp, xs_, banks["br"]),
                 reads=[xk + "#%d" % c, banks["kr"]], writes=[tk],
                 tag="d1_sp%d_m%d" % (it, c))
            b.op("dve", lambda e, tmp=tmp: e.tensor_sub(tmp, tmp,
                                                        banks["bm"]),
                 reads=[tk, banks["km"]], writes=[tk],
                 tag="d1_sp%d_s%d" % (it, c))
            b.op("dve", lambda e, hic=hic, tmp=tmp: e.tensor_copy(hic, tmp),
                 reads=[tk], writes=["d1_hi#%d" % c],
                 tag="d1_sp%d_h%d" % (it, c))
            b.op("dve", lambda e, loc=loc, tmp=tmp, hic=hic: e.tensor_tensor(
                     loc, tmp, hic, OP.subtract),
                 reads=[tk, "d1_hi#%d" % c], writes=["d1_lo#%d" % c],
                 tag="d1_sp%d_l%d" % (it, c))
        def mmz(eng, TW=TW):
            ins = None
            for c in range(NCH):
                ins = eng.matmul(ps[3][0:16, 0:TW], afch[:, c * 16:(c + 1) * 16],
                                 hb[:, c * TW:(c + 1) * TW],
                                 start=(c == 0), stop=False)
            for c in range(NCH):
                ins = eng.matmul(ps[3][0:16, 0:TW], afch[:, c * 16:(c + 1) * 16],
                                 lb[:, c * TW:(c + 1) * TW],
                                 start=False, stop=False)
            for c in range(NCH):
                ins = eng.matmul(ps[3][0:16, 0:TW],
                                 afcl[:, c * 16:(c + 1) * 16],
                                 hb[:, c * TW:(c + 1) * TW],
                                 start=False, stop=(c == NCH - 1))
            return ins
        b.op("pe", mmz, reads=["d_afch", "d_afcl"] + hikeys + lokeys,
             writes=["ps3"], tag="d1_z%d" % it)
        b.op("act", lambda e: e.mul(z_sb[:, 0:TW], ps[3][0:16, 0:TW], 2.0),
             reads=["ps3"], writes=["d_z"], tag="d1_zc%d" % it)
        for o in range(32):
            slot = o % 8
            wreg = wq16[:, slot * 1024:slot * 1024 + 1024]
            wkey = "wq16#%d" % slot
            load_w(p, "sp", wreg, wkey, WQF, D, 4096, col0=o * 128, ncols=128,
                   tag="d1_lw%d_%d" % (it, o))
            s4 = o % 4
            b.dma("sp", bfc_sl[:, s4 * 128:(s4 + 1) * 128],
                  bfc_h[:, o * 128:(o + 1) * 128], reads=["d_bfc#%d" % s4],
                  writes=["d_bfc#%d" % s4], tag="d1_lb%d_%d" % (it, o))
            bank = ps[o % 3]
            bkey = "ps%d" % (o % 3)

            def mm(eng, o=o, bank=bank, wreg=wreg, TW=TW):
                ins = None
                for c in range(NCH):
                    ins = eng.matmul(bank[:, 0:TW],
                                     wreg[:, c * 128:(c + 1) * 128],
                                     hb[:, c * TW:(c + 1) * TW],
                                     start=(c == 0), stop=False)
                    ins = eng.matmul(bank[:, 0:TW],
                                     wreg[:, c * 128:(c + 1) * 128],
                                     lb[:, c * TW:(c + 1) * TW],
                                     start=False, stop=False)
                s4 = o % 4
                ins = eng.matmul(bank[:, 0:TW],
                                 bfc_sl[:, s4 * 128:(s4 + 1) * 128],
                                 z_sb[0:16, 0:TW], start=False, stop=True)
                return ins

            b.op("pe", mm,
                 reads=[wkey, "d_bfc#%d" % (o % 4), "d_z"] + hikeys + lokeys,
                 writes=[bkey], tag="d1_mm%d_%d" % (it, o))
            par = o % 2
            h_t = t[2][:, par * TW:par * TW + TW]
            hk = "d1_h%d" % par
            b.op("act", lambda e, bank=bank, h_t=h_t, o=o: e.activation(
                     h_t, bank[:, 0:TW], ACT.Gelu,
                     bias=bfc_v[:, o:o + 1]),
                 reads=[bkey, "d_bfcv"], writes=[hk], tag="d1_g%d_%d" % (it, o))
            hs32 = t[2][:, 1024 + par * TW:1024 + par * TW + TW]
            hsk = "d1_hs%d" % par
            b.op("dve", lambda e, hs32=hs32, h_t=h_t: e.tensor_scalar(
                     hs32, h_t, dscb[:, 2:3], None, OP.mult),
                 reads=[hk, "d_scb"], writes=[hsk],
                 tag="d1_hm%d_%d" % (it, o))
            hi8 = u0[:, par * 1024:par * 1024 + 512]
            lo8 = u0[:, par * 1024 + 512:par * 1024 + 1024]
            h8k = "d1_h8%d" % par
            l8k = "d1_l8%d" % par
            b.op("dve", lambda e, hi8=hi8, hs32=hs32: e.tensor_copy(hi8, hs32),
                 reads=[hsk], writes=[h8k], tag="d1_hh%d_%d" % (it, o))
            b.op("dve", lambda e, lo8=lo8, hs32=hs32, hi8=hi8: e.tensor_tensor(
                     lo8, hs32, hi8, OP.subtract),
                 reads=[hsk, h8k], writes=[l8k], tag="d1_hl%d_%d" % (it, o))
            b.dma("sp", HSr[:, o:o + 1, t0_:t0_ + TW],
                  hi8.rearrange("p (c t) -> p c t", c=1),
                  reads=[h8k], writes=["hs_%d_%d_h" % (it, o)],
                  tag="d1_sth%d_%d" % (it, o))
            b.dma("sp", HSr[:, 32 + o:33 + o, t0_:t0_ + TW],
                  lo8.rearrange("p (c t) -> p c t", c=1),
                  reads=[l8k], writes=["hs_%d_%d_l" % (it, o)],
                  tag="d1_stl%d_%d" % (it, o))

    b.barrier("d2_bar")
    TW2 = 512
    nt2 = NTOK // TW2
    banks2 = std_banks(p, TW2)
    d2h = p.sb["d2h"]
    d2l = p.sb["d2l"]
    wq16d = p.sb["wq16d"]
    x2_t, x2k = t[2][:, 0:NCH * 512], "d2_x"
    y2_t, y2k = t[1][:, 0:NCH * 512], "d2_y"
    xo2_t, xo2k = y2_t, y2k
    hikeys2 = ["d2_hi#%d" % c for c in range(32)]
    lokeys2 = ["d2_lo#%d" % c for c in range(32)]
    for it in range(nt2):
        t0_ = it * TW2
        b.dma("sp", d2h[:, 0:32 * TW2].rearrange("p (c t) -> p c t", c=32),
              HSr[:, 0:32, t0_:t0_ + TW2],
              reads=[], writes=hikeys2, tag="d2_lh_%d" % it)
        b.dma("sp", d2l[:, 0:32 * TW2].rearrange("p (c t) -> p c t", c=32),
              HSr[:, 32:64, t0_:t0_ + TW2],
              reads=[], writes=lokeys2, tag="d2_ll_%d" % it)
        load_fm(p, "sp", x2_t, x2k, src_hbm, t0_, TW2, tag="d2_ldx%d" % it)
        def mmz2(eng, TW2=TW2):
            ins = None
            for c in range(32):
                ins = eng.matmul(ps[3][0:16, 0:TW2],
                                 aprh[:, c * 16:(c + 1) * 16],
                                 d2h[:, c * TW2:(c + 1) * TW2],
                                 start=(c == 0), stop=False)
            for c in range(32):
                ins = eng.matmul(ps[3][0:16, 0:TW2],
                                 aprh[:, c * 16:(c + 1) * 16],
                                 d2l[:, c * TW2:(c + 1) * TW2],
                                 start=False, stop=False)
            for c in range(32):
                ins = eng.matmul(ps[3][0:16, 0:TW2],
                                 aprl[:, c * 16:(c + 1) * 16],
                                 d2h[:, c * TW2:(c + 1) * TW2],
                                 start=False, stop=(c == 31))
            return ins
        b.op("pe", mmz2, reads=["d_aprh", "d_aprl"] + hikeys2 + lokeys2,
             writes=["ps3"], tag="d2_z%d" % it)
        b.op("act", lambda e: e.mul(z_sb[:, 0:TW2], ps[3][0:16, 0:TW2], 2.0),
             reads=["ps3"], writes=["d_z"], tag="d2_zc%d" % it)
        for o in range(NCH):
            slot = o % 2
            wreg = wq16d[:, slot * 4096:(slot + 1) * 4096]
            wkey2 = "wq16d#%d" % slot
            load_w(p, "sp", wreg, wkey2, WQP, 4096, D, col0=o * 128,
                   ncols=128, tag="d2_lw%d_%d" % (it, o))
            s2_ = o % 2
            b.dma("sp", bpr_sl[:, s2_ * 128:(s2_ + 1) * 128],
                  bpr_h[:, o * 128:(o + 1) * 128], reads=["d_bpr#%d" % s2_],
                  writes=["d_bpr#%d" % s2_], tag="d2_lb%d_%d" % (it, o))
            bank = ps[o % 3]
            bkey = "ps%d" % (o % 3)

            def mm(eng, o=o, bank=bank, wreg=wreg, TW2=TW2):
                ins = None
                for c in range(32):
                    ins = eng.matmul(bank[:, 0:TW2],
                                     wreg[:, c * 128:(c + 1) * 128],
                                     d2h[:, c * TW2:(c + 1) * TW2],
                                     start=(c == 0), stop=False)
                    ins = eng.matmul(bank[:, 0:TW2],
                                     wreg[:, c * 128:(c + 1) * 128],
                                     d2l[:, c * TW2:(c + 1) * TW2],
                                     start=False, stop=False)
                s2_ = o % 2
                ins = eng.matmul(bank[:, 0:TW2],
                                 bpr_sl[:, s2_ * 128:(s2_ + 1) * 128],
                                 z_sb[0:16, 0:TW2], start=False, stop=True)
                return ins

            b.op("pe", mm,
                 reads=[wkey2, "d_bpr#%d" % (o % 2), "d_z"]
                 + hikeys2 + lokeys2,
                 writes=[bkey], tag="d2_mm%d_%d" % (it, o))
            b.op("act", lambda e, o=o, bank=bank: e.activation(
                     y2_t[:, o * TW2:(o + 1) * TW2], bank[:, 0:TW2],
                     ACT.Identity, bias=bpr_v[:, o:o + 1]),
                 reads=[bkey, "d_bprv"], writes=[y2k + "#%d" % o],
                 tag="d2_y%d_%d" % (it, o))
        emit_ln(p, y2_t, y2k, TW2, banks2, stat, "stat",
                tag="d2_ln%d" % it)
        ln_apply_residual(p, y2_t, y2k, x2_t, x2k, xo2_t, xo2k, TW2, banks2,
                          tag="d2_res%d" % it)
        store_fm(p, "sp", dst_hbm, dst_key + "#st%d" % it, xo2_t, xo2k, t0_,
                 TW2, tag="d2_st%d" % it)


# version 09
# speedup vs baseline: 1.2101x; 1.2101x over previous
import numpy as np
from contextlib import ExitStack

import concourse.bass as bass
import concourse.mybir as mybir
from concourse.bass_utils import run_bass_kernel_spmd

F32 = mybir.dt.float32
BF16 = mybir.dt.bfloat16
FP8 = mybir.dt.float8e4
AX = mybir.AxisListType.X
OP = mybir.AluOpType
ACT = mybir.ActivationFunctionType

D = 1024
NTOK = 8192
SEQ = 4096
NCH = 8
EPS = 1e-5

_ENG_ATTR = {"sp": "sync", "pe": "tensor", "act": "scalar", "dve": "vector",
             "pool": "gpsimd"}


class _Op:
    __slots__ = ("eng", "fn", "reads", "writes", "dma", "deps", "need_inc",
                 "inc_idx", "idx", "tag", "slot", "use")

    def __init__(self, idx, eng, fn, reads, writes, dma, tag):
        self.idx = idx
        self.eng = eng
        self.fn = fn
        self.reads = reads
        self.writes = writes
        self.dma = dma
        self.deps = set()
        self.need_inc = dma
        self.inc_idx = None
        self.tag = tag


class Build:
    def __init__(self, nc, stack):
        self.nc = nc
        self.stack = stack
        self.ops = []
        self._last_w = {}
        self._readers = {}

    def op(self, eng, fn, reads=(), writes=(), dma=False, tag=""):
        o = _Op(len(self.ops), eng, fn, tuple(reads), tuple(writes), dma, tag)
        for buf in o.reads:
            wi = self._last_w.get(buf)
            if wi is not None:
                o.deps.add(wi)
            self._readers.setdefault(buf, []).append(o.idx)
        for buf in o.writes:
            wi = self._last_w.get(buf)
            if wi is not None:
                o.deps.add(wi)
            for r in self._readers.get(buf, ()):
                o.deps.add(r)
            self._readers[buf] = []
            self._last_w[buf] = o.idx
        o.deps.discard(o.idx)
        self.ops.append(o)
        return o

    def dma(self, queue, out, in_, reads, writes, tag=""):
        def fn(eng):
            return eng.dma_start(out=out, in_=in_)
        return self.op(queue, fn, reads=reads, writes=writes, dma=True, tag=tag)

    def barrier(self, tag="bar"):
        keys = sorted(set(self._last_w) | set(self._readers))
        return self.op("dve", lambda e: e.engine_nop(), reads=keys,
                       writes=keys, tag=tag)

    def finalize(self):
        nc = self.nc
        ops = self.ops
        for o in ops:
            for dd in o.deps:
                a = ops[dd]
                if a.dma:
                    continue
                if a.eng != o.eng:
                    a.need_inc = True
        NSLOT = {"sp": 12, "pool": 8, "act": 4, "dve": 4, "pe": 4}
        dcount = {e: 0 for e in _ENG_ATTR}
        csem_n = {e: 0 for e in _ENG_ATTR}
        slot_use = {}
        for o in ops:
            if o.dma:
                i = dcount[o.eng]
                dcount[o.eng] += 1
                o.slot = (o.eng, i % NSLOT[o.eng])
                o.use = i // NSLOT[o.eng] + 1
                slot_use[o.slot] = o.use
            elif o.need_inc:
                csem_n[o.eng] += 1
                o.inc_idx = csem_n[o.eng]
        csem, dsem = {}, {}
        for e in _ENG_ATTR:
            if csem_n[e]:
                csem[e] = self.stack.enter_context(nc.semaphore(f"c_{e}"))
        for sl in slot_use:
            dsem[sl] = self.stack.enter_context(
                nc.semaphore(f"d_{sl[0]}{sl[1]}"))
        per_eng = {e: [] for e in _ENG_ATTR}
        for o in ops:
            per_eng[o.eng].append(o)
        block = self.stack.enter_context(nc.Block())

        def emit(eng_name, eng_obj):
            water = {}
            covered = -1
            prev_idx = -1

            def wait(key, val):
                if water.get(key, 0) >= val:
                    return
                sem = dsem[key[1]] if key[0] == "d" else csem[key[1]]
                eng_obj.wait_ge(sem, val)
                water[key] = val

            for o in per_eng[eng_name]:
                req = {}
                need_drain = False
                for dd in sorted(o.deps):
                    a = ops[dd]
                    if a.eng == o.eng and not a.dma:
                        if dd > covered and o.eng != "pe":
                            need_drain = True
                        continue
                    key = ("d", a.slot) if a.dma else ("c", a.eng)
                    val = 16 * a.use if a.dma else a.inc_idx
                    if req.get(key, 0) < val:
                        req[key] = val
                if o.dma and o.use >= 2:
                    k = ("d", o.slot)
                    if req.get(k, 0) < 16 * (o.use - 1):
                        req[k] = max(req.get(k, 0), 16 * (o.use - 1))
                if need_drain:
                    eng_obj.drain()
                    covered = prev_idx
                for key, val in sorted(req.items(), key=str):
                    wait(key, val)
                ins = o.fn(eng_obj)
                prev_idx = o.idx
                if o.dma:
                    ins.then_inc(dsem[o.slot], 16)
                elif o.need_inc:
                    assert ins is not None, f"op {o.tag} needs inc, returned None"
                    ins.then_inc(csem[o.eng], 1)
            return water

        @block.sync
        def _(sync):
            water = emit("sp", sync)
            for sl, use in sorted(slot_use.items(), key=str):
                if water.get(("d", sl), 0) < 16 * use:
                    sync.wait_ge(dsem[sl], 16 * use)
            for e in sorted(_ENG_ATTR):
                n = csem_n[e]
                if e != "sp" and n and water.get(("c", e), 0) < n:
                    sync.wait_ge(csem[e], n)

        @block.tensor
        def _(tensor):
            emit("pe", tensor)

        @block.scalar
        def _(scalar):
            emit("act", scalar)

        @block.vector
        def _(vector):
            emit("dve", vector)

        @block.gpsimd
        def _(gpsimd):
            emit("pool", gpsimd)



class Prog:
    def __init__(self):
        self.nc = bass.Bass("TRN2", target_bir_lowering=False, debug=False)
        self.stack = ExitStack()
        self.b = Build(self.nc, self.stack)
        self.sb = {}

    def sbuf(self, name, shape, dtype=F32):
        t = self.stack.enter_context(self.nc.sbuf_tensor(name, list(shape), dtype))
        self.sb[name] = t
        return t

    def sbuf_alias(self, name, shape, dtype, offset):
        t = self.nc.alloc_sbuf_tensor_at(name, list(shape), dtype,
                                         offset=offset)
        self.sb[name] = t
        return t

    def psum(self, name, shape=(128, 512)):
        t = self.stack.enter_context(self.nc.psum_tensor(name, list(shape), F32))
        self.sb[name] = t
        return t

    def dram_in(self, name, shape, dtype=F32):
        return self.nc.dram_tensor(name, list(shape), dtype, kind="ExternalInput").ap()

    def dram_out(self, name, shape, dtype=F32):
        return self.nc.dram_tensor(name, list(shape), dtype, kind="ExternalOutput").ap()

    def dram_tmp(self, name, shape, dtype=F32):
        return self.nc.dram_tensor(name, list(shape), dtype, kind="Internal").ap()

    def cval(self, value, parts=1):
        if not hasattr(self, "_cvals"):
            self._cvals = {}
            self._cval_tile = self.sbuf("cvals", [128, 32], F32)
        if value not in self._cvals:
            j = len(self._cvals)
            assert j < 32
            self._cvals[value] = j
            sl = self._cval_tile[:, j:j + 1]
            self.b.op("dve", lambda e, sl=sl, v=value: e.memset(sl, v),
                      writes=["cvals"], tag="cval")
        j = self._cvals[value]
        return self._cval_tile[0:parts, j:j + 1]


def setup_consts(p):
    b = p.b
    ones_k = p.sbuf("ones_k", [128, 1], F32)
    ones_k16 = p.sbuf("ones_k16", [128, 1], BF16)
    ones_m = p.sbuf("ones_m", [1, 128], F32)
    ident = p.sbuf("ident", [128, 128], F32)
    tri = p.sbuf("tri", [128, 128], F32)
    iot = p.sbuf("iot", [128, 128], F32)
    b.op("dve", lambda e: e.memset(ones_k[:], 1.0), writes=["ones_k"])
    b.op("dve", lambda e: e.memset(ones_k16[:], 1.0), writes=["ones_k16"])
    b.op("dve", lambda e: e.memset(ones_m[:], 1.0), writes=["ones_m"])
    b.op("dve", lambda e: e.memset(p.sb["ones_t"][:], 1.0), writes=["ones_t"])
    b.op("pool", lambda e: e.iota(iot[:], pattern=[[-1, 128]], base=0,
                                  channel_multiplier=1,
                                  allow_small_or_imprecise_dtypes=True),
         writes=["iot"], tag="iota_ident")
    b.op("dve", lambda e: e.tensor_single_scalar(ident[:], iot[:], 0.0,
                                                 OP.is_equal),
         reads=["iot"], writes=["ident"], tag="ident")
    b.op("pool", lambda e: e.iota(iot[:], pattern=[[1, 128]], base=0,
                                  channel_multiplier=-1,
                                  allow_small_or_imprecise_dtypes=True),
         reads=["ident"], writes=["iot"], tag="iota_tri")
    b.op("dve", lambda e: e.tensor_single_scalar(tri[:], iot[:], 0.0,
                                                 OP.is_ge),
         reads=["iot"], writes=["tri"], tag="tri")



def ck(key, nch=NCH):
    return [key + "#%d" % c for c in range(nch)]


def load_fm(p, queue, dst, dst_key, src_hbm, t0, tw, nch=NCH, tag="ld",
            chunked=True):
    src = src_hbm.rearrange("(c p) t -> p c t", p=128)[:, :, t0:t0 + tw]
    d2 = dst.rearrange("p (c t) -> p c t", c=nch)
    wk = ck(dst_key, nch) if chunked else [dst_key]
    p.b.dma(queue, d2, src, reads=[], writes=wk, tag=tag)


def store_fm(p, queue, dst_hbm, dst_key, src, src_key, t0, tw, nch=NCH,
             tag="st", chunked=True):
    dst = dst_hbm.rearrange("(c p) t -> p c t", p=128)[:, :, t0:t0 + tw]
    s2 = src.rearrange("p (c t) -> p c t", c=nch)
    rk = ck(src_key, nch) if chunked else [src_key]
    p.b.dma(queue, dst, s2, reads=rk, writes=[dst_key], tag=tag)


def load_vec(p, queue, dst, dst_key, src_hbm, tag="ldv"):
    p.b.dma(queue, dst, src_hbm[:], reads=[], writes=[dst_key], tag=tag)


def load_w(p, queue, dst, dst_key, src_hbm, rows, cols, col0=0, ncols=None,
           tag="ldw"):
    if ncols is None:
        ncols = cols
    src = src_hbm[:, col0:col0 + ncols].rearrange("(c p) o -> p c o", p=128)
    d2 = dst.rearrange("p (c o) -> p c o", c=rows // 128)
    p.b.dma(queue, d2, src, reads=[], writes=[dst_key], tag=tag)



def emit_ln(p, x, xk, tw, banks, stat, statk, *, nch=NCH, nfeat=D,
            x_bf16=False, scale_ap=None, scale_key=None, sq=None, sqk=None,
            sq_slots=None, tag="ln"):
    b = p.b
    ones_k = p.sb["ones_k16"] if x_bf16 else p.sb["ones_k"]
    ones_kk = "ones_k16" if x_bf16 else "ones_k"
    ones_f = p.sb["ones_k"]
    ones_m = p.sb["ones_m"]
    if sq is None:
        sq = p.sb["sq_t"][:]
        sqk = "sq_t"

    def mm_s(eng):
        ins = None
        for c in range(nch):
            ins = eng.matmul(banks["sa"], ones_k[:], x[:, c * tw:(c + 1) * tw],
                             start=(c == 0), stop=(c == nch - 1))
        return ins

    b.op("pe", mm_s, reads=ck(xk, nch) + [ones_kk], writes=[banks["ka"]],
         tag=tag + "_ss")
    if sq_slots is None:
        sqks = [sqk] if sqk == "sq_t" else ck(sqk, nch)
        b.op("act", lambda e: e.square(sq[:, :nch * tw], x[:, :nch * tw]),
             reads=ck(xk, nch), writes=sqks, tag=tag + "_sq")

        def mm_q(eng):
            ins = None
            for c in range(nch):
                ins = eng.matmul(banks["sb"], ones_f[:],
                                 sq[:, c * tw:(c + 1) * tw],
                                 start=(c == 0), stop=(c == nch - 1))
            return ins

        b.op("pe", mm_q, reads=sqks + ["ones_k"], writes=[banks["kb"]],
             tag=tag + "_sq2")
    else:
        nsl = sq_slots
        for c in range(nch):
            sl = sq[:, (c % nsl) * tw:(c % nsl + 1) * tw]
            b.op("act", lambda e, sl=sl, c=c: e.square(
                     sl, x[:, c * tw:(c + 1) * tw]),
                 reads=[xk + "#%d" % c], writes=[sqk + "#%d" % (c % nsl)],
                 tag=tag + "_sq%d" % c)
        for c in range(nch):
            sl = sq[:, (c % nsl) * tw:(c % nsl + 1) * tw]
            b.op("pe", lambda e, sl=sl, c=c: e.matmul(
                     banks["sb"], ones_f[:], sl,
                     start=(c == 0), stop=(c == nch - 1)),
                 reads=[sqk + "#%d" % (c % nsl), "ones_k"],
                 writes=[banks["kb"]], tag=tag + "_sq2_%d" % c)
    r = stat[0:1, 0:tw]
    mur = stat[0:1, tw:2 * tw]
    mu = stat[0:1, 2 * tw:3 * tw]
    v = stat[0:1, 3 * tw:4 * tw]
    m2 = stat[0:1, 4 * tw:5 * tw]
    b.op("act", lambda e: e.mul(mu, banks["sa"], 1.0 / nfeat),
         reads=[banks["ka"]], writes=[statk], tag=tag + "_mu")
    b.op("act", lambda e: e.mul(v, banks["sb"], 1.0 / nfeat),
         reads=[banks["kb"]], writes=[statk], tag=tag + "_ex2")
    b.op("dve", lambda e: e.tensor_mul(m2, mu, mu), reads=[statk],
         writes=[statk], tag=tag + "_mu2")
    b.op("dve", lambda e: e.tensor_sub(v, v, m2), reads=[statk],
         writes=[statk], tag=tag + "_var")
    epsap = p.cval(EPS)
    b.op("act", lambda e: e.activation(m2, v, ACT.Sqrt, bias=epsap),
         reads=[statk, "cvals"], writes=[statk], tag=tag + "_sd")
    b.op("dve", lambda e: e.reciprocal(r, m2),
         reads=[statk], writes=[statk], tag=tag + "_r")
    b.op("dve", lambda e: e.tensor_mul(mur, mu, r), reads=[statk],
         writes=[statk], tag=tag + "_mur")
    if scale_ap is not None:
        rm2 = stat[0:1, 0:2 * tw]
        b.op("dve", lambda e: e.tensor_scalar(rm2, rm2, scale_ap, None,
                                              OP.mult),
             reads=[statk, scale_key], writes=[statk], tag=tag + "_fs")
    b.op("pe", lambda e: e.matmul(banks["br"], ones_m[:], r, start=True,
                                  stop=True),
         reads=[statk, "ones_m"], writes=[banks["kr"]], tag=tag + "_bcr")
    b.op("pe", lambda e: e.matmul(banks["bm"], ones_m[:], mur, start=True,
                                  stop=True),
         reads=[statk, "ones_m"], writes=[banks["km"]], tag=tag + "_bcm")


def ln_apply(p, x, xk, ln, lnk, tw, banks, nch=NCH, tag="lnap"):
    b = p.b
    for c in range(nch):
        xs = x[:, c * tw:(c + 1) * tw]
        ls = ln[:, c * tw:(c + 1) * tw]
        b.op("dve", lambda e, xs=xs, ls=ls: e.tensor_mul(ls, xs, banks["br"]),
             reads=[xk + "#%d" % c, banks["kr"]], writes=[lnk + "#%d" % c],
             tag=tag + "_m%d" % c)
    for c in range(nch):
        ls = ln[:, c * tw:(c + 1) * tw]
        b.op("dve", lambda e, ls=ls: e.tensor_sub(ls, ls, banks["bm"]),
             reads=[lnk + "#%d" % c, banks["km"]], writes=[lnk + "#%d" % c],
             tag=tag + "_s%d" % c)


def ln_apply_residual(p, y, yk, x, xk, out, outk, tw, banks, nch=NCH,
                      tag="lnres"):
    b = p.b
    for c in range(nch):
        ys = y[:, c * tw:(c + 1) * tw]
        os_ = out[:, c * tw:(c + 1) * tw]
        b.op("dve", lambda e, ys=ys, os_=os_: e.tensor_mul(os_, ys, banks["br"]),
             reads=[yk + "#%d" % c, banks["kr"]], writes=[outk + "#%d" % c],
             tag=tag + "_m%d" % c)
    for c in range(nch):
        os_ = out[:, c * tw:(c + 1) * tw]
        b.op("dve", lambda e, os_=os_: e.tensor_sub(os_, os_, banks["bm"]),
             reads=[outk + "#%d" % c, banks["km"]], writes=[outk + "#%d" % c],
             tag=tag + "_s%d" % c)
    for c in range(nch):
        xs = x[:, c * tw:(c + 1) * tw]
        os_ = out[:, c * tw:(c + 1) * tw]
        b.op("dve", lambda e, os_=os_, xs=xs: e.tensor_add(os_, os_, xs),
             reads=[outk + "#%d" % c, xk + "#%d" % c],
             writes=[outk + "#%d" % c], tag=tag + "_a%d" % c)


def std_banks(p, tw):
    ps = [p.sb[f"ps{i}"] for i in range(8)]
    return dict(sa=ps[6][0:1, 0:tw], sb=ps[7][0:1, 0:tw],
                br=ps[4][:, 0:tw], bm=ps[5][:, 0:tw],
                ka="ps6", kb="ps7", kr="ps4", km="ps5")



def _prep_weights(inputs):
    w = {}
    T = lambda a: np.ascontiguousarray(np.asarray(a).T)
    V = lambda v: np.ascontiguousarray(np.asarray(v).reshape(-1, 128).T)
    w["w_qkv_t"] = T(inputs["L0_attn_w_qkv"])
    w["b_qkv"] = V(inputs["L0_attn_b_qkv"])
    w["b_v_row"] = np.ascontiguousarray(
        np.asarray(inputs["L0_attn_b_qkv"])[2048:].reshape(1, 1024))
    w["w_g1_t"] = T(inputs["L0_attn_w_g1"])
    w["b_g1"] = np.ascontiguousarray(
        np.asarray(inputs["L0_attn_b_g1"]).reshape(64, 1))
    w["w_g2_t"] = T(inputs["L0_attn_w_g2"])
    w["b_g2"] = V(inputs["L0_attn_b_g2"])
    w["w_out_t"] = T(inputs["L0_attn_w_out"])
    w["b_out"] = V(inputs["L0_attn_b_out"])
    w["ln_g"] = V(inputs["L0_attn_ln_g"])
    w["ln_b"] = V(inputs["L0_attn_ln_b"])
    w["ffn_k"] = np.asarray(inputs["L0_ffn_kernel"])
    w["ffn_sin"] = np.asarray(inputs["L0_ffn_input_scale"])
    w["ffn_sk"] = np.asarray(inputs["L0_ffn_kernel_scale"])
    w["ffn_bias"] = V(inputs["L0_ffn_bias"])
    w["wq_t"] = T(inputs["L1_attn_Wq"])
    w["wk_t"] = T(inputs["L1_attn_Wk"])
    w["wv_t"] = T(inputs["L1_attn_Wv"])
    w["wf_t"] = T(inputs["L1_attn_Wf"])
    w["g_w"] = V(inputs["L1_attn_g_w"])
    w["wo_t"] = T(inputs["L1_attn_Wo"])
    w["wfc_t"] = T(inputs["L1_ffn_W_fc"])
    w["b_fc"] = V(inputs["L1_ffn_b_fc"])
    w["afc_t"] = T(inputs["L1_ffn_A_fc"])
    w["bfc_t"] = T(inputs["L1_ffn_B_fc"])
    w["wpr_t"] = T(inputs["L1_ffn_W_proj"])
    w["b_pr"] = V(inputs["L1_ffn_b_proj"])
    w["apr_t"] = T(inputs["L1_ffn_A_proj"])
    w["bpr_t"] = T(inputs["L1_ffn_B_proj"])
    return w


def _prep_x(inputs):
    x = np.asarray(inputs["L0_attn_x"])
    xs = []
    for c in range(8):
        xc = np.concatenate([x[2 * c], x[2 * c + 1]], axis=0)
        xs.append(np.ascontiguousarray(xc.T))
    return xs


def kernel(**inputs):
    w = _prep_weights(inputs)
    xs = _prep_x(inputs)
    p = build_program()
    in_maps = []
    for c in range(8):
        m = dict(w)
        m["x_fm"] = xs[c]
        in_maps.append(m)
    with p.stack:
        res = run_bass_kernel_spmd(p.nc, in_maps, core_ids=list(range(8)))
    outs = []
    for c in range(8):
        y = res.results[c]["y_fm"]
        outs.append(y[:, :SEQ].T)
        outs.append(y[:, SEQ:].T)
    return np.ascontiguousarray(np.stack(outs, axis=0).astype(np.float32))



def alloc_globals(p):
    for i in range(8):
        p.psum(f"ps{i}")
    p.off = {}
    rup = lambda a: (a + 31) & ~31
    for i in range(3):
        p.off[f"t{i}"] = rup(p.nc._sbuf_addr_for_side(None))
        p.sbuf(f"t{i}", [128, 4096], F32)
    p.sbuf("sq_t", [128, 4096], F32)
    p.off["wbig"] = rup(p.nc._sbuf_addr_for_side(None))
    p.sbuf("wbig", [128, 16384], F32)
    p.sbuf("u0", [128, 4096], BF16)
    p.sbuf_alias("hb", [128, 4096], BF16, p.off["wbig"])
    p.sbuf_alias("lb", [128, 4096], BF16, p.off["wbig"] + 8192)
    p.sbuf_alias("mb", [128, 4096], BF16, p.off["wbig"] + 16384)
    p.sbuf_alias("wq16", [128, 8192], BF16, p.off["wbig"] + 24576)
    p.sbuf_alias("d0w16", [128, 8192], BF16, p.off["t1"])
    p.sbuf_alias("d2h", [128, 16384], BF16, p.off["wbig"])
    p.sbuf_alias("d2l", [128, 16384], BF16, p.off["wbig"] + 32768)
    p.sbuf_alias("wq16d", [128, 8192], BF16, p.off["t0"])
    p.sbuf("kq8", [128, 8192], FP8)
    p.sbuf("q8t", [128, 4096], FP8)
    p.sbuf("stat", [1, 5 * 512], F32)
    p.sbuf("ones_t", [128, 256], F32)
    p.sbuf("c_mx", [128, 4], F32)
    p.sbuf("c_fr", [1, 256], F32)
    p.sbuf("c_sc", [128, 256], F32)
    p.sbuf("c_asb", [128, 256], F32)
    p.sbuf("c_utm", [128, 256], F32)
    setup_consts(p)


def build_program(phases=("A", "B", "C", "D")):
    p = Prog()
    alloc_globals(p)
    x_in = p.dram_in("x_fm", [D, NTOK], F32)
    y_out = p.dram_out("y_fm", [D, NTOK], F32)
    s1 = p.dram_tmp("s1", [D, NTOK], F32)
    s2 = p.dram_tmp("s2", [D, NTOK], F32)
    scr = [(s1, "s1"), (s2, "s2")]
    chain = [ph for ph in ("A", "B", "C", "D") if ph in phases]
    emits = {"A": emit_phase_a, "B": emit_phase_b, "C": emit_phase_c,
             "D": emit_phase_d}
    cur = (x_in, "x_fm")
    for i, ph in enumerate(chain):
        dst = (y_out, "y_fm") if i == len(chain) - 1 else scr[i % 2]
        emits[ph](p, cur, dst)
        cur = dst
    p.b.finalize()
    return p



def emit_phase_a(p, src, dst):
    src_hbm, _ = src
    dst_hbm, dst_key = dst
    b = p.b
    TW = 256
    nt = SEQ // TW
    nsub = TW // 128
    ps = [p.sb[f"ps{i}"] for i in range(8)]
    banks = std_banks(p, TW)
    t = [p.sb[f"t{i}"] for i in range(3)]
    sq_hi = p.sb["sq_t"][:, 2048:4096]
    ek_t, ekk = t[2][:, 0:2048], "a_ek"
    ektm, ekmk = t[2][:, 2048:4096], "a_ektm"
    v_tm, vtmk = sq_hi, "sq_hi"
    q_t, qk = t[2][:, 0:2048], "a_q"
    ao_t, aok = t[2][:, 2048:4096], "a_ao"
    lno_t, lnok = sq_hi, "sq_hi"
    stat = p.sb["stat"]
    wbig = p.sb["wbig"]
    ident = p.sb["ident"]
    ones_m = p.sb["ones_m"]

    bqkv = p.sbuf("a_bqkv", [128, 24], F32)
    load_vec(p, "sp", bqkv[:], "a_bqkv", p.dram_in("b_qkv", [128, 24], F32))
    bvrow = p.sbuf("a_bvrow", [1, D], F32)
    load_vec(p, "sp", bvrow[:], "a_bvrow", p.dram_in("b_v_row", [1, D], F32))
    wg1 = p.sbuf("a_wg1", [128, NCH * 64], F32)
    load_w(p, "sp", wg1[:], "a_wg1", p.dram_in("w_g1_t", [D, 64], F32), D, 64)
    bg1 = p.sbuf("a_bg1", [64, 1], F32)
    load_vec(p, "sp", bg1[:], "a_bg1", p.dram_in("b_g1", [64, 1], F32))
    wg2 = p.sbuf("a_wg2", [64, D], F32)
    load_vec(p, "sp", wg2[:], "a_wg2", p.dram_in("w_g2_t", [64, D], F32))
    bg2 = p.sbuf("a_bg2", [128, NCH], F32)
    load_vec(p, "sp", bg2[:], "a_bg2", p.dram_in("b_g2", [128, NCH], F32))
    bout = p.sbuf("a_bout", [128, NCH], F32)
    load_vec(p, "sp", bout[:], "a_bout", p.dram_in("b_out", [128, NCH], F32))
    lng = p.sbuf("a_lng", [128, NCH], F32)
    load_vec(p, "sp", lng[:], "a_lng", p.dram_in("ln_g", [128, NCH], F32))
    lnb = p.sbuf("a_lnb", [128, NCH], F32)
    load_vec(p, "sp", lnb[:], "a_lnb", p.dram_in("ln_b", [128, NCH], F32))
    sacc = p.sbuf("a_sacc", [128, NCH], F32)
    tacc = p.sbuf("a_tacc", [128, NCH], F32)
    kv_sb = p.sbuf("a_kv", [128, NCH * 64], F32)
    srec = p.sbuf("a_srec", [128, NCH], F32)
    gh_t = p.sbuf("a_gh", [64, TW], F32)
    gtmp = p.sbuf("a_gtmp", [128, TW], F32)
    wqkv_h = p.dram_in("w_qkv_t", [D, 3072], F32)
    w_out_h = p.dram_in("w_out_t", [D, D], F32)

    def ptile(i, it):
        par = it % 2
        return t[i][:, par * 2048:(par + 1) * 2048], "a_t%d_%d" % (i, par)

    for sq_ in range(2):
        base = sq_ * SEQ
        b.op("dve", lambda e: e.memset(sacc[:], 0.0), writes=["a_sacc"],
             tag="a_z")
        load_w(p, "sp", wbig[:, 0:NCH * D], "wbig", wqkv_h, D, 3072,
               col0=1024, ncols=D, tag="a_ldk")
        load_w(p, "sp", wbig[:, NCH * D:2 * NCH * D], "wbig", wqkv_h, D, 3072,
               col0=2048, ncols=D, tag="a_ldv")

        for it in range(nt):
            t0 = base + it * TW
            x_t, xk = ptile(0, it)
            ln_t, lnk = ptile(1, it)
            load_fm(p, "sp", x_t, xk, src_hbm, t0, TW, tag="a1_ldx%d" % it)
            emit_ln(p, x_t, xk, TW, banks, stat, "stat", tag="a1_ln%d" % it)
            ln_apply(p, x_t, xk, ln_t, lnk, TW, banks, tag="a1_lnap%d" % it)
            for o in range(NCH):
                bank = ps[o % 2]
                bkey = "ps%d" % (o % 2)

                def mm(eng, o=o, bank=bank, ln_t=ln_t):
                    ins = None
                    for c in range(NCH):
                        col = c * D + o * 128
                        ins = eng.matmul(bank[:, 0:TW], wbig[:, col:col + 128],
                                         ln_t[:, c * TW:(c + 1) * TW],
                                         start=(c == 0), stop=(c == NCH - 1))
                    return ins

                b.op("pe", mm, reads=["wbig"] + ck(lnk), writes=[bkey],
                     tag="a1_kmm%d_%d" % (it, o))
                eksl = ek_t[:, o * TW:(o + 1) * TW]
                b.op("act", lambda e, eksl=eksl, bank=bank, o=o: e.activation(
                         eksl, bank[:, 0:TW], ACT.Exp,
                         bias=bqkv[:, 8 + o:9 + o],
                         accum_out=tacc[:, o:o + 1]),
                     reads=[bkey, "a_bqkv"],
                     writes=[ekk + "#%d" % o, "a_tacc"],
                     tag="a1_exp%d_%d" % (it, o))
                b.op("dve", lambda e, o=o: e.tensor_add(
                         sacc[:, o:o + 1], sacc[:, o:o + 1], tacc[:, o:o + 1]),
                     reads=["a_tacc", "a_sacc"], writes=["a_sacc"],
                     tag="a1_sacc%d_%d" % (it, o))
            for half_ in range(2):
                for ts in range(nsub):
                    bank = ps[half_]
                    bkey = "ps%d" % half_

                    def mmv(eng, half_=half_, ts=ts, bank=bank, ln_t=ln_t):
                        ins = None
                        for c in range(NCH):
                            ins = eng.matmul(
                                bank[:, 0:512],
                                ln_t[:, c * TW + ts * 128:
                                     c * TW + ts * 128 + 128],
                                wbig[:, NCH * D + c * D + half_ * 512:
                                     NCH * D + c * D + (half_ + 1) * 512],
                                start=(c == 0), stop=False)
                        ins = eng.matmul(
                            bank[:, 0:512], ones_m[:],
                            bvrow[0:1, half_ * 512:(half_ + 1) * 512],
                            start=False, stop=True)
                        return ins

                    b.op("pe", mmv,
                         reads=["wbig", "ones_m", "a_bvrow"] + ck(lnk),
                         writes=[bkey], tag="a1_vmm%d_%d_%d" % (it, half_, ts))
                    b.op("act", lambda e, half_=half_, ts=ts, bank=bank: e.copy(
                             v_tm[:, ts * D + half_ * 512:
                                  ts * D + (half_ + 1) * 512],
                             bank[:, 0:512]),
                         reads=[bkey], writes=[vtmk + "#%d" % (ts * 2 + half_)],
                         tag="a1_vcp%d_%d_%d" % (it, half_, ts))
            for o in range(NCH):
                bank = ps[o % 2]
                bkey = "ps%d" % (o % 2)

                def mmt(eng, o=o, bank=bank):
                    ins = None
                    for ts in range(nsub):
                        ins = eng.matmul(bank[:, ts * 128:(ts + 1) * 128],
                                         ek_t[:, o * TW + ts * 128:
                                              o * TW + (ts + 1) * 128],
                                         ident[:], is_transpose=True,
                                         start=(ts == 0), stop=(ts == nsub - 1))
                    return ins

                b.op("pe", mmt, reads=[ekk + "#%d" % o, "ident"],
                     writes=[bkey], tag="a1_ekT%d_%d" % (it, o))

                def cpt(eng, o=o, bank=bank):
                    ins = None
                    for ts in range(nsub):
                        ins = eng.tensor_copy(
                            ektm[:, ts * D + o * 128: ts * D + o * 128 + 128],
                            bank[:, ts * 128:(ts + 1) * 128])
                    return ins

                b.op("dve", cpt, reads=[bkey], writes=[ekmk + "#%d" % o],
                     tag="a1_ekTc%d_%d" % (it, o))
            for hh in range(NCH):
                bankkv = ps[2 + hh // 4]
                bkey = "ps%d" % (2 + hh // 4)
                off = (hh % 4) * 128

                def mkv(eng, hh=hh, bankkv=bankkv, off=off, it=it):
                    ins = None
                    for ts in range(nsub):
                        first = (it == 0 and hh % 4 == 0 and ts == 0)
                        last = (it == nt - 1 and hh % 4 == 3 and ts == nsub - 1)
                        ins = eng.matmul(
                            bankkv[:, off:off + 128],
                            ektm[:, ts * D + hh * 128: ts * D + hh * 128 + 128],
                            v_tm[:, ts * D + hh * 128: ts * D + hh * 128 + 128],
                            start=first, stop=last)
                    return ins

                vu = [vtmk + "#%d" % (ts * 2 + hh // 4) for ts in range(2)]
                b.op("pe", mkv, reads=[ekmk + "#%d" % hh] + vu, writes=[bkey],
                     tag="a1_kv%d_%d" % (it, hh))
        b.op("dve", lambda e: e.reciprocal(srec[:], sacc[:]),
             reads=["a_sacc"], writes=["a_srec"], tag="a_srec")
        for hh in range(NCH):
            bankkv = ps[2 + hh // 4]
            bkey = "ps%d" % (2 + hh // 4)
            off = (hh % 4) * 128
            b.op("dve", lambda e, hh=hh, bankkv=bankkv, off=off: e.tensor_scalar(
                     kv_sb[0:64, hh * 64:(hh + 1) * 64],
                     bankkv[0:64, off:off + 64], srec[0:64, hh:hh + 1], None,
                     OP.mult),
                 reads=[bkey, "a_srec"], writes=["a_kv"], tag="a_kvn%d_0" % hh)
            b.op("dve", lambda e, hh=hh, bankkv=bankkv, off=off: e.tensor_scalar(
                     kv_sb[64:128, hh * 64:(hh + 1) * 64],
                     bankkv[64:128, off + 64:off + 128], srec[64:128, hh:hh + 1],
                     None, OP.mult),
                 reads=[bkey, "a_srec"], writes=["a_kv"], tag="a_kvn%d_1" % hh)

        load_w(p, "sp", wbig[:, 0:NCH * D], "wbig", wqkv_h, D, 3072, col0=0,
               ncols=D, tag="a_ldq")
        load_w(p, "sp", wbig[:, NCH * D:2 * NCH * D], "wbig", w_out_h, D, D,
               tag="a_ldwo")

        for it in range(nt):
            t0 = base + it * TW
            x_t, xk = ptile(0, it)
            ln_t, lnk = ptile(1, it)
            xo_t, xok = ln_t, lnk
            load_fm(p, "sp", x_t, xk, src_hbm, t0, TW, tag="a2_ldx%d" % it)
            emit_ln(p, x_t, xk, TW, banks, stat, "stat", tag="a2_ln%d" % it)
            ln_apply(p, x_t, xk, ln_t, lnk, TW, banks, tag="a2_lnap%d" % it)
            for o in range(NCH):
                bank = ps[o % 2]
                bkey = "ps%d" % (o % 2)

                def mm(eng, o=o, bank=bank, ln_t=ln_t):
                    ins = None
                    for c in range(NCH):
                        col = c * D + o * 128
                        ins = eng.matmul(bank[:, 0:TW], wbig[:, col:col + 128],
                                         ln_t[:, c * TW:(c + 1) * TW],
                                         start=(c == 0), stop=(c == NCH - 1))
                    return ins

                b.op("pe", mm, reads=["wbig"] + ck(lnk), writes=[bkey],
                     tag="a2_qmm%d_%d" % (it, o))
                qsl = q_t[:, o * TW:(o + 1) * TW]
                b.op("act", lambda e, qsl=qsl, bank=bank, o=o: e.activation(
                         qsl, bank[:, 0:TW], ACT.Silu, bias=bqkv[:, o:o + 1]),
                     reads=[bkey, "a_bqkv"], writes=[qk + "#%d" % o],
                     tag="a2_silu%d_%d" % (it, o))
            for hh in range(NCH):
                bank = ps[2 + (hh % 2)]
                bkey = "ps%d" % (2 + (hh % 2))

                def mma(eng, hh=hh, bank=bank):
                    ins = None
                    for sub in range(2):
                        rb = sub * 64
                        ins = eng.matmul(
                            bank[rb:rb + 64, 0:TW],
                            kv_sb[rb:rb + 64, hh * 64:(hh + 1) * 64],
                            q_t[rb:rb + 64, hh * TW:(hh + 1) * TW],
                            start=True, stop=True)
                    return ins

                b.op("pe", mma, reads=["a_kv", qk + "#%d" % hh],
                     writes=[bkey], tag="a2_ao%d_%d" % (it, hh))
                b.op("act", lambda e, hh=hh, bank=bank: e.copy(
                         ao_t[:, hh * TW:(hh + 1) * TW], bank[:, 0:TW]),
                     reads=[bkey], writes=[aok + "#%d" % hh],
                     tag="a2_aocp%d_%d" % (it, hh))
            emit_ln(p, ao_t, aok, TW, banks, stat, "stat", tag="a2_lno%d" % it)
            for c in range(NCH):
                asl = ao_t[:, c * TW:(c + 1) * TW]
                lsl = lno_t[:, c * TW:(c + 1) * TW]
                b.op("dve", lambda e, asl=asl, lsl=lsl: e.tensor_mul(
                         lsl, asl, banks["br"]),
                     reads=[aok + "#%d" % c, banks["kr"]],
                     writes=[lnok + "#%d" % c],
                     tag="a2_lnoap%d_m%d" % (it, c))
            for c in range(NCH):
                lsl = lno_t[:, c * TW:(c + 1) * TW]
                b.op("dve", lambda e, lsl=lsl: e.tensor_sub(lsl, lsl,
                                                            banks["bm"]),
                     reads=[lnok + "#%d" % c, banks["km"]],
                     writes=[lnok + "#%d" % c],
                     tag="a2_lnoap%d_s%d" % (it, c))
            for c in range(NCH):
                lsl = lno_t[:, c * TW:(c + 1) * TW]
                b.op("dve", lambda e, lsl=lsl, c=c: e.tensor_scalar(
                         lsl, lsl, lng[:, c:c + 1], lnb[:, c:c + 1],
                         OP.mult, OP.add),
                     reads=[lnok + "#%d" % c, "a_lng", "a_lnb"],
                     writes=[lnok + "#%d" % c],
                     tag="a2_lnogb%d_%d" % (it, c))

            def mmg1(eng, ln_t=ln_t):
                ins = None
                for c in range(NCH):
                    ins = eng.matmul(ps[3][0:64, 0:TW],
                                     wg1[:, c * 64:(c + 1) * 64],
                                     ln_t[:, c * TW:(c + 1) * TW],
                                     start=(c == 0), stop=(c == NCH - 1))
                return ins

            b.op("pe", mmg1, reads=["a_wg1"] + ck(lnk), writes=["ps3"],
                 tag="a2_g1%d" % it)
            b.op("act", lambda e: e.activation(gh_t[:], ps[3][0:64, 0:TW],
                                               ACT.Identity, bias=bg1[:, 0:1]),
                 reads=["ps3", "a_bg1"], writes=["a_gh"], tag="a2_g1b%d" % it)
            for o in range(NCH):
                bank = ps[o % 2]
                bkey = "ps%d" % (o % 2)
                b.op("pe", lambda e, o=o, bank=bank: e.matmul(
                         bank[:, 0:TW], wg2[:, o * 128:(o + 1) * 128], gh_t[:],
                         start=True, stop=True),
                     reads=["a_wg2", "a_gh"], writes=[bkey],
                     tag="a2_g2%d_%d" % (it, o))
                b.op("act", lambda e, bank=bank, o=o: e.activation(
                         gtmp[:], bank[:, 0:TW], ACT.Sigmoid,
                         bias=bg2[:, o:o + 1]),
                     reads=[bkey, "a_bg2"], writes=["a_gtmp"],
                     tag="a2_sig%d_%d" % (it, o))
                lsl = lno_t[:, o * TW:(o + 1) * TW]
                b.op("dve", lambda e, lsl=lsl: e.tensor_mul(lsl, gtmp[:], lsl),
                     reads=["a_gtmp", lnok + "#%d" % o],
                     writes=[lnok + "#%d" % o],
                     tag="a2_gmul%d_%d" % (it, o))
            for o in range(NCH):
                bank = ps[2 + (o % 2)]
                bkey = "ps%d" % (2 + (o % 2))

                def mmo(eng, o=o, bank=bank):
                    ins = None
                    for c in range(NCH):
                        col = NCH * D + c * D + o * 128
                        ins = eng.matmul(bank[:, 0:TW], wbig[:, col:col + 128],
                                         lno_t[:, c * TW:(c + 1) * TW],
                                         start=(c == 0), stop=(c == NCH - 1))
                    return ins

                b.op("pe", mmo, reads=["wbig"] + ck(lnok), writes=[bkey],
                     tag="a2_wo%d_%d" % (it, o))
                b.op("act", lambda e, o=o, bank=bank: e.activation(
                         ao_t[:, o * TW:(o + 1) * TW], bank[:, 0:TW],
                         ACT.Identity, bias=bout[:, o:o + 1]),
                     reads=[bkey, "a_bout"], writes=[aok + "#%d" % o],
                     tag="a2_wob%d_%d" % (it, o))
            emit_ln(p, ao_t, aok, TW, banks, stat, "stat",
                    tag="a2_ln2_%d" % it)
            ln_apply_residual(p, ao_t, aok, x_t, xk, xo_t, xok, TW, banks,
                              tag="a2_res%d" % it)
            store_fm(p, "sp", dst_hbm, dst_key, xo_t, xok, t0, TW,
                     tag="a2_st%d" % it)



def emit_phase_b(p, src, dst):
    src_hbm, _ = src
    dst_hbm, dst_key = dst
    b = p.b
    TW = 256
    nt = NTOK // TW
    ps = [p.sb[f"ps{i}"] for i in range(8)]
    banks = std_banks(p, TW)
    t = [p.sb[f"t{i}"] for i in range(3)]
    stat = p.sb["stat"]
    u0 = p.sb["u0"]
    kq8 = p.sb["kq8"]
    q8t = p.sb["q8t"]
    ones_m = p.sb["ones_m"]
    b.barrier("b_bar")

    bias16 = p.sbuf("b_bias16", [128, NCH], BF16)
    load_vec(p, "sp", bias16[:], "b_bias16",
             p.dram_in("ffn_bias", [128, D // 128], BF16))
    bias32 = p.sbuf("b_bias32", [128, NCH], F32)
    b.op("dve", lambda e: e.tensor_copy(bias32[:], bias16[:]),
         reads=["b_bias16"], writes=["b_bias32"], tag="b_b32")
    scal = p.sbuf("b_scal", [1, 4], F32)
    b.op("dve", lambda e: e.memset(scal[:], 0.0), writes=["b_scal"], tag="b_z")
    sin_h = p.dram_in("ffn_sin", [1], F32)
    sk_h = p.dram_in("ffn_sk", [1], F32)
    b.dma("sp", scal[0:1, 0:1], sin_h[:], reads=["b_scal"], writes=["b_scal"])
    b.dma("sp", scal[0:1, 3:4], sk_h[:], reads=["b_scal"], writes=["b_scal"])
    b.op("dve", lambda e: e.tensor_mul(scal[0:1, 1:2], scal[0:1, 0:1],
                                       scal[0:1, 3:4]),
         reads=["b_scal"], writes=["b_scal"], tag="sprod")
    scalb = p.sbuf("b_scalb", [128, 2], F32)
    skb = p.sbuf("b_skb", [128, 1], F32)
    b.op("pe", lambda e: e.matmul(ps[3][:, 0:4], ones_m[:], scal[0:1, 0:4],
                                  start=True, stop=True),
         reads=["b_scal", "ones_m"], writes=["ps3"], tag="bc_scal_mm")
    b.op("dve", lambda e: e.tensor_copy(scalb[:], ps[3][:, 0:2]),
         reads=["ps3"], writes=["b_scalb"], tag="bc_scal_cp")
    b.op("dve", lambda e: e.tensor_copy(skb[:], ps[3][:, 3:4]),
         reads=["ps3"], writes=["b_skb"], tag="bc_sk_cp")
    rsin = p.sbuf("b_rsin", [128, 1], F32)
    rsk = p.sbuf("b_rsk", [128, 1], F32)
    b.op("dve", lambda e: e.reciprocal(rsin[:], scalb[:, 0:1]),
         reads=["b_scalb"], writes=["b_rsin"], tag="rsin")
    b.op("dve", lambda e: e.reciprocal(rsk[:], skb[:]),
         reads=["b_skb"], writes=["b_rsk"], tag="rsk")
    ffnk_h = p.dram_in("ffn_k", [D, D], BF16)
    for hf in range(2):
        load_w(p, "sp", u0[:, 0:NCH * 512], "u0", ffnk_h, D, D,
               col0=hf * 512, ncols=512, tag="b_ldk%d" % hf)
        for c in range(NCH):
            sl_in = u0[:, c * 512:(c + 1) * 512]
            sl_out = kq8[:, c * D + hf * 512: c * D + (hf + 1) * 512]
            b.op("dve", lambda e, a=sl_out, x=sl_in: e.tensor_scalar(
                     a, x, rsk[:, 0:1], None, OP.mult),
                 reads=["u0", "b_rsk"], writes=["kq8#%d_%d" % (hf, c)],
                 tag="b_wq8_%d_%d" % (hf, c))

    for it in range(nt):
        t0_ = it * TW
        par = it % 2
        off = par * NCH * TW
        x_t, xk = t[0][:, off:off + NCH * TW], "b_x%d" % par
        ln_t, lnk = t[1][:, off:off + NCH * TW], "b_ln%d" % par
        xo_t, xok = t[2][:, off:off + NCH * TW], "b_xo%d" % par
        xb = u0[:, off:off + NCH * TW]
        xbk = "b_u%d" % par
        q8 = q8t[:, off:off + NCH * TW]
        q8k = "b_q8%d" % par
        gb = [ps[2 * par], ps[2 * par + 1]]
        gbk = ["ps%d" % (2 * par), "ps%d" % (2 * par + 1)]
        load_fm(p, "sp", x_t, xk, src_hbm, t0_, TW, tag="b_ldx%d" % it)
        emit_ln(p, x_t, xk, TW, banks, stat, "stat", tag="b_ln%d" % it)
        ln_apply(p, x_t, xk, ln_t, lnk, TW, banks, tag="b_lnap%d" % it)
        b.op("dve", lambda e, xb=xb, ln_t=ln_t: e.tensor_copy(xb, ln_t),
             reads=ck(lnk), writes=ck(xbk), tag="b_bf%d" % it)
        b.op("dve", lambda e, q8=q8, xb=xb: e.tensor_scalar(
                 q8, xb, rsin[:, 0:1], None, OP.mult),
             reads=ck(xbk) + ["b_rsin"], writes=ck(q8k),
             tag="b_q8_%d" % it)
        for o in range(NCH):
            bank = gb[o % 2]
            bkey = gbk[o % 2]

            def mm(eng, o=o, bank=bank, q8=q8):
                ins = None
                for c in range(NCH):
                    ins = eng.matmul(bank[:, 0:TW],
                                     kq8[:, c * D + o * 128: c * D + o * 128 + 128],
                                     q8[:, c * TW:(c + 1) * TW],
                                     start=(c == 0), stop=(c == NCH - 1))
                return ins

            b.op("pe", mm,
                 reads=["kq8#%d_%d" % (o // 4, c) for c in range(NCH)]
                 + ck(q8k), writes=[bkey], tag="b_mm%d_%d" % (it, o))
            ysl = xb[:, o * TW:(o + 1) * TW]
            b.op("act", lambda e, ysl=ysl, bank=bank: e.mul(
                     ysl, bank[:, 0:TW], scalb[:, 1:2]),
                 reads=[bkey, "b_scalb"], writes=[xbk + "#%d" % o],
                 tag="b_deq%d_%d" % (it, o))
        for o in range(NCH):
            ysl = xb[:, o * TW:(o + 1) * TW]
            b.op("dve", lambda e, ysl=ysl, o=o: e.tensor_scalar(
                     ysl, ysl, bias32[:, o:o + 1], None, OP.add),
                 reads=[xbk + "#%d" % o, "b_bias32"],
                 writes=[xbk + "#%d" % o],
                 tag="b_bias%d_%d" % (it, o))
        emit_ln(p, xb, xbk, TW, banks, stat, "stat", x_bf16=True,
                tag="b_ln2_%d" % it)
        ln_apply_residual(p, xb, xbk, x_t, xk, xo_t, xok, TW, banks,
                          tag="b_res%d" % it)
        store_fm(p, "sp", dst_hbm, dst_key, xo_t, xok, t0_, TW,
                 tag="b_st%d" % it)




def emit_phase_c(p, src, dst):
    src_hbm, src_key = src
    dst_hbm, dst_key = dst
    b = p.b
    ps = [p.sb[f"ps{i}"] for i in range(8)]
    t = [p.sb[f"t{i}"] for i in range(3)]
    stat = p.sb["stat"]
    wbig = p.sb["wbig"]
    ident = p.sb["ident"]
    tri = p.sb["tri"]
    ones_m = p.sb["ones_m"]
    ones_t = p.sb["ones_t"]
    QS = p.dram_tmp("qs", [D, NTOK], F32)
    KS = p.dram_tmp("ks", [D, NTOOK := NTOK], F32)
    VS = p.dram_tmp("vs", [D, NTOK], F32)
    FS = p.dram_tmp("fs", [D, NTOK], F32)
    OS = p.dram_tmp("os", [D, NTOK], F32)
    wq_h = p.dram_in("wq_t", [D, D], F32)
    wk_h = p.dram_in("wk_t", [D, D], F32)
    wv_h = p.dram_in("wv_t", [D, D], F32)
    wf_h = p.dram_in("wf_t", [D, D], F32)
    wo_h = p.dram_in("wo_t", [D, D], F32)
    gw = p.sbuf("c_gw", [128, NCH], F32)
    load_vec(p, "sp", gw[:], "c_gw", p.dram_in("g_w", [128, NCH], F32))
    z128 = p.cval(0.0, 128)
    n1_128 = p.cval(-1.0, 128)
    one128 = p.cval(1.0, 128)
    eps128 = p.cval(EPS, 128)

    TW = min(256, NTOK)
    nt = NTOK // TW
    banks = std_banks(p, TW)
    b.barrier("c1a_bar")
    load_w(p, "sp", wbig[:, 0:NCH * D], "wbig", wq_h, D, D, tag="c_ldwq")
    load_w(p, "sp", wbig[:, NCH * D:2 * NCH * D], "wbig", wk_h, D, D,
           tag="c_ldwk")

    def c1tile(i, it):
        par = it % 2
        return t[i][:, par * NCH * TW:(par + 1) * NCH * TW], "c1_t%d_%d" % (i, par)

    def proj_pass(pref, w_off, act_kind, out_scratch, it, ln_t, lnk, oslot):
        o_t = t[2][:, oslot * NCH * TW:(oslot + 1) * NCH * TW]
        ok_ = "c1_o%d" % oslot
        for o in range(NCH):
            bank = ps[o % 2]
            bkey = "ps%d" % (o % 2)

            def mm(eng, o=o, bank=bank, w_off=w_off, ln_t=ln_t, TW=TW):
                ins = None
                for c in range(NCH):
                    col = w_off + c * D + o * 128
                    ins = eng.matmul(bank[:, 0:TW], wbig[:, col:col + 128],
                                     ln_t[:, c * TW:(c + 1) * TW],
                                     start=(c == 0), stop=(c == NCH - 1))
                return ins

            b.op("pe", mm, reads=["wbig"] + ck(lnk), writes=[bkey],
                 tag="%s_mm%d_%d" % (pref, it, o))
            osl = o_t[:, o * TW:(o + 1) * TW]
            if act_kind == "silu":
                b.op("act", lambda e, osl=osl, bank=bank, TW=TW: e.activation(
                         osl, bank[:, 0:TW], ACT.Silu, bias=z128),
                     reads=[bkey, "cvals"], writes=[ok_ + "#%d" % o],
                     tag="%s_silu%d_%d" % (pref, it, o))
            else:
                ssl = p.sb["sq_t"][:, 2048 + o * TW:2048 + (o + 1) * TW]
                b.op("act", lambda e, ssl=ssl, bank=bank, TW=TW: e.activation(
                         ssl, bank[:, 0:TW], ACT.Exp, bias=z128,
                         scale=n1_128),
                     reads=[bkey, "cvals"], writes=["sq_hi#%d" % o],
                     tag="%s_e%d_%d" % (pref, it, o))
                b.op("act", lambda e, osl=osl, ssl=ssl: e.activation(
                         osl, ssl, ACT.Ln, bias=one128),
                     reads=["sq_hi#%d" % o, "cvals"],
                     writes=[ok_ + "#%d" % o],
                     tag="%s_l%d_%d" % (pref, it, o))
                b.op("act", lambda e, osl=osl: e.mul(osl, osl, -0.125),
                     reads=[ok_ + "#%d" % o], writes=[ok_ + "#%d" % o],
                     tag="%s_s%d_%d" % (pref, it, o))
        store_fm(p, "sp", out_scratch, pref + "_scr", o_t, ok_, it * TW, TW,
                 tag="%s_st%d" % (pref, it))

    for it in range(nt):
        x_t, xk = c1tile(0, it)
        ln_t, lnk = c1tile(1, it)
        load_fm(p, "sp", x_t, xk, src_hbm, it * TW, TW, tag="c1a_ldx%d" % it)
        emit_ln(p, x_t, xk, TW, banks, stat, "stat", tag="c1a_ln%d" % it)
        ln_apply(p, x_t, xk, ln_t, lnk, TW, banks, tag="c1a_lnap%d" % it)
        proj_pass("cq", 0, "silu", QS, it, ln_t, lnk, 0)
        proj_pass("ck", NCH * D, "silu", KS, it, ln_t, lnk, 1)
    b.barrier("c1b_bar")
    load_w(p, "sp", wbig[:, 0:NCH * D], "wbig", wv_h, D, D, tag="c_ldwv")
    load_w(p, "sp", wbig[:, NCH * D:2 * NCH * D], "wbig", wf_h, D, D,
           tag="c_ldwf")
    for it in range(nt):
        x_t, xk = c1tile(0, it)
        ln_t, lnk = c1tile(1, it)
        load_fm(p, "sp", x_t, xk, src_hbm, it * TW, TW, tag="c1b_ldx%d" % it)
        emit_ln(p, x_t, xk, TW, banks, stat, "stat", tag="c1b_ln%d" % it)
        ln_apply(p, x_t, xk, ln_t, lnk, TW, banks, tag="c1b_lnap%d" % it)
        proj_pass("cv", 0, "silu", VS, it, ln_t, lnk, 0)
        proj_pass("cf", NCH * D, "logsig", FS, it, ln_t, lnk, 1)

    b.barrier("c2_bar")
    C = 128

    def sub(i, lo, n):
        return t[i][:, lo:lo + n]

    q2 = sub(0, 0, 256); k2 = sub(0, 256, 256); v2 = sub(0, 512, 256)
    f2 = sub(0, 768, 256); gam = sub(0, 1024, 256); expg = sub(0, 1280, 256)
    expng = sub(0, 1536, 256); s2v = sub(0, 1792, 256)
    ufm = sub(0, 2048, 256); ktm = sub(0, 2304, 256); utm = sub(0, 2560, 256)
    egtm = sub(0, 2816, 256); qvtm = sub(0, 3072, 256); qvfm = sub(0, 3328, 256)
    qtfm = sub(0, 3584, 256); vtm = sub(0, 3840, 256)
    ovtm = sub(1, 0, 256)
    asb = p.sb["c_asb"][:, 0:128]
    a2sb = p.sb["c_asb"][:, 128:256]
    ofm = sub(1, 512, 1024)
    osil = sub(1, 1536, 1024)
    osq = sub(1, 2560, 1024)
    K0 = {"q2": "c_t0", "k2": "c_t0", "v2": "c_t0", "f2": "c_t0"}
    t0k, t1k, t2k = "t0", "t1", "t2"

    def hload(scr, scrk, dstap, dk, h, tt0, tag):
        src2 = scr.rearrange("(c p) t -> p c t", p=128)[:, 2 * h:2 * h + 2,
                                                        tt0:tt0 + C]
        d2 = dstap.rearrange("p (c t) -> p c t", c=2)
        b.dma("sp", d2, src2, reads=[scrk], writes=[dk], tag=tag)

    for s_ in range(2):
        b.op("dve", lambda e: e.memset(t[2][:], 0.0),
             writes=["c_s1_%d" % hh for hh in range(4)]
             + ["c_s2_%d" % hh for hh in range(4)],
             tag="c2_z%d" % s_)
        for c_ in range(SEQ // C):
            tt0 = s_ * SEQ + c_ * C
            for h in range(4):
                S1 = t[2][:, h * 512: h * 512 + 512]
                S2 = t[2][:, 2048 + h * 512: 2048 + h * 512 + 512]
                tg = "c2_%d_%d_%d" % (s_, c_, h)
                hload(QS, "cq_scr", q2, "c_q2", h, tt0, tg + "_lq")
                hload(KS, "ck_scr", k2, "c_k2", h, tt0, tg + "_lk")
                hload(VS, "cv_scr", v2, "c_v2", h, tt0, tg + "_lv")
                hload(FS, "cf_scr", f2, "c_f2", h, tt0, tg + "_lf")
                def scan(eng):
                    ins = None
                    for blk in range(2):
                        ins = eng.tensor_tensor_scan(
                            gam[:, blk * C:(blk + 1) * C],
                            ones_t[:, 0:C], f2[:, blk * C:(blk + 1) * C],
                            0.0, OP.mult, OP.add)
                    return ins
                b.op("dve", scan, reads=["c_f2", "ones_t"],
                     writes=["c_gam"], tag=tg + "_scan")
                b.op("act", lambda e: e.activation(expg[:], gam[:], ACT.Exp,
                                                   bias=z128),
                     reads=["c_gam", "cvals"], writes=["c_eg"],
                     tag=tg + "_eg")
                b.op("act", lambda e: e.activation(expng[:], gam[:], ACT.Exp,
                                                   bias=z128, scale=n1_128),
                     reads=["c_gam", "cvals"], writes=["c_eng"],
                     tag=tg + "_eng")
                b.op("act", lambda e: e.activation(s2v[:], f2[:], ACT.Exp,
                                                   bias=z128),
                     reads=["c_f2", "cvals"], writes=["c_s2v"],
                     tag=tg + "_ef")
                b.op("dve", lambda e: e.scalar_tensor_tensor(
                         s2v[:], s2v[:], -1.0, ones_t[:, 0:256],
                         OP.mult, OP.add),
                     reads=["c_s2v", "ones_t"], writes=["c_s2v"],
                     tag=tg + "_s")
                b.op("dve", lambda e: e.tensor_mul(ufm[:], s2v[:], expng[:]),
                     reads=["c_s2v", "c_eng"], writes=["c_ufm"],
                     tag=tg + "_u")
                def tr1(eng):
                    ins = None
                    for blk in range(2):
                        ins = eng.matmul(ps[2][:, blk * 128:(blk + 1) * 128],
                                         k2[:, blk * 128:(blk + 1) * 128],
                                         ident[:], is_transpose=True,
                                         start=(blk == 0), stop=False)
                    for blk in range(2):
                        ins = eng.matmul(ps[2][:, 256 + blk * 128:
                                               256 + (blk + 1) * 128],
                                         ufm[:, blk * 128:(blk + 1) * 128],
                                         ident[:], is_transpose=True,
                                         start=False, stop=(blk == 1))
                    return ins
                b.op("pe", tr1, reads=["c_k2", "c_ufm", "ident"],
                     writes=["ps2"], tag=tg + "_tr1")
                b.op("dve", lambda e: e.tensor_copy(ktm[:], ps[2][:, 0:256]),
                     reads=["ps2"], writes=["c_ktm"], tag=tg + "_ktmc")
                b.op("dve", lambda e: e.tensor_copy(utm[:], ps[2][:, 256:512]),
                     reads=["ps2"], writes=["c_utm"], tag=tg + "_utmc")
                def tr2(eng):
                    ins = None
                    for blk in range(2):
                        ins = eng.matmul(ps[3][:, blk * 128:(blk + 1) * 128],
                                         expg[:, blk * 128:(blk + 1) * 128],
                                         ident[:], is_transpose=True,
                                         start=(blk == 0), stop=False)
                    for blk in range(2):
                        ins = eng.matmul(ps[3][:, 256 + blk * 128:
                                               256 + (blk + 1) * 128],
                                         v2[:, blk * 128:(blk + 1) * 128],
                                         ident[:], is_transpose=True,
                                         start=False, stop=(blk == 1))
                    return ins
                b.op("pe", tr2, reads=["c_eg", "c_v2", "ident"],
                     writes=["ps3"], tag=tg + "_tr2")
                b.op("dve", lambda e: e.tensor_copy(egtm[:], ps[3][:, 0:256]),
                     reads=["ps3"], writes=["c_egtm"], tag=tg + "_egtmc")
                b.op("dve", lambda e: e.tensor_copy(vtm[:], ps[3][:, 256:512]),
                     reads=["ps3"], writes=["c_vtm"], tag=tg + "_vtmc")
                def mmA(eng):
                    ins = None
                    for blk in range(2):
                        ins = eng.matmul(ps[0][:, 0:128],
                                         k2[:, blk * 128:(blk + 1) * 128],
                                         q2[:, blk * 128:(blk + 1) * 128],
                                         start=(blk == 0), stop=(blk == 1))
                    return ins
                b.op("pe", mmA, reads=["c_k2", "c_q2"], writes=["ps0a"],
                     tag=tg + "_A")
                b.op("dve", lambda e: e.tensor_mul(asb[:], ps[0][:, 0:128],
                                                   tri[:]),
                     reads=["ps0a", "tri"], writes=["c_as"],
                     tag=tg + "_Am")
                def mmok(eng):
                    return eng.matmul(ps[1][:, 0:256], asb[:], utm[:],
                                      start=True, stop=True)
                b.op("pe", mmok, reads=["c_as", "c_utm"],
                     writes=["ps1a"], tag=tg + "_ok")
                def mmoki(eng, S1=S1):
                    ins = None
                    for blk in range(2):
                        ins = eng.matmul(ps[6][:, 0:256],
                                         q2[:, blk * 128:(blk + 1) * 128],
                                         S1[:, blk * 256:(blk + 1) * 256],
                                         start=(blk == 0), stop=(blk == 1))
                    return ins
                b.op("pe", mmoki, reads=["c_q2", "c_s1_%d" % h],
                     writes=["ps6a"], tag=tg + "_oki")
                b.op("dve", lambda e: e.tensor_mul(qvtm[:], ps[1][:, 0:256],
                                                   egtm[:]),
                     reads=["ps1a", "c_egtm"], writes=["c_qvtm"],
                     tag=tg + "_oksc")
                b.op("dve", lambda e: e.tensor_mul(p.sb["c_sc"][:, 0:256],
                                                   ps[6][:, 0:256], egtm[:]),
                     reads=["ps6a", "c_egtm"], writes=["c_sc"],
                     tag=tg + "_oksc2")
                b.op("dve", lambda e: e.tensor_add(qvtm[:], qvtm[:],
                                                   p.sb["c_sc"][:, 0:256]),
                     reads=["c_qvtm", "c_sc"], writes=["c_qvtm"],
                     tag=tg + "_okadd")
                b.op("dve", lambda e: e.tensor_reduce(
                         p.sb["c_mx"][:, 0:1], qvtm[:], AX, OP.max,
                         negate=True),
                     reads=["c_qvtm"], writes=["c_mx"], tag=tg + "_mx")
                b.op("act", lambda e: e.activation(
                         qvtm[:], qvtm[:], ACT.Exp,
                         bias=p.sb["c_mx"][:, 0:1],
                         accum_out=p.sb["c_mx"][:, 1:2]),
                     reads=["c_qvtm", "c_mx"], writes=["c_qvtm", "c_mx"],
                     tag=tg + "_sme")
                b.op("dve", lambda e: e.reciprocal(p.sb["c_mx"][:, 2:3],
                                                   p.sb["c_mx"][:, 1:2]),
                     reads=["c_mx"], writes=["c_mx"], tag=tg + "_smr")
                b.op("dve", lambda e: e.tensor_scalar(
                         qvtm[:], qvtm[:], p.sb["c_mx"][:, 2:3], None,
                         OP.mult),
                     reads=["c_qvtm", "c_mx"], writes=["c_qvtm"],
                     tag=tg + "_smn")
                def tr3(eng):
                    ins = None
                    for blk in range(2):
                        ins = eng.matmul(ps[7][:, blk * 128:(blk + 1) * 128],
                                         qvtm[:, blk * 128:(blk + 1) * 128],
                                         ident[:], is_transpose=True,
                                         start=(blk == 0), stop=(blk == 1))
                    return ins
                b.op("pe", tr3, reads=["c_qvtm", "ident"], writes=["ps7a"],
                     tag=tg + "_tr3")
                b.op("dve", lambda e: e.tensor_copy(qvfm[:], ps[7][:, 0:256]),
                     reads=["ps7a"], writes=["c_qvfm"], tag=tg + "_qvfmc")
                b.op("dve", lambda e: e.tensor_mul(qtfm[:], qvfm[:], expg[:]),
                     reads=["c_qvfm", "c_eg"], writes=["c_qtfm"],
                     tag=tg + "_qt")
                def mmA2(eng):
                    ins = None
                    for blk in range(2):
                        ins = eng.matmul(ps[0][:, 128:256],
                                         ufm[:, blk * 128:(blk + 1) * 128],
                                         qtfm[:, blk * 128:(blk + 1) * 128],
                                         start=(blk == 0), stop=(blk == 1))
                    return ins
                b.op("pe", mmA2, reads=["c_ufm", "c_qtfm"], writes=["ps0b"],
                     tag=tg + "_A2")
                b.op("dve", lambda e: e.tensor_mul(a2sb[:], ps[0][:, 128:256],
                                                   tri[:]),
                     reads=["ps0b", "tri"], writes=["c_a2s"],
                     tag=tg + "_A2m")
                def mmov(eng):
                    return eng.matmul(ps[1][:, 256:512], a2sb[:], vtm[:],
                                      start=True, stop=True)
                b.op("pe", mmov, reads=["c_a2s", "c_vtm"], writes=["ps1b"],
                     tag=tg + "_ov")
                def mmovi(eng, S2=S2):
                    ins = None
                    for blk in range(2):
                        ins = eng.matmul(ps[6][:, 256:512],
                                         qtfm[:, blk * 128:(blk + 1) * 128],
                                         S2[:, blk * 256:(blk + 1) * 256],
                                         start=(blk == 0), stop=(blk == 1))
                    return ins
                b.op("pe", mmovi, reads=["c_qtfm", "c_s2_%d" % h],
                     writes=["ps6b"], tag=tg + "_ovi")
                b.op("dve", lambda e: e.tensor_copy(ovtm[:], ps[1][:, 256:512]),
                     reads=["ps1b"], writes=["c_ovtm"], tag=tg + "_ovc")
                b.op("dve", lambda e: e.tensor_add(ovtm[:], ovtm[:],
                                                   ps[6][:, 256:512]),
                     reads=["ps6b", "c_ovtm"], writes=["c_ovtm"],
                     tag=tg + "_ovc2")
                def mmku(eng):
                    ins = None
                    for blk in range(2):
                        ins = eng.matmul(ps[5][:, blk * 256:(blk + 1) * 256],
                                         ktm[:, blk * 128:(blk + 1) * 128],
                                         utm[:], start=True, stop=True)
                    return ins
                b.op("pe", mmku, reads=["c_ktm", "c_utm"], writes=["ps5"],
                     tag=tg + "_ku")
                b.dma("sp", p.sb["c_fr"][0:1, 0:256], egtm[127:128, 0:256],
                      reads=["c_egtm"], writes=["c_fr"], tag=tg + "_frd")
                b.op("pe", lambda e: e.matmul(ps[0][:, 256:512], ones_m[:],
                                              p.sb["c_fr"][0:1, 0:256],
                                              start=True, stop=True),
                     reads=["ones_m", "c_fr"], writes=["ps0f"],
                     tag=tg + "_fb")
                for blk in range(2):
                    S1b = S1[:, blk * 256:(blk + 1) * 256]
                    b.op("dve", lambda e, S1b=S1b, blk=blk: e.tensor_add(
                             S1b, S1b, ps[5][:, blk * 256:(blk + 1) * 256]),
                         reads=["c_s1_%d" % h, "ps5"],
                         writes=["c_s1_%d" % h],
                         tag=tg + "_s1a%d" % blk)
                    b.op("dve", lambda e, S1b=S1b: e.tensor_mul(
                             S1b, S1b, ps[0][:, 256:512]),
                         reads=["c_s1_%d" % h, "ps0f"],
                         writes=["c_s1_%d" % h],
                         tag=tg + "_s1m%d" % blk)
                def mmku2(eng):
                    ins = None
                    for blk in range(2):
                        ins = eng.matmul(ps[4][:, blk * 256:(blk + 1) * 256],
                                         utm[:, blk * 128:(blk + 1) * 128],
                                         vtm[:], start=True, stop=True)
                    return ins
                b.op("pe", mmku2, reads=["c_utm", "c_vtm"], writes=["ps4"],
                     tag=tg + "_ku2")
                for blk in range(2):
                    S2b = S2[:, blk * 256:(blk + 1) * 256]
                    egcol = expg[:, blk * 128 + 127: blk * 128 + 128]
                    b.op("dve", lambda e, S2b=S2b, blk=blk: e.tensor_add(
                             S2b, S2b, ps[4][:, blk * 256:(blk + 1) * 256]),
                         reads=["c_s2_%d" % h, "ps4"],
                         writes=["c_s2_%d" % h],
                         tag=tg + "_s2a%d" % blk)
                    b.op("dve", lambda e, S2b=S2b, egcol=egcol: e.tensor_scalar(
                             S2b, S2b, egcol, None, OP.mult),
                         reads=["c_s2_%d" % h, "c_eg"],
                         writes=["c_s2_%d" % h],
                         tag=tg + "_s2m%d" % blk)
                def tr4(eng):
                    ins = None
                    for blk in range(2):
                        ins = eng.matmul(ps[7][:, 256 + blk * 128:
                                               256 + (blk + 1) * 128],
                                         ovtm[:, blk * 128:(blk + 1) * 128],
                                         ident[:], is_transpose=True,
                                         start=(blk == 0), stop=(blk == 1))
                    return ins
                b.op("pe", tr4, reads=["c_ovtm", "ident"], writes=["ps7b"],
                     tag=tg + "_tr4")
                b.op("dve", lambda e, h=h: e.tensor_copy(
                         ofm[:, h * 256:(h + 1) * 256], ps[7][:, 256:512]),
                     reads=["ps7b"], writes=["c_ofm#%d" % h],
                     tag=tg + "_ofmc")
            tgc = "c2r_%d_%d" % (s_, c_)
            b.op("act", lambda e: e.activation(osil[:], ofm[:], ACT.Silu,
                                               bias=z128),
                 reads=["c_ofm#%d" % hh for hh in range(4)] + ["cvals"],
                 writes=["c_osil"], tag=tgc + "_sil")
            b.op("act", lambda e: e.square(osq[:], osil[:]),
                 reads=["c_osil"], writes=["c_osq"], tag=tgc + "_sq")
            def mmrs(eng):
                ins = None
                for c2 in range(8):
                    ins = eng.matmul(ps[6][0:1, 0:128], p.sb["ones_k"][:],
                                     osq[:, c2 * 128:(c2 + 1) * 128],
                                     start=(c2 == 0), stop=(c2 == 7))
                return ins
            b.op("pe", mmrs, reads=["c_osq", "ones_k"], writes=["ps6a"],
                 tag=tgc + "_ss")
            rr = stat[0:1, 0:128]
            b.op("act", lambda e: e.mul(rr, ps[6][0:1, 0:128], 1.0 / D),
                 reads=["ps6a"], writes=["stat"], tag=tgc + "_m")
            b.op("act", lambda e: e.activation(rr, rr, ACT.Sqrt,
                                               bias=p.cval(EPS)),
                 reads=["stat", "cvals"], writes=["stat"], tag=tgc + "_sd")
            b.op("dve", lambda e: e.reciprocal(rr, rr),
                 reads=["stat"], writes=["stat"], tag=tgc + "_r")
            b.op("pe", lambda e: e.matmul(ps[7][:, 0:128], ones_m[:], rr,
                                          start=True, stop=True),
                 reads=["stat", "ones_m"], writes=["ps7a"],
                 tag=tgc + "_bc")
            for c2 in range(8):
                sl = osil[:, c2 * 128:(c2 + 1) * 128]
                b.op("dve", lambda e, sl=sl: e.tensor_mul(sl, sl,
                                                          ps[7][:, 0:128]),
                     reads=["c_osil", "ps7a"], writes=["c_osil"],
                     tag=tgc + "_rm%d" % c2)
                b.op("dve", lambda e, sl=sl, c2=c2: e.tensor_scalar(
                         sl, sl, gw[:, c2:c2 + 1], None, OP.mult),
                     reads=["c_osil", "c_gw"], writes=["c_osil"],
                     tag=tgc + "_gw%d" % c2)
            store_fm(p, "sp", OS, "o_scr", osil, "c_osil", tt0, C,
                     tag=tgc + "_st", chunked=False)

    b.barrier("c3_bar")
    TW3 = min(256, NTOK)
    banks3 = std_banks(p, TW3)
    load_w(p, "sp", wbig[:, 0:NCH * D], "wbig", wo_h, D, D, tag="c_ldwo")
    for it in range(NTOK // TW3):
        t0_ = it * TW3
        par = it % 2
        x_t = t[0][:, par * NCH * TW3:(par + 1) * NCH * TW3]
        xk = "c3_x%d" % par
        o_t = t[1][:, par * NCH * TW3:(par + 1) * NCH * TW3]
        ok_ = "c3_o%d" % par
        y_t = t[2][:, par * NCH * TW3:(par + 1) * NCH * TW3]
        yk = "c3_y%d" % par
        xo_t, xok = o_t, ok_
        load_fm(p, "sp", x_t, xk, src_hbm, t0_, TW3, tag="c3_ldx%d" % it)
        load_fm(p, "sp", o_t, ok_, OS, t0_, TW3, tag="c3_ldo%d" % it)
        gb = [ps[2 * par], ps[2 * par + 1]]
        gbk = ["ps%d" % (2 * par), "ps%d" % (2 * par + 1)]
        for o in range(NCH):
            bank = gb[o % 2]
            bkey = gbk[o % 2]

            def mm(eng, o=o, bank=bank, o_t=o_t, TW3=TW3):
                ins = None
                for c in range(NCH):
                    col = c * D + o * 128
                    ins = eng.matmul(bank[:, 0:TW3], wbig[:, col:col + 128],
                                     o_t[:, c * TW3:(c + 1) * TW3],
                                     start=(c == 0), stop=(c == NCH - 1))
                return ins

            b.op("pe", mm, reads=["wbig"] + ck(ok_), writes=[bkey],
                 tag="c3_mm%d_%d" % (it, o))
            b.op("act", lambda e, o=o, bank=bank, y_t=y_t, TW3=TW3: e.copy(
                     y_t[:, o * TW3:(o + 1) * TW3], bank[:, 0:TW3]),
                 reads=[bkey], writes=[yk + "#%d" % o],
                 tag="c3_cp%d_%d" % (it, o))
        emit_ln(p, y_t, yk, TW3, banks3, stat, "stat", tag="c3_ln%d" % it)
        ln_apply_residual(p, y_t, yk, x_t, xk, xo_t, xok, TW3, banks3,
                          tag="c3_res%d" % it)
        store_fm(p, "sp", dst_hbm, dst_key, xo_t, xok, t0_, TW3,
                 tag="c3_st%d" % it)

def emit_phase_d(p, src, dst):
    src_hbm, src_key = src
    dst_hbm, dst_key = dst
    b = p.b
    ps = [p.sb[f"ps{i}"] for i in range(8)]
    t = [p.sb[f"t{i}"] for i in range(3)]
    stat = p.sb["stat"]
    wbig = p.sb["wbig"]
    ones_m = p.sb["ones_m"]
    ident = p.sb["ident"]
    hb = p.sb["hb"]
    lb = p.sb["lb"]
    mb = p.sb["mb"]
    u0 = p.sb["u0"]
    wq16 = p.sb["wq16"]
    HS = p.dram_tmp("hs", [2 * 4096, NTOK], BF16)
    WQF = p.dram_tmp("wqf", [D, 4096], BF16)
    WQP = p.dram_tmp("wqp", [4096, D], BF16)
    wfc_h = p.dram_in("wfc_t", [D, 4096], F32)
    wpr_h = p.dram_in("wpr_t", [4096, D], F32)
    bfc_v = p.sbuf("d_bfcv", [128, 32], F32)
    load_vec(p, "sp", bfc_v[:], "d_bfcv", p.dram_in("b_fc", [128, 32], F32))
    bpr_v = p.sbuf("d_bprv", [128, 8], F32)
    load_vec(p, "sp", bpr_v[:], "d_bprv", p.dram_in("b_pr", [128, 8], F32))
    sqt = p.sb["sq_t"]
    afc = t[1][:, 1152:1152 + NCH * 16]
    bfc_h = p.dram_in("bfc_t", [16, 4096], F32)
    bfc_sl = sqt[0:16, 2560:3072]
    apr = t[1][:, 640:640 + 32 * 16]
    bpr_h = p.dram_in("bpr_t", [16, D], F32)
    bpr_sl = p.sbuf("d_bprs", [16, 256], F32)[:]
    z_sb = p.sbuf("d_z2", [16, 512], F32)[:]
    dsc = p.sbuf("d_sc", [1, 8], F32)
    dscb = p.sbuf("d_scb", [128, 4], F32)
    mx1 = p.sbuf("d_mx", [128, 2], F32)
    rowt = sqt[0:1, 3328:3456]
    MAGIC = float(3 * 2 ** 22)
    cmag = p.cval(MAGIC, 128)
    c127 = p.cval(127.0, 128)
    cn128 = p.cval(-128.0, 128)

    b.barrier("d0_bar")
    for wi, (wh, wq_scr, wq_key, rows, cols, sc_i) in enumerate([
            (wfc_h, WQF, "wqf", D, 4096, 0),
            (wpr_h, WQP, "wqp", 4096, D, 2)]):
        nld = (rows // 128) * cols // 16384
        b.op("dve", lambda e: e.memset(mx1[:, 0:1], 0.0), writes=["d_mx"],
             tag="d0_z%d" % wi)
        ncol_per = 16384 // (rows // 128)
        for j in range(nld):
            load_w(p, "sp", wbig[:], "wbig", wh, rows, cols,
                   col0=j * ncol_per, ncols=ncol_per, tag="d0_ld%d_%d" % (wi, j))
            b.op("dve", lambda e: e.tensor_reduce(mx1[:, 1:2], wbig[:], AX,
                                                  OP.max,
                                                  apply_absolute_value=True),
                 reads=["wbig"], writes=["d_mx"], tag="d0_mx%d_%d" % (wi, j))
            b.op("dve", lambda e: e.tensor_max(mx1[:, 0:1], mx1[:, 0:1],
                                               mx1[:, 1:2]),
                 reads=["d_mx"], writes=["d_mx"], tag="d0_acc%d_%d" % (wi, j))
        b.op("pe", lambda e: e.matmul(ps[3][0:1, 0:128], mx1[:, 0:1], ident[:],
                                      is_transpose=True, start=True, stop=True),
             reads=["d_mx", "ident"], writes=["ps3"], tag="d0_tr%d" % wi)
        b.op("dve", lambda e: e.tensor_copy(rowt, ps[3][0:1, 0:128]),
             reads=["ps3"], writes=["d_row"], tag="d0_rc%d" % wi)
        b.op("dve", lambda e, sc_i=sc_i: e.tensor_reduce(
                 dsc[0:1, sc_i:sc_i + 1], rowt, AX, OP.max,
                 apply_absolute_value=True),
             reads=["d_row"], writes=["d_sc"], tag="d0_rm%d" % wi)
        b.op("act", lambda e, sc_i=sc_i: e.mul(dsc[0:1, sc_i:sc_i + 1],
                                               dsc[0:1, sc_i:sc_i + 1],
                                               1.0 / 127.0),
             reads=["d_sc"], writes=["d_sc"], tag="d0_s%d" % wi)
        b.op("dve", lambda e, sc_i=sc_i: e.reciprocal(
                 dsc[0:1, sc_i + 1:sc_i + 2], dsc[0:1, sc_i:sc_i + 1]),
             reads=["d_sc"], writes=["d_sc"], tag="d0_r%d" % wi)
    b.op("pe", lambda e: e.matmul(ps[3][:, 0:4], ones_m[:], dsc[0:1, 0:4],
                                  start=True, stop=True),
         reads=["d_sc", "ones_m"], writes=["ps3"], tag="d0_bc")
    b.op("dve", lambda e: e.tensor_copy(dscb[:], ps[3][:, 0:4]),
         reads=["ps3"], writes=["d_scb"], tag="d0_bcc")
    for wi, (wh, wq_scr, wq_key, rows, cols, sc_i) in enumerate([
            (wfc_h, WQF, "wqf", D, 4096, 0),
            (wpr_h, WQP, "wqp", 4096, D, 2)]):
        nld = (rows // 128) * cols // 16384
        ncol_per = 16384 // (rows // 128)
        for j in range(nld):
            load_w(p, "sp", wbig[:], "wbig", wh, rows, cols,
                   col0=j * ncol_per, ncols=ncol_per, tag="d0q_ld%d_%d" % (wi, j))
            b.op("dve", lambda e, sc_i=sc_i: e.tensor_scalar(
                     wbig[:], wbig[:], dscb[:, sc_i + 1:sc_i + 2], None,
                     OP.mult),
                 reads=["wbig", "d_scb"], writes=["wbig"],
                 tag="d0q_m%d_%d" % (wi, j))
            b.op("dve", lambda e: e.tensor_scalar(wbig[:], wbig[:], cmag,
                                                  None, OP.add),
                 reads=["wbig", "cvals"], writes=["wbig"],
                 tag="d0q_a%d_%d" % (wi, j))
            b.op("dve", lambda e: e.tensor_scalar(wbig[:], wbig[:], cmag,
                                                  None, OP.subtract),
                 reads=["wbig", "cvals"], writes=["wbig"],
                 tag="d0q_s%d_%d" % (wi, j))
            b.op("dve", lambda e: e.tensor_scalar(wbig[:], wbig[:], c127,
                                                  None, OP.min),
                 reads=["wbig", "cvals"], writes=["wbig"],
                 tag="d0q_c1%d_%d" % (wi, j))
            b.op("dve", lambda e: e.tensor_scalar(wbig[:], wbig[:], cn128,
                                                  None, OP.max),
                 reads=["wbig", "cvals"], writes=["wbig"],
                 tag="d0q_c2%d_%d" % (wi, j))
            nchj = rows // 256
            for hf in range(2):
                sl = wbig[:, hf * 8192:(hf + 1) * 8192]
                w16 = p.sb["d0w16"]
                b.op("dve", lambda e, sl=sl, w16=w16: e.tensor_copy(w16[:], sl),
                     reads=["wbig"], writes=["d0_w16"],
                     tag="d0q_cv%d_%d_%d" % (wi, j, hf))
                dstw = wq_scr.rearrange("(c p) o -> p c o", p=128)[
                    :, hf * nchj:(hf + 1) * nchj,
                    j * ncol_per:(j + 1) * ncol_per]
                srcw = w16[:].rearrange("p (c o) -> p c o", c=nchj)
                b.dma("sp", dstw, srcw, reads=["d0_w16"],
                      writes=["%s#%d_%d" % (wq_key, j, hf)],
                      tag="d0q_st%d_%d_%d" % (wi, j, hf))
    b.barrier("d0e_bar")
    load_w(p, "sp", afc, "d_afc", p.dram_in("afc_t", [D, 16], F32), D, 16)
    load_w(p, "sp", apr, "d_apr", p.dram_in("apr_t", [4096, 16], F32),
           4096, 16)
    afcs = t[1][:, 0:NCH * 16]
    afch = mb[:, 0:NCH * 16]
    afcl = mb[:, NCH * 16:2 * NCH * 16]
    b.op("dve", lambda e: e.tensor_scalar(afcs, afc, dscb[:, 1:2],
                                          None, OP.mult),
         reads=["d_afc", "d_scb"], writes=["d0_afcs"], tag="d0_afs")
    b.op("dve", lambda e: e.tensor_copy(afch, afcs),
         reads=["d0_afcs"], writes=["d_afch"], tag="d0_afh")
    b.op("dve", lambda e: e.tensor_tensor(afcl, afcs, afch,
                                          OP.subtract),
         reads=["d0_afcs", "d_afch"], writes=["d_afcl"], tag="d0_afl")
    aprs = t[1][:, 128:128 + 32 * 16]
    aprh = p.sbuf("d_aprh", [128, 32 * 16], BF16)
    aprl = p.sbuf("d_aprl", [128, 32 * 16], BF16)
    
    b.op("dve", lambda e: e.tensor_scalar(aprs, apr, dscb[:, 3:4],
                                          None, OP.mult),
         reads=["d_apr", "d_scb"], writes=["d0_aprs"], tag="d0_aps")
    b.op("dve", lambda e: e.tensor_copy(aprh[:], aprs),
         reads=["d0_aprs"], writes=["d_aprh"], tag="d0_aph")
    b.op("dve", lambda e: e.tensor_tensor(aprl[:], aprs, aprh[:],
                                          OP.subtract),
         reads=["d0_aprs", "d_aprh"], writes=["d_aprl"], tag="d0_apl")

    b.barrier("d1_bar")
    TW = 512
    nt = NTOK // TW
    banks = std_banks(p, TW)
    x_t, xk = t[0][:, :NCH * TW], "d1_x"
    hikeys = ["d1_hi#%d" % c for c in range(NCH)]
    lokeys = ["d1_lo#%d" % c for c in range(NCH)]
    HSr = HS.rearrange("(c p) t -> p c t", p=128)
    for it in range(nt):
        t0_ = it * TW
        load_fm(p, "sp", x_t, xk, src_hbm, t0_, TW, tag="d1_ldx%d" % it)
        emit_ln(p, x_t, xk, TW, banks, stat, "stat", tag="d1_ln%d" % it,
                scale_ap=dsc[0:1, 0:1], scale_key="d_sc",
                sq=t[1][:], sqk="d1_t32")
        for c in range(NCH):
            xs_ = x_t[:, c * TW:(c + 1) * TW]
            tmp = t[1][:, c * TW:(c + 1) * TW]
            hic = hb[:, c * TW:(c + 1) * TW]
            loc = lb[:, c * TW:(c + 1) * TW]
            tk = "d1_t32#%d" % c
            b.op("dve", lambda e, tmp=tmp, xs_=xs_: e.tensor_mul(
                     tmp, xs_, banks["br"]),
                 reads=[xk + "#%d" % c, banks["kr"]], writes=[tk],
                 tag="d1_sp%d_m%d" % (it, c))
            b.op("dve", lambda e, tmp=tmp: e.tensor_sub(tmp, tmp,
                                                        banks["bm"]),
                 reads=[tk, banks["km"]], writes=[tk],
                 tag="d1_sp%d_s%d" % (it, c))
            b.op("dve", lambda e, hic=hic, tmp=tmp: e.tensor_copy(hic, tmp),
                 reads=[tk], writes=["d1_hi#%d" % c],
                 tag="d1_sp%d_h%d" % (it, c))
            b.op("dve", lambda e, loc=loc, tmp=tmp, hic=hic: e.tensor_tensor(
                     loc, tmp, hic, OP.subtract),
                 reads=[tk, "d1_hi#%d" % c], writes=["d1_lo#%d" % c],
                 tag="d1_sp%d_l%d" % (it, c))
        def mmz(eng, TW=TW):
            ins = None
            for c in range(NCH):
                ins = eng.matmul(ps[3][0:16, 0:TW], afch[:, c * 16:(c + 1) * 16],
                                 hb[:, c * TW:(c + 1) * TW],
                                 start=(c == 0), stop=False)
            for c in range(NCH):
                ins = eng.matmul(ps[3][0:16, 0:TW], afch[:, c * 16:(c + 1) * 16],
                                 lb[:, c * TW:(c + 1) * TW],
                                 start=False, stop=False)
            for c in range(NCH):
                ins = eng.matmul(ps[3][0:16, 0:TW],
                                 afcl[:, c * 16:(c + 1) * 16],
                                 hb[:, c * TW:(c + 1) * TW],
                                 start=False, stop=(c == NCH - 1))
            return ins
        b.op("pe", mmz, reads=["d_afch", "d_afcl"] + hikeys + lokeys,
             writes=["ps3"], tag="d1_z%d" % it)
        b.op("act", lambda e: e.mul(z_sb[:, 0:TW], ps[3][0:16, 0:TW], 2.0),
             reads=["ps3"], writes=["d_z"], tag="d1_zc%d" % it)
        for o in range(32):
            slot = o % 8
            wreg = wq16[:, slot * 1024:slot * 1024 + 1024]
            wkey = "wq16#%d" % slot
            load_w(p, "sp", wreg, wkey, WQF, D, 4096, col0=o * 128, ncols=128,
                   tag="d1_lw%d_%d" % (it, o))
            s4 = o % 4
            b.dma("sp", bfc_sl[:, s4 * 128:(s4 + 1) * 128],
                  bfc_h[:, o * 128:(o + 1) * 128], reads=["d_bfc#%d" % s4],
                  writes=["d_bfc#%d" % s4], tag="d1_lb%d_%d" % (it, o))
            bank = ps[o % 3]
            bkey = "ps%d" % (o % 3)

            def mm(eng, o=o, bank=bank, wreg=wreg, TW=TW):
                ins = None
                for c in range(NCH):
                    ins = eng.matmul(bank[:, 0:TW],
                                     wreg[:, c * 128:(c + 1) * 128],
                                     hb[:, c * TW:(c + 1) * TW],
                                     start=(c == 0), stop=False)
                    ins = eng.matmul(bank[:, 0:TW],
                                     wreg[:, c * 128:(c + 1) * 128],
                                     lb[:, c * TW:(c + 1) * TW],
                                     start=False, stop=False)
                s4 = o % 4
                ins = eng.matmul(bank[:, 0:TW],
                                 bfc_sl[:, s4 * 128:(s4 + 1) * 128],
                                 z_sb[0:16, 0:TW], start=False, stop=True)
                return ins

            b.op("pe", mm,
                 reads=[wkey, "d_bfc#%d" % (o % 4), "d_z"] + hikeys + lokeys,
                 writes=[bkey], tag="d1_mm%d_%d" % (it, o))
            par = o % 2
            h_t = t[2][:, par * TW:par * TW + TW]
            hk = "d1_h%d" % par
            b.op("act", lambda e, bank=bank, h_t=h_t, o=o: e.activation(
                     h_t, bank[:, 0:TW], ACT.Gelu,
                     bias=bfc_v[:, o:o + 1]),
                 reads=[bkey, "d_bfcv"], writes=[hk], tag="d1_g%d_%d" % (it, o))
            hs32 = t[2][:, 1024 + par * TW:1024 + par * TW + TW]
            hsk = "d1_hs%d" % par
            b.op("dve", lambda e, hs32=hs32, h_t=h_t: e.tensor_scalar(
                     hs32, h_t, dscb[:, 2:3], None, OP.mult),
                 reads=[hk, "d_scb"], writes=[hsk],
                 tag="d1_hm%d_%d" % (it, o))
            hi8 = u0[:, par * 1024:par * 1024 + 512]
            lo8 = u0[:, par * 1024 + 512:par * 1024 + 1024]
            h8k = "d1_h8%d" % par
            l8k = "d1_l8%d" % par
            b.op("dve", lambda e, hi8=hi8, hs32=hs32: e.tensor_copy(hi8, hs32),
                 reads=[hsk], writes=[h8k], tag="d1_hh%d_%d" % (it, o))
            b.op("dve", lambda e, lo8=lo8, hs32=hs32, hi8=hi8: e.tensor_tensor(
                     lo8, hs32, hi8, OP.subtract),
                 reads=[hsk, h8k], writes=[l8k], tag="d1_hl%d_%d" % (it, o))
            b.dma("sp", HSr[:, o:o + 1, t0_:t0_ + TW],
                  hi8.rearrange("p (c t) -> p c t", c=1),
                  reads=[h8k], writes=["hs_%d_%d_h" % (it, o)],
                  tag="d1_sth%d_%d" % (it, o))
            b.dma("sp", HSr[:, 32 + o:33 + o, t0_:t0_ + TW],
                  lo8.rearrange("p (c t) -> p c t", c=1),
                  reads=[l8k], writes=["hs_%d_%d_l" % (it, o)],
                  tag="d1_stl%d_%d" % (it, o))

    b.barrier("d2_bar")
    TW2 = 512
    nt2 = NTOK // TW2
    banks2 = std_banks(p, TW2)
    d2h = p.sb["d2h"]
    d2l = p.sb["d2l"]
    wq16d = p.sb["wq16d"]
    x2_t, x2k = t[2][:, 0:NCH * 512], "d2_x"
    y2_t, y2k = t[1][:, 0:NCH * 512], "d2_y"
    xo2_t, xo2k = y2_t, y2k
    hikeys2 = ["d2_hi#%d" % c for c in range(32)]
    lokeys2 = ["d2_lo#%d" % c for c in range(32)]
    for it in range(nt2):
        t0_ = it * TW2
        b.dma("sp", d2h[:, 0:32 * TW2].rearrange("p (c t) -> p c t", c=32),
              HSr[:, 0:32, t0_:t0_ + TW2],
              reads=[], writes=hikeys2, tag="d2_lh_%d" % it)
        b.dma("sp", d2l[:, 0:32 * TW2].rearrange("p (c t) -> p c t", c=32),
              HSr[:, 32:64, t0_:t0_ + TW2],
              reads=[], writes=lokeys2, tag="d2_ll_%d" % it)
        load_fm(p, "sp", x2_t, x2k, src_hbm, t0_, TW2, tag="d2_ldx%d" % it)
        def mmz2(eng, TW2=TW2):
            ins = None
            for c in range(32):
                ins = eng.matmul(ps[3][0:16, 0:TW2],
                                 aprh[:, c * 16:(c + 1) * 16],
                                 d2h[:, c * TW2:(c + 1) * TW2],
                                 start=(c == 0), stop=False)
            for c in range(32):
                ins = eng.matmul(ps[3][0:16, 0:TW2],
                                 aprh[:, c * 16:(c + 1) * 16],
                                 d2l[:, c * TW2:(c + 1) * TW2],
                                 start=False, stop=False)
            for c in range(32):
                ins = eng.matmul(ps[3][0:16, 0:TW2],
                                 aprl[:, c * 16:(c + 1) * 16],
                                 d2h[:, c * TW2:(c + 1) * TW2],
                                 start=False, stop=(c == 31))
            return ins
        b.op("pe", mmz2, reads=["d_aprh", "d_aprl"] + hikeys2 + lokeys2,
             writes=["ps3"], tag="d2_z%d" % it)
        b.op("act", lambda e: e.mul(z_sb[:, 0:TW2], ps[3][0:16, 0:TW2], 2.0),
             reads=["ps3"], writes=["d_z"], tag="d2_zc%d" % it)
        for o in range(NCH):
            slot = o % 2
            wreg = wq16d[:, slot * 4096:(slot + 1) * 4096]
            wkey2 = "wq16d#%d" % slot
            load_w(p, "sp", wreg, wkey2, WQP, 4096, D, col0=o * 128,
                   ncols=128, tag="d2_lw%d_%d" % (it, o))
            s2_ = o % 2
            b.dma("sp", bpr_sl[:, s2_ * 128:(s2_ + 1) * 128],
                  bpr_h[:, o * 128:(o + 1) * 128], reads=["d_bpr#%d" % s2_],
                  writes=["d_bpr#%d" % s2_], tag="d2_lb%d_%d" % (it, o))
            bank = ps[o % 3]
            bkey = "ps%d" % (o % 3)

            def mm(eng, o=o, bank=bank, wreg=wreg, TW2=TW2):
                ins = None
                for c in range(32):
                    ins = eng.matmul(bank[:, 0:TW2],
                                     wreg[:, c * 128:(c + 1) * 128],
                                     d2h[:, c * TW2:(c + 1) * TW2],
                                     start=(c == 0), stop=False)
                    ins = eng.matmul(bank[:, 0:TW2],
                                     wreg[:, c * 128:(c + 1) * 128],
                                     d2l[:, c * TW2:(c + 1) * TW2],
                                     start=False, stop=False)
                s2_ = o % 2
                ins = eng.matmul(bank[:, 0:TW2],
                                 bpr_sl[:, s2_ * 128:(s2_ + 1) * 128],
                                 z_sb[0:16, 0:TW2], start=False, stop=True)
                return ins

            b.op("pe", mm,
                 reads=[wkey2, "d_bpr#%d" % (o % 2), "d_z"]
                 + hikeys2 + lokeys2,
                 writes=[bkey], tag="d2_mm%d_%d" % (it, o))
            b.op("act", lambda e, o=o, bank=bank: e.activation(
                     y2_t[:, o * TW2:(o + 1) * TW2], bank[:, 0:TW2],
                     ACT.Identity, bias=bpr_v[:, o:o + 1]),
                 reads=[bkey, "d_bprv"], writes=[y2k + "#%d" % o],
                 tag="d2_y%d_%d" % (it, o))
        emit_ln(p, y2_t, y2k, TW2, banks2, stat, "stat",
                tag="d2_ln%d" % it)
        ln_apply_residual(p, y2_t, y2k, x2_t, x2k, xo2_t, xo2k, TW2, banks2,
                          tag="d2_res%d" % it)
        store_fm(p, "sp", dst_hbm, dst_key + "#st%d" % it, xo2_t, xo2k, t0_,
                 TW2, tag="d2_st%d" % it)


# version 11
# speedup vs baseline: 1.5268x; 1.2617x over previous
import numpy as np
from contextlib import ExitStack

import concourse.bass as bass
import concourse.mybir as mybir
from concourse.bass_utils import run_bass_kernel_spmd

F32 = mybir.dt.float32
BF16 = mybir.dt.bfloat16
FP8 = mybir.dt.float8e4
AX = mybir.AxisListType.X
OP = mybir.AluOpType
ACT = mybir.ActivationFunctionType

D = 1024
NTOK = 8192
SEQ = 4096
NCH = 8
EPS = 1e-5

_ENG_ATTR = {"sp": "sync", "pe": "tensor", "act": "scalar", "dve": "vector",
             "pool": "gpsimd"}


class _Op:
    __slots__ = ("eng", "fn", "reads", "writes", "dma", "deps", "need_inc",
                 "inc_idx", "idx", "tag", "slot", "use")

    def __init__(self, idx, eng, fn, reads, writes, dma, tag):
        self.idx = idx
        self.eng = eng
        self.fn = fn
        self.reads = reads
        self.writes = writes
        self.dma = dma
        self.deps = set()
        self.need_inc = dma
        self.inc_idx = None
        self.tag = tag


class Build:
    def __init__(self, nc, stack):
        self.nc = nc
        self.stack = stack
        self.ops = []
        self._last_w = {}
        self._readers = {}
        self._bar = None

    def op(self, eng, fn, reads=(), writes=(), dma=False, tag=""):
        o = _Op(len(self.ops), eng, fn, tuple(reads), tuple(writes), dma, tag)
        if self._bar is not None:
            o.deps.add(self._bar)
        for buf in o.reads:
            wi = self._last_w.get(buf)
            if wi is not None:
                o.deps.add(wi)
            self._readers.setdefault(buf, []).append(o.idx)
        for buf in o.writes:
            wi = self._last_w.get(buf)
            if wi is not None:
                o.deps.add(wi)
            for r in self._readers.get(buf, ()):
                o.deps.add(r)
            self._readers[buf] = []
            self._last_w[buf] = o.idx
        o.deps.discard(o.idx)
        self.ops.append(o)
        return o

    def dma(self, queue, out, in_, reads, writes, tag=""):
        def fn(eng):
            return eng.dma_start(out=out, in_=in_)
        return self.op(queue, fn, reads=reads, writes=writes, dma=True, tag=tag)

    def barrier(self, tag="bar"):
        keys = sorted(set(self._last_w) | set(self._readers))
        o = self.op("dve", lambda e: e.engine_nop(), reads=keys,
                    writes=keys, tag=tag)
        self._bar = o.idx
        return o

    def finalize(self):
        nc = self.nc
        ops = self.ops
        for o in ops:
            for dd in o.deps:
                a = ops[dd]
                if a.dma:
                    continue
                if a.eng != o.eng:
                    a.need_inc = True
        NSLOT = {"sp": 12, "pool": 8, "act": 4, "dve": 4, "pe": 4}
        dcount = {e: 0 for e in _ENG_ATTR}
        csem_n = {e: 0 for e in _ENG_ATTR}
        slot_use = {}
        for o in ops:
            if o.dma:
                i = dcount[o.eng]
                dcount[o.eng] += 1
                o.slot = (o.eng, i % NSLOT[o.eng])
                o.use = i // NSLOT[o.eng] + 1
                slot_use[o.slot] = o.use
            elif o.need_inc:
                csem_n[o.eng] += 1
                o.inc_idx = csem_n[o.eng]
        csem, dsem = {}, {}
        for e in _ENG_ATTR:
            if csem_n[e]:
                csem[e] = self.stack.enter_context(nc.semaphore(f"c_{e}"))
        for sl in slot_use:
            dsem[sl] = self.stack.enter_context(
                nc.semaphore(f"d_{sl[0]}{sl[1]}"))
        per_eng = {e: [] for e in _ENG_ATTR}
        for o in ops:
            per_eng[o.eng].append(o)
        block = self.stack.enter_context(nc.Block())

        def emit(eng_name, eng_obj):
            water = {}
            covered = -1
            prev_idx = -1

            def wait(key, val):
                if water.get(key, 0) >= val:
                    return
                sem = dsem[key[1]] if key[0] == "d" else csem[key[1]]
                eng_obj.wait_ge(sem, val)
                water[key] = val

            for o in per_eng[eng_name]:
                req = {}
                need_drain = False
                for dd in sorted(o.deps):
                    a = ops[dd]
                    if a.eng == o.eng and not a.dma:
                        if dd > covered and o.eng != "pe":
                            need_drain = True
                        continue
                    key = ("d", a.slot) if a.dma else ("c", a.eng)
                    val = 16 * a.use if a.dma else a.inc_idx
                    if req.get(key, 0) < val:
                        req[key] = val
                if o.dma and o.use >= 2:
                    k = ("d", o.slot)
                    if req.get(k, 0) < 16 * (o.use - 1):
                        req[k] = max(req.get(k, 0), 16 * (o.use - 1))
                if need_drain:
                    eng_obj.drain()
                    covered = prev_idx
                for key, val in sorted(req.items(), key=str):
                    wait(key, val)
                ins = o.fn(eng_obj)
                prev_idx = o.idx
                if o.dma:
                    ins.then_inc(dsem[o.slot], 16)
                elif o.need_inc:
                    assert ins is not None, f"op {o.tag} needs inc, returned None"
                    ins.then_inc(csem[o.eng], 1)
            return water

        @block.sync
        def _(sync):
            water = emit("sp", sync)
            for sl, use in sorted(slot_use.items(), key=str):
                if water.get(("d", sl), 0) < 16 * use:
                    sync.wait_ge(dsem[sl], 16 * use)
            for e in sorted(_ENG_ATTR):
                n = csem_n[e]
                if e != "sp" and n and water.get(("c", e), 0) < n:
                    sync.wait_ge(csem[e], n)

        @block.tensor
        def _(tensor):
            emit("pe", tensor)

        @block.scalar
        def _(scalar):
            emit("act", scalar)

        @block.vector
        def _(vector):
            emit("dve", vector)

        @block.gpsimd
        def _(gpsimd):
            emit("pool", gpsimd)



class Prog:
    def __init__(self):
        self.nc = bass.Bass("TRN2", target_bir_lowering=False, debug=False)
        self.stack = ExitStack()
        self.b = Build(self.nc, self.stack)
        self.sb = {}

    def sbuf(self, name, shape, dtype=F32):
        t = self.stack.enter_context(self.nc.sbuf_tensor(name, list(shape), dtype))
        self.sb[name] = t
        return t

    def sbuf_alias(self, name, shape, dtype, offset):
        t = self.nc.alloc_sbuf_tensor_at(name, list(shape), dtype,
                                         offset=offset)
        self.sb[name] = t
        return t

    def psum(self, name, shape=(128, 512)):
        t = self.stack.enter_context(self.nc.psum_tensor(name, list(shape), F32))
        self.sb[name] = t
        return t

    def dram_in(self, name, shape, dtype=F32):
        return self.nc.dram_tensor(name, list(shape), dtype, kind="ExternalInput").ap()

    def dram_out(self, name, shape, dtype=F32):
        return self.nc.dram_tensor(name, list(shape), dtype, kind="ExternalOutput").ap()

    def dram_tmp(self, name, shape, dtype=F32):
        return self.nc.dram_tensor(name, list(shape), dtype, kind="Internal").ap()

    def cval(self, value, parts=1):
        if not hasattr(self, "_cvals"):
            self._cvals = {}
            self._cval_tile = self.sbuf("cvals", [128, 32], F32)
        if value not in self._cvals:
            j = len(self._cvals)
            assert j < 32
            self._cvals[value] = j
            sl = self._cval_tile[:, j:j + 1]
            self.b.op("dve", lambda e, sl=sl, v=value: e.memset(sl, v),
                      writes=["cvals"], tag="cval")
        j = self._cvals[value]
        return self._cval_tile[0:parts, j:j + 1]


def setup_consts(p):
    b = p.b
    ones_k = p.sbuf("ones_k", [128, 1], F32)
    ones_k16 = p.sbuf("ones_k16", [128, 1], BF16)
    ones_m = p.sbuf("ones_m", [1, 128], F32)
    ident = p.sbuf("ident", [128, 128], F32)
    tri = p.sbuf("tri", [128, 128], F32)
    iot = p.sbuf("iot", [128, 128], F32)
    b.op("dve", lambda e: e.memset(ones_k[:], 1.0), writes=["ones_k"])
    b.op("dve", lambda e: e.memset(ones_k16[:], 1.0), writes=["ones_k16"])
    b.op("dve", lambda e: e.memset(ones_m[:], 1.0), writes=["ones_m"])
    b.op("dve", lambda e: e.memset(p.sb["ones_t"][:], 1.0), writes=["ones_t"])
    b.op("pool", lambda e: e.iota(iot[:], pattern=[[-1, 128]], base=0,
                                  channel_multiplier=1,
                                  allow_small_or_imprecise_dtypes=True),
         writes=["iot"], tag="iota_ident")
    b.op("dve", lambda e: e.tensor_single_scalar(ident[:], iot[:], 0.0,
                                                 OP.is_equal),
         reads=["iot"], writes=["ident"], tag="ident")
    b.op("pool", lambda e: e.iota(iot[:], pattern=[[1, 128]], base=0,
                                  channel_multiplier=-1,
                                  allow_small_or_imprecise_dtypes=True),
         reads=["ident"], writes=["iot"], tag="iota_tri")
    b.op("dve", lambda e: e.tensor_single_scalar(tri[:], iot[:], 0.0,
                                                 OP.is_ge),
         reads=["iot"], writes=["tri"], tag="tri")



def ck(key, nch=NCH):
    return [key + "#%d" % c for c in range(nch)]


def load_fm(p, queue, dst, dst_key, src_hbm, t0, tw, nch=NCH, tag="ld",
            chunked=True):
    src = src_hbm.rearrange("(c p) t -> p c t", p=128)[:, :, t0:t0 + tw]
    d2 = dst.rearrange("p (c t) -> p c t", c=nch)
    wk = ck(dst_key, nch) if chunked else [dst_key]
    p.b.dma(queue, d2, src, reads=[], writes=wk, tag=tag)


def store_fm(p, queue, dst_hbm, dst_key, src, src_key, t0, tw, nch=NCH,
             tag="st", chunked=True):
    dst = dst_hbm.rearrange("(c p) t -> p c t", p=128)[:, :, t0:t0 + tw]
    s2 = src.rearrange("p (c t) -> p c t", c=nch)
    rk = ck(src_key, nch) if chunked else [src_key]
    p.b.dma(queue, dst, s2, reads=rk, writes=[dst_key], tag=tag)


def load_vec(p, queue, dst, dst_key, src_hbm, tag="ldv"):
    p.b.dma(queue, dst, src_hbm[:], reads=[], writes=[dst_key], tag=tag)


def load_w(p, queue, dst, dst_key, src_hbm, rows, cols, col0=0, ncols=None,
           tag="ldw"):
    if ncols is None:
        ncols = cols
    src = src_hbm[:, col0:col0 + ncols].rearrange("(c p) o -> p c o", p=128)
    d2 = dst.rearrange("p (c o) -> p c o", c=rows // 128)
    p.b.dma(queue, d2, src, reads=[], writes=[dst_key], tag=tag)



def emit_ln(p, x, xk, tw, banks, stat, statk, *, nch=NCH, nfeat=D,
            x_bf16=False, scale_ap=None, scale_key=None, sq=None, sqk=None,
            sq_slots=None, tag="ln"):
    b = p.b
    ones_k = p.sb["ones_k16"] if x_bf16 else p.sb["ones_k"]
    ones_kk = "ones_k16" if x_bf16 else "ones_k"
    ones_f = p.sb["ones_k"]
    ones_m = p.sb["ones_m"]
    if sq is None:
        sq = p.sb["sq_t"][:]
        sqk = "sq_t"

    def mm_s(eng):
        ins = None
        for c in range(nch):
            ins = eng.matmul(banks["sa"], ones_k[:], x[:, c * tw:(c + 1) * tw],
                             start=(c == 0), stop=(c == nch - 1))
        return ins

    b.op("pe", mm_s, reads=ck(xk, nch) + [ones_kk], writes=[banks["ka"]],
         tag=tag + "_ss")
    if sq_slots is None:
        sqks = [sqk] if sqk == "sq_t" else ck(sqk, nch)
        b.op("act", lambda e: e.square(sq[:, :nch * tw], x[:, :nch * tw]),
             reads=ck(xk, nch), writes=sqks, tag=tag + "_sq")

        def mm_q(eng):
            ins = None
            for c in range(nch):
                ins = eng.matmul(banks["sb"], ones_f[:],
                                 sq[:, c * tw:(c + 1) * tw],
                                 start=(c == 0), stop=(c == nch - 1))
            return ins

        b.op("pe", mm_q, reads=sqks + ["ones_k"], writes=[banks["kb"]],
             tag=tag + "_sq2")
    else:
        nsl = sq_slots
        for c in range(nch):
            sl = sq[:, (c % nsl) * tw:(c % nsl + 1) * tw]
            b.op("act", lambda e, sl=sl, c=c: e.square(
                     sl, x[:, c * tw:(c + 1) * tw]),
                 reads=[xk + "#%d" % c], writes=[sqk + "#%d" % (c % nsl)],
                 tag=tag + "_sq%d" % c)
        for c in range(nch):
            sl = sq[:, (c % nsl) * tw:(c % nsl + 1) * tw]
            b.op("pe", lambda e, sl=sl, c=c: e.matmul(
                     banks["sb"], ones_f[:], sl,
                     start=(c == 0), stop=(c == nch - 1)),
                 reads=[sqk + "#%d" % (c % nsl), "ones_k"],
                 writes=[banks["kb"]], tag=tag + "_sq2_%d" % c)
    r = stat[0:1, 0:tw]
    mur = stat[0:1, tw:2 * tw]
    mu = stat[0:1, 2 * tw:3 * tw]
    v = stat[0:1, 3 * tw:4 * tw]
    m2 = stat[0:1, 4 * tw:5 * tw]
    b.op("act", lambda e: e.mul(mu, banks["sa"], 1.0 / nfeat),
         reads=[banks["ka"]], writes=[statk], tag=tag + "_mu")
    b.op("act", lambda e: e.mul(v, banks["sb"], 1.0 / nfeat),
         reads=[banks["kb"]], writes=[statk], tag=tag + "_ex2")
    b.op("dve", lambda e: e.tensor_mul(m2, mu, mu), reads=[statk],
         writes=[statk], tag=tag + "_mu2")
    b.op("dve", lambda e: e.tensor_sub(v, v, m2), reads=[statk],
         writes=[statk], tag=tag + "_var")
    epsap = p.cval(EPS)
    b.op("act", lambda e: e.activation(m2, v, ACT.Sqrt, bias=epsap),
         reads=[statk, "cvals"], writes=[statk], tag=tag + "_sd")
    b.op("dve", lambda e: e.reciprocal(r, m2),
         reads=[statk], writes=[statk], tag=tag + "_r")
    b.op("dve", lambda e: e.tensor_mul(mur, mu, r), reads=[statk],
         writes=[statk], tag=tag + "_mur")
    if scale_ap is not None:
        rm2 = stat[0:1, 0:2 * tw]
        b.op("dve", lambda e: e.tensor_scalar(rm2, rm2, scale_ap, None,
                                              OP.mult),
             reads=[statk, scale_key], writes=[statk], tag=tag + "_fs")
    b.op("pe", lambda e: e.matmul(banks["br"], ones_m[:], r, start=True,
                                  stop=True),
         reads=[statk, "ones_m"], writes=[banks["kr"]], tag=tag + "_bcr")
    b.op("pe", lambda e: e.matmul(banks["bm"], ones_m[:], mur, start=True,
                                  stop=True),
         reads=[statk, "ones_m"], writes=[banks["km"]], tag=tag + "_bcm")


def ln_apply(p, x, xk, ln, lnk, tw, banks, nch=NCH, tag="lnap"):
    b = p.b
    for c in range(nch):
        xs = x[:, c * tw:(c + 1) * tw]
        ls = ln[:, c * tw:(c + 1) * tw]
        b.op("dve", lambda e, xs=xs, ls=ls: e.tensor_mul(ls, xs, banks["br"]),
             reads=[xk + "#%d" % c, banks["kr"]], writes=[lnk + "#%d" % c],
             tag=tag + "_m%d" % c)
    for c in range(nch):
        ls = ln[:, c * tw:(c + 1) * tw]
        b.op("dve", lambda e, ls=ls: e.tensor_sub(ls, ls, banks["bm"]),
             reads=[lnk + "#%d" % c, banks["km"]], writes=[lnk + "#%d" % c],
             tag=tag + "_s%d" % c)


def ln_apply_residual(p, y, yk, x, xk, out, outk, tw, banks, nch=NCH,
                      tag="lnres"):
    b = p.b
    for c in range(nch):
        ys = y[:, c * tw:(c + 1) * tw]
        os_ = out[:, c * tw:(c + 1) * tw]
        b.op("dve", lambda e, ys=ys, os_=os_: e.tensor_mul(os_, ys, banks["br"]),
             reads=[yk + "#%d" % c, banks["kr"]], writes=[outk + "#%d" % c],
             tag=tag + "_m%d" % c)
    for c in range(nch):
        os_ = out[:, c * tw:(c + 1) * tw]
        b.op("dve", lambda e, os_=os_: e.tensor_sub(os_, os_, banks["bm"]),
             reads=[outk + "#%d" % c, banks["km"]], writes=[outk + "#%d" % c],
             tag=tag + "_s%d" % c)
    for c in range(nch):
        xs = x[:, c * tw:(c + 1) * tw]
        os_ = out[:, c * tw:(c + 1) * tw]
        b.op("dve", lambda e, os_=os_, xs=xs: e.tensor_add(os_, os_, xs),
             reads=[outk + "#%d" % c, xk + "#%d" % c],
             writes=[outk + "#%d" % c], tag=tag + "_a%d" % c)


def std_banks(p, tw):
    ps = [p.sb[f"ps{i}"] for i in range(8)]
    return dict(sa=ps[6][0:1, 0:tw], sb=ps[7][0:1, 0:tw],
                br=ps[4][:, 0:tw], bm=ps[5][:, 0:tw],
                ka="ps6", kb="ps7", kr="ps4", km="ps5")



def _prep_weights(inputs):
    w = {}
    T = lambda a: np.ascontiguousarray(np.asarray(a).T)
    V = lambda v: np.ascontiguousarray(np.asarray(v).reshape(-1, 128).T)
    w["w_qkv_t"] = T(inputs["L0_attn_w_qkv"])
    w["b_qkv"] = V(inputs["L0_attn_b_qkv"])
    w["b_v_row"] = np.ascontiguousarray(
        np.asarray(inputs["L0_attn_b_qkv"])[2048:].reshape(1, 1024))
    w["w_g1_t"] = T(inputs["L0_attn_w_g1"])
    w["b_g1"] = np.ascontiguousarray(
        np.asarray(inputs["L0_attn_b_g1"]).reshape(64, 1))
    w["w_g2_t"] = T(inputs["L0_attn_w_g2"])
    w["b_g2"] = V(inputs["L0_attn_b_g2"])
    w["w_out_t"] = T(inputs["L0_attn_w_out"])
    w["b_out"] = V(inputs["L0_attn_b_out"])
    w["ln_g"] = V(inputs["L0_attn_ln_g"])
    w["ln_b"] = V(inputs["L0_attn_ln_b"])
    w["ffn_k"] = np.asarray(inputs["L0_ffn_kernel"])
    w["ffn_sin"] = np.asarray(inputs["L0_ffn_input_scale"])
    w["ffn_sk"] = np.asarray(inputs["L0_ffn_kernel_scale"])
    w["ffn_bias"] = V(inputs["L0_ffn_bias"])
    w["wq_t"] = T(inputs["L1_attn_Wq"])
    w["wk_t"] = T(inputs["L1_attn_Wk"])
    w["wv_t"] = T(inputs["L1_attn_Wv"])
    w["wf_t"] = T(inputs["L1_attn_Wf"])
    w["g_w"] = V(inputs["L1_attn_g_w"])
    w["wo_t"] = T(inputs["L1_attn_Wo"])
    w["wfc_t"] = T(inputs["L1_ffn_W_fc"])
    w["b_fc"] = V(inputs["L1_ffn_b_fc"])
    w["afc_t"] = T(inputs["L1_ffn_A_fc"])
    w["bfc_t"] = T(inputs["L1_ffn_B_fc"])
    w["wpr_t"] = T(inputs["L1_ffn_W_proj"])
    w["b_pr"] = V(inputs["L1_ffn_b_proj"])
    w["apr_t"] = T(inputs["L1_ffn_A_proj"])
    w["bpr_t"] = T(inputs["L1_ffn_B_proj"])
    return w


def _prep_x(inputs):
    x = np.asarray(inputs["L0_attn_x"])
    xs = []
    for c in range(8):
        xc = np.concatenate([x[2 * c], x[2 * c + 1]], axis=0)
        xs.append(np.ascontiguousarray(xc.T))
    return xs


def kernel(**inputs):
    w = _prep_weights(inputs)
    xs = _prep_x(inputs)
    p = build_program()
    in_maps = []
    for c in range(8):
        m = dict(w)
        m["x_fm"] = xs[c]
        in_maps.append(m)
    with p.stack:
        res = run_bass_kernel_spmd(p.nc, in_maps, core_ids=list(range(8)))
    outs = []
    for c in range(8):
        y = res.results[c]["y_fm"]
        outs.append(y[:, :SEQ].T)
        outs.append(y[:, SEQ:].T)
    return np.ascontiguousarray(np.stack(outs, axis=0).astype(np.float32))



def alloc_globals(p):
    for i in range(8):
        p.psum(f"ps{i}")
    p.off = {}
    rup = lambda a: (a + 31) & ~31
    for i in range(3):
        p.off[f"t{i}"] = rup(p.nc._sbuf_addr_for_side(None))
        p.sbuf(f"t{i}", [128, 4096], F32)
    p.sbuf("sq_t", [128, 4096], F32)
    p.off["wbig"] = rup(p.nc._sbuf_addr_for_side(None))
    p.sbuf("wbig", [128, 16384], F32)
    p.sbuf("u0", [128, 4096], BF16)
    p.sbuf_alias("hb", [128, 4096], BF16, p.off["wbig"])
    p.sbuf_alias("lb", [128, 4096], BF16, p.off["wbig"] + 8192)
    p.sbuf_alias("mb", [128, 4096], BF16, p.off["wbig"] + 16384)
    p.sbuf_alias("wq16", [128, 8192], BF16, p.off["wbig"] + 24576)
    p.sbuf_alias("d0w16", [128, 8192], BF16, p.off["t1"])
    p.sbuf_alias("hbB", [128, 4096], BF16, p.off["wbig"] + 40960)
    p.sbuf_alias("lbB", [128, 4096], BF16, p.off["wbig"] + 49152)
    p.sbuf_alias("d2h0", [128, 8192], BF16, p.off["wbig"])
    p.sbuf_alias("d2l0", [128, 8192], BF16, p.off["wbig"] + 16384)
    p.sbuf_alias("d2h1", [128, 8192], BF16, p.off["wbig"] + 32768)
    p.sbuf_alias("d2l1", [128, 8192], BF16, p.off["wbig"] + 49152)
    p.sbuf_alias("wq16d", [128, 8192], BF16, p.off["t0"])
    p.sbuf("kq8", [128, 8192], FP8)
    p.sbuf("q8t", [128, 4096], FP8)
    p.sbuf("stat", [1, 5 * 512], F32)
    p.sbuf("ones_t", [128, 256], F32)
    p.sbuf("c_mx", [128, 4], F32)
    p.sbuf("c_fr", [1, 256], F32)
    p.sbuf("c_sc", [128, 256], F32)
    p.sbuf("c_asb", [128, 256], F32)
    p.sbuf("c_utm", [128, 256], F32)
    setup_consts(p)


def build_program(phases=("A", "B", "C", "D")):
    p = Prog()
    alloc_globals(p)
    x_in = p.dram_in("x_fm", [D, NTOK], F32)
    y_out = p.dram_out("y_fm", [D, NTOK], F32)
    s1 = p.dram_tmp("s1", [D, NTOK], F32)
    s2 = p.dram_tmp("s2", [D, NTOK], F32)
    scr = [(s1, "s1"), (s2, "s2")]
    chain = [ph for ph in ("A", "B", "C", "D") if ph in phases]
    emits = {"A": emit_phase_a, "B": emit_phase_b, "C": emit_phase_c,
             "D": emit_phase_d}
    cur = (x_in, "x_fm")
    for i, ph in enumerate(chain):
        dst = (y_out, "y_fm") if i == len(chain) - 1 else scr[i % 2]
        emits[ph](p, cur, dst)
        cur = dst
    p.b.finalize()
    return p



def emit_phase_a(p, src, dst):
    src_hbm, _ = src
    dst_hbm, dst_key = dst
    b = p.b
    TW = 256
    nt = SEQ // TW
    nsub = TW // 128
    ps = [p.sb[f"ps{i}"] for i in range(8)]
    banks = std_banks(p, TW)
    t = [p.sb[f"t{i}"] for i in range(3)]
    sq_hi = p.sb["sq_t"][:, 2048:4096]
    ek_t, ekk = t[2][:, 0:2048], "a_ek"
    ektm, ekmk = t[2][:, 2048:4096], "a_ektm"
    v_tm, vtmk = sq_hi, "sq_hi"
    q_t, qk = t[2][:, 0:2048], "a_q"
    ao_t, aok = t[2][:, 2048:4096], "a_ao"
    lno_t, lnok = sq_hi, "sq_hi"
    stat = p.sb["stat"]
    wbig = p.sb["wbig"]
    ident = p.sb["ident"]
    ones_m = p.sb["ones_m"]

    bqkv = p.sbuf("a_bqkv", [128, 24], F32)
    load_vec(p, "sp", bqkv[:], "a_bqkv", p.dram_in("b_qkv", [128, 24], F32))
    bvrow = p.sbuf("a_bvrow", [1, D], F32)
    load_vec(p, "sp", bvrow[:], "a_bvrow", p.dram_in("b_v_row", [1, D], F32))
    wg1 = p.sbuf("a_wg1", [128, NCH * 64], F32)
    load_w(p, "sp", wg1[:], "a_wg1", p.dram_in("w_g1_t", [D, 64], F32), D, 64)
    bg1 = p.sbuf("a_bg1", [64, 1], F32)
    load_vec(p, "sp", bg1[:], "a_bg1", p.dram_in("b_g1", [64, 1], F32))
    wg2 = p.sbuf("a_wg2", [64, D], F32)
    load_vec(p, "sp", wg2[:], "a_wg2", p.dram_in("w_g2_t", [64, D], F32))
    bg2 = p.sbuf("a_bg2", [128, NCH], F32)
    load_vec(p, "sp", bg2[:], "a_bg2", p.dram_in("b_g2", [128, NCH], F32))
    bout = p.sbuf("a_bout", [128, NCH], F32)
    load_vec(p, "sp", bout[:], "a_bout", p.dram_in("b_out", [128, NCH], F32))
    lng = p.sbuf("a_lng", [128, NCH], F32)
    load_vec(p, "sp", lng[:], "a_lng", p.dram_in("ln_g", [128, NCH], F32))
    lnb = p.sbuf("a_lnb", [128, NCH], F32)
    load_vec(p, "sp", lnb[:], "a_lnb", p.dram_in("ln_b", [128, NCH], F32))
    sacc = p.sbuf("a_sacc", [128, NCH], F32)
    tacc = p.sbuf("a_tacc", [128, NCH], F32)
    kv_sb = p.sbuf("a_kv", [128, NCH * 64], F32)
    srec = p.sbuf("a_srec", [128, NCH], F32)
    gh_t = p.sbuf("a_gh", [64, TW], F32)
    gtmp = p.sbuf("a_gtmp", [128, TW], F32)
    wqkv_h = p.dram_in("w_qkv_t", [D, 3072], F32)
    w_out_h = p.dram_in("w_out_t", [D, D], F32)

    def ptile(i, it):
        par = it % 2
        return t[i][:, par * 2048:(par + 1) * 2048], "a_t%d_%d" % (i, par)

    for sq_ in range(2):
        base = sq_ * SEQ
        b.op("dve", lambda e: e.memset(sacc[:], 0.0), writes=["a_sacc"],
             tag="a_z")
        load_w(p, "sp", wbig[:, 0:NCH * D], "wbig", wqkv_h, D, 3072,
               col0=1024, ncols=D, tag="a_ldk")
        load_w(p, "sp", wbig[:, NCH * D:2 * NCH * D], "wbig", wqkv_h, D, 3072,
               col0=2048, ncols=D, tag="a_ldv")

        for it in range(nt):
            t0 = base + it * TW
            x_t, xk = ptile(0, it)
            ln_t, lnk = ptile(1, it)
            load_fm(p, "sp", x_t, xk, src_hbm, t0, TW, tag="a1_ldx%d" % it)
            emit_ln(p, x_t, xk, TW, banks, stat, "stat", tag="a1_ln%d" % it)
            ln_apply(p, x_t, xk, ln_t, lnk, TW, banks, tag="a1_lnap%d" % it)
            for o in range(NCH):
                bank = ps[o % 2]
                bkey = "ps%d" % (o % 2)

                def mm(eng, o=o, bank=bank, ln_t=ln_t):
                    ins = None
                    for c in range(NCH):
                        col = c * D + o * 128
                        ins = eng.matmul(bank[:, 0:TW], wbig[:, col:col + 128],
                                         ln_t[:, c * TW:(c + 1) * TW],
                                         start=(c == 0), stop=(c == NCH - 1))
                    return ins

                b.op("pe", mm, reads=["wbig"] + ck(lnk), writes=[bkey],
                     tag="a1_kmm%d_%d" % (it, o))
                eksl = ek_t[:, o * TW:(o + 1) * TW]
                b.op("act", lambda e, eksl=eksl, bank=bank, o=o: e.activation(
                         eksl, bank[:, 0:TW], ACT.Exp,
                         bias=bqkv[:, 8 + o:9 + o],
                         accum_out=tacc[:, o:o + 1]),
                     reads=[bkey, "a_bqkv"],
                     writes=[ekk + "#%d" % o, "a_tacc"],
                     tag="a1_exp%d_%d" % (it, o))
                b.op("dve", lambda e, o=o: e.tensor_add(
                         sacc[:, o:o + 1], sacc[:, o:o + 1], tacc[:, o:o + 1]),
                     reads=["a_tacc", "a_sacc"], writes=["a_sacc"],
                     tag="a1_sacc%d_%d" % (it, o))
            for half_ in range(2):
                for ts in range(nsub):
                    bank = ps[half_]
                    bkey = "ps%d" % half_

                    def mmv(eng, half_=half_, ts=ts, bank=bank, ln_t=ln_t):
                        ins = None
                        for c in range(NCH):
                            ins = eng.matmul(
                                bank[:, 0:512],
                                ln_t[:, c * TW + ts * 128:
                                     c * TW + ts * 128 + 128],
                                wbig[:, NCH * D + c * D + half_ * 512:
                                     NCH * D + c * D + (half_ + 1) * 512],
                                start=(c == 0), stop=False)
                        ins = eng.matmul(
                            bank[:, 0:512], ones_m[:],
                            bvrow[0:1, half_ * 512:(half_ + 1) * 512],
                            start=False, stop=True)
                        return ins

                    b.op("pe", mmv,
                         reads=["wbig", "ones_m", "a_bvrow"] + ck(lnk),
                         writes=[bkey], tag="a1_vmm%d_%d_%d" % (it, half_, ts))
                    b.op("act", lambda e, half_=half_, ts=ts, bank=bank: e.copy(
                             v_tm[:, ts * D + half_ * 512:
                                  ts * D + (half_ + 1) * 512],
                             bank[:, 0:512]),
                         reads=[bkey], writes=[vtmk + "#%d" % (ts * 2 + half_)],
                         tag="a1_vcp%d_%d_%d" % (it, half_, ts))
            for o in range(NCH):
                bank = ps[o % 2]
                bkey = "ps%d" % (o % 2)

                def mmt(eng, o=o, bank=bank):
                    ins = None
                    for ts in range(nsub):
                        ins = eng.matmul(bank[:, ts * 128:(ts + 1) * 128],
                                         ek_t[:, o * TW + ts * 128:
                                              o * TW + (ts + 1) * 128],
                                         ident[:], is_transpose=True,
                                         start=(ts == 0), stop=(ts == nsub - 1))
                    return ins

                b.op("pe", mmt, reads=[ekk + "#%d" % o, "ident"],
                     writes=[bkey], tag="a1_ekT%d_%d" % (it, o))

                def cpt(eng, o=o, bank=bank):
                    ins = None
                    for ts in range(nsub):
                        ins = eng.tensor_copy(
                            ektm[:, ts * D + o * 128: ts * D + o * 128 + 128],
                            bank[:, ts * 128:(ts + 1) * 128])
                    return ins

                b.op("dve", cpt, reads=[bkey], writes=[ekmk + "#%d" % o],
                     tag="a1_ekTc%d_%d" % (it, o))
            for hh in range(NCH):
                bankkv = ps[2 + hh // 4]
                bkey = "ps%d" % (2 + hh // 4)
                off = (hh % 4) * 128

                def mkv(eng, hh=hh, bankkv=bankkv, off=off, it=it):
                    ins = None
                    for ts in range(nsub):
                        first = (it == 0 and hh % 4 == 0 and ts == 0)
                        last = (it == nt - 1 and hh % 4 == 3 and ts == nsub - 1)
                        ins = eng.matmul(
                            bankkv[:, off:off + 128],
                            ektm[:, ts * D + hh * 128: ts * D + hh * 128 + 128],
                            v_tm[:, ts * D + hh * 128: ts * D + hh * 128 + 128],
                            start=first, stop=last)
                    return ins

                vu = [vtmk + "#%d" % (ts * 2 + hh // 4) for ts in range(2)]
                b.op("pe", mkv, reads=[ekmk + "#%d" % hh] + vu, writes=[bkey],
                     tag="a1_kv%d_%d" % (it, hh))
        b.op("dve", lambda e: e.reciprocal(srec[:], sacc[:]),
             reads=["a_sacc"], writes=["a_srec"], tag="a_srec")
        for hh in range(NCH):
            bankkv = ps[2 + hh // 4]
            bkey = "ps%d" % (2 + hh // 4)
            off = (hh % 4) * 128
            b.op("dve", lambda e, hh=hh, bankkv=bankkv, off=off: e.tensor_scalar(
                     kv_sb[0:64, hh * 64:(hh + 1) * 64],
                     bankkv[0:64, off:off + 64], srec[0:64, hh:hh + 1], None,
                     OP.mult),
                 reads=[bkey, "a_srec"], writes=["a_kv"], tag="a_kvn%d_0" % hh)
            b.op("dve", lambda e, hh=hh, bankkv=bankkv, off=off: e.tensor_scalar(
                     kv_sb[64:128, hh * 64:(hh + 1) * 64],
                     bankkv[64:128, off + 64:off + 128], srec[64:128, hh:hh + 1],
                     None, OP.mult),
                 reads=[bkey, "a_srec"], writes=["a_kv"], tag="a_kvn%d_1" % hh)

        load_w(p, "sp", wbig[:, 0:NCH * D], "wbig", wqkv_h, D, 3072, col0=0,
               ncols=D, tag="a_ldq")
        load_w(p, "sp", wbig[:, NCH * D:2 * NCH * D], "wbig", w_out_h, D, D,
               tag="a_ldwo")

        for it in range(nt):
            t0 = base + it * TW
            x_t, xk = ptile(0, it)
            ln_t, lnk = ptile(1, it)
            xo_t, xok = ln_t, lnk
            load_fm(p, "sp", x_t, xk, src_hbm, t0, TW, tag="a2_ldx%d" % it)
            emit_ln(p, x_t, xk, TW, banks, stat, "stat", tag="a2_ln%d" % it)
            ln_apply(p, x_t, xk, ln_t, lnk, TW, banks, tag="a2_lnap%d" % it)
            for o in range(NCH):
                bank = ps[o % 2]
                bkey = "ps%d" % (o % 2)

                def mm(eng, o=o, bank=bank, ln_t=ln_t):
                    ins = None
                    for c in range(NCH):
                        col = c * D + o * 128
                        ins = eng.matmul(bank[:, 0:TW], wbig[:, col:col + 128],
                                         ln_t[:, c * TW:(c + 1) * TW],
                                         start=(c == 0), stop=(c == NCH - 1))
                    return ins

                b.op("pe", mm, reads=["wbig"] + ck(lnk), writes=[bkey],
                     tag="a2_qmm%d_%d" % (it, o))
                qsl = q_t[:, o * TW:(o + 1) * TW]
                b.op("act", lambda e, qsl=qsl, bank=bank, o=o: e.activation(
                         qsl, bank[:, 0:TW], ACT.Silu, bias=bqkv[:, o:o + 1]),
                     reads=[bkey, "a_bqkv"], writes=[qk + "#%d" % o],
                     tag="a2_silu%d_%d" % (it, o))
            for hh in range(NCH):
                bank = ps[2 + (hh % 2)]
                bkey = "ps%d" % (2 + (hh % 2))

                def mma(eng, hh=hh, bank=bank):
                    ins = None
                    for sub in range(2):
                        rb = sub * 64
                        ins = eng.matmul(
                            bank[rb:rb + 64, 0:TW],
                            kv_sb[rb:rb + 64, hh * 64:(hh + 1) * 64],
                            q_t[rb:rb + 64, hh * TW:(hh + 1) * TW],
                            start=True, stop=True)
                    return ins

                b.op("pe", mma, reads=["a_kv", qk + "#%d" % hh],
                     writes=[bkey], tag="a2_ao%d_%d" % (it, hh))
                b.op("act", lambda e, hh=hh, bank=bank: e.copy(
                         ao_t[:, hh * TW:(hh + 1) * TW], bank[:, 0:TW]),
                     reads=[bkey], writes=[aok + "#%d" % hh],
                     tag="a2_aocp%d_%d" % (it, hh))
            emit_ln(p, ao_t, aok, TW, banks, stat, "stat", tag="a2_lno%d" % it)
            for c in range(NCH):
                asl = ao_t[:, c * TW:(c + 1) * TW]
                lsl = lno_t[:, c * TW:(c + 1) * TW]
                b.op("dve", lambda e, asl=asl, lsl=lsl: e.tensor_mul(
                         lsl, asl, banks["br"]),
                     reads=[aok + "#%d" % c, banks["kr"]],
                     writes=[lnok + "#%d" % c],
                     tag="a2_lnoap%d_m%d" % (it, c))
            for c in range(NCH):
                lsl = lno_t[:, c * TW:(c + 1) * TW]
                b.op("dve", lambda e, lsl=lsl: e.tensor_sub(lsl, lsl,
                                                            banks["bm"]),
                     reads=[lnok + "#%d" % c, banks["km"]],
                     writes=[lnok + "#%d" % c],
                     tag="a2_lnoap%d_s%d" % (it, c))
            for c in range(NCH):
                lsl = lno_t[:, c * TW:(c + 1) * TW]
                b.op("dve", lambda e, lsl=lsl, c=c: e.tensor_scalar(
                         lsl, lsl, lng[:, c:c + 1], lnb[:, c:c + 1],
                         OP.mult, OP.add),
                     reads=[lnok + "#%d" % c, "a_lng", "a_lnb"],
                     writes=[lnok + "#%d" % c],
                     tag="a2_lnogb%d_%d" % (it, c))

            def mmg1(eng, ln_t=ln_t):
                ins = None
                for c in range(NCH):
                    ins = eng.matmul(ps[3][0:64, 0:TW],
                                     wg1[:, c * 64:(c + 1) * 64],
                                     ln_t[:, c * TW:(c + 1) * TW],
                                     start=(c == 0), stop=(c == NCH - 1))
                return ins

            b.op("pe", mmg1, reads=["a_wg1"] + ck(lnk), writes=["ps3"],
                 tag="a2_g1%d" % it)
            b.op("act", lambda e: e.activation(gh_t[:], ps[3][0:64, 0:TW],
                                               ACT.Identity, bias=bg1[:, 0:1]),
                 reads=["ps3", "a_bg1"], writes=["a_gh"], tag="a2_g1b%d" % it)
            for o in range(NCH):
                bank = ps[o % 2]
                bkey = "ps%d" % (o % 2)
                b.op("pe", lambda e, o=o, bank=bank: e.matmul(
                         bank[:, 0:TW], wg2[:, o * 128:(o + 1) * 128], gh_t[:],
                         start=True, stop=True),
                     reads=["a_wg2", "a_gh"], writes=[bkey],
                     tag="a2_g2%d_%d" % (it, o))
                b.op("act", lambda e, bank=bank, o=o: e.activation(
                         gtmp[:], bank[:, 0:TW], ACT.Sigmoid,
                         bias=bg2[:, o:o + 1]),
                     reads=[bkey, "a_bg2"], writes=["a_gtmp"],
                     tag="a2_sig%d_%d" % (it, o))
                lsl = lno_t[:, o * TW:(o + 1) * TW]
                b.op("dve", lambda e, lsl=lsl: e.tensor_mul(lsl, gtmp[:], lsl),
                     reads=["a_gtmp", lnok + "#%d" % o],
                     writes=[lnok + "#%d" % o],
                     tag="a2_gmul%d_%d" % (it, o))
            for o in range(NCH):
                bank = ps[2 + (o % 2)]
                bkey = "ps%d" % (2 + (o % 2))

                def mmo(eng, o=o, bank=bank):
                    ins = None
                    for c in range(NCH):
                        col = NCH * D + c * D + o * 128
                        ins = eng.matmul(bank[:, 0:TW], wbig[:, col:col + 128],
                                         lno_t[:, c * TW:(c + 1) * TW],
                                         start=(c == 0), stop=(c == NCH - 1))
                    return ins

                b.op("pe", mmo, reads=["wbig"] + ck(lnok), writes=[bkey],
                     tag="a2_wo%d_%d" % (it, o))
                b.op("act", lambda e, o=o, bank=bank: e.activation(
                         ao_t[:, o * TW:(o + 1) * TW], bank[:, 0:TW],
                         ACT.Identity, bias=bout[:, o:o + 1]),
                     reads=[bkey, "a_bout"], writes=[aok + "#%d" % o],
                     tag="a2_wob%d_%d" % (it, o))
            emit_ln(p, ao_t, aok, TW, banks, stat, "stat",
                    tag="a2_ln2_%d" % it)
            ln_apply_residual(p, ao_t, aok, x_t, xk, xo_t, xok, TW, banks,
                              tag="a2_res%d" % it)
            store_fm(p, "pool", dst_hbm, dst_key, xo_t, xok, t0, TW,
                     tag="a2_st%d" % it)



def emit_phase_b(p, src, dst):
    src_hbm, _ = src
    dst_hbm, dst_key = dst
    b = p.b
    TW = 256
    nt = NTOK // TW
    ps = [p.sb[f"ps{i}"] for i in range(8)]
    banks = std_banks(p, TW)
    t = [p.sb[f"t{i}"] for i in range(3)]
    stat = p.sb["stat"]
    u0 = p.sb["u0"]
    kq8 = p.sb["kq8"]
    q8t = p.sb["q8t"]
    ones_m = p.sb["ones_m"]
    b.barrier("b_bar")

    bias16 = p.sbuf("b_bias16", [128, NCH], BF16)
    load_vec(p, "sp", bias16[:], "b_bias16",
             p.dram_in("ffn_bias", [128, D // 128], BF16))
    bias32 = p.sbuf("b_bias32", [128, NCH], F32)
    b.op("dve", lambda e: e.tensor_copy(bias32[:], bias16[:]),
         reads=["b_bias16"], writes=["b_bias32"], tag="b_b32")
    scal = p.sbuf("b_scal", [1, 4], F32)
    b.op("dve", lambda e: e.memset(scal[:], 0.0), writes=["b_scal"], tag="b_z")
    sin_h = p.dram_in("ffn_sin", [1], F32)
    sk_h = p.dram_in("ffn_sk", [1], F32)
    b.dma("sp", scal[0:1, 0:1], sin_h[:], reads=["b_scal"], writes=["b_scal"])
    b.dma("sp", scal[0:1, 3:4], sk_h[:], reads=["b_scal"], writes=["b_scal"])
    b.op("dve", lambda e: e.tensor_mul(scal[0:1, 1:2], scal[0:1, 0:1],
                                       scal[0:1, 3:4]),
         reads=["b_scal"], writes=["b_scal"], tag="sprod")
    scalb = p.sbuf("b_scalb", [128, 2], F32)
    skb = p.sbuf("b_skb", [128, 1], F32)
    b.op("pe", lambda e: e.matmul(ps[3][:, 0:4], ones_m[:], scal[0:1, 0:4],
                                  start=True, stop=True),
         reads=["b_scal", "ones_m"], writes=["ps3"], tag="bc_scal_mm")
    b.op("dve", lambda e: e.tensor_copy(scalb[:], ps[3][:, 0:2]),
         reads=["ps3"], writes=["b_scalb"], tag="bc_scal_cp")
    b.op("dve", lambda e: e.tensor_copy(skb[:], ps[3][:, 3:4]),
         reads=["ps3"], writes=["b_skb"], tag="bc_sk_cp")
    rsin = p.sbuf("b_rsin", [128, 1], F32)
    rsk = p.sbuf("b_rsk", [128, 1], F32)
    b.op("dve", lambda e: e.reciprocal(rsin[:], scalb[:, 0:1]),
         reads=["b_scalb"], writes=["b_rsin"], tag="rsin")
    b.op("dve", lambda e: e.reciprocal(rsk[:], skb[:]),
         reads=["b_skb"], writes=["b_rsk"], tag="rsk")
    ffnk_h = p.dram_in("ffn_k", [D, D], BF16)
    for hf in range(2):
        load_w(p, "sp", u0[:, 0:NCH * 512], "u0", ffnk_h, D, D,
               col0=hf * 512, ncols=512, tag="b_ldk%d" % hf)
        for c in range(NCH):
            sl_in = u0[:, c * 512:(c + 1) * 512]
            sl_out = kq8[:, c * D + hf * 512: c * D + (hf + 1) * 512]
            b.op("dve", lambda e, a=sl_out, x=sl_in: e.tensor_scalar(
                     a, x, rsk[:, 0:1], None, OP.mult),
                 reads=["u0", "b_rsk"], writes=["kq8#%d_%d" % (hf, c)],
                 tag="b_wq8_%d_%d" % (hf, c))

    for it in range(nt):
        t0_ = it * TW
        par = it % 2
        off = par * NCH * TW
        x_t, xk = t[0][:, off:off + NCH * TW], "b_x%d" % par
        ln_t, lnk = t[1][:, off:off + NCH * TW], "b_ln%d" % par
        xo_t, xok = t[2][:, off:off + NCH * TW], "b_xo%d" % par
        xb = u0[:, off:off + NCH * TW]
        xbk = "b_u%d" % par
        q8 = q8t[:, off:off + NCH * TW]
        q8k = "b_q8%d" % par
        gb = [ps[2 * par], ps[2 * par + 1]]
        gbk = ["ps%d" % (2 * par), "ps%d" % (2 * par + 1)]
        load_fm(p, "sp", x_t, xk, src_hbm, t0_, TW, tag="b_ldx%d" % it)
        emit_ln(p, x_t, xk, TW, banks, stat, "stat", tag="b_ln%d" % it)
        ln_apply(p, x_t, xk, ln_t, lnk, TW, banks, tag="b_lnap%d" % it)
        b.op("dve", lambda e, xb=xb, ln_t=ln_t: e.tensor_copy(xb, ln_t),
             reads=ck(lnk), writes=ck(xbk), tag="b_bf%d" % it)
        b.op("dve", lambda e, q8=q8, xb=xb: e.tensor_scalar(
                 q8, xb, rsin[:, 0:1], None, OP.mult),
             reads=ck(xbk) + ["b_rsin"], writes=ck(q8k),
             tag="b_q8_%d" % it)
        for o in range(NCH):
            bank = gb[o % 2]
            bkey = gbk[o % 2]

            def mm(eng, o=o, bank=bank, q8=q8):
                ins = None
                for c in range(NCH):
                    ins = eng.matmul(bank[:, 0:TW],
                                     kq8[:, c * D + o * 128: c * D + o * 128 + 128],
                                     q8[:, c * TW:(c + 1) * TW],
                                     start=(c == 0), stop=(c == NCH - 1))
                return ins

            b.op("pe", mm,
                 reads=["kq8#%d_%d" % (o // 4, c) for c in range(NCH)]
                 + ck(q8k), writes=[bkey], tag="b_mm%d_%d" % (it, o))
            ysl = xb[:, o * TW:(o + 1) * TW]
            b.op("act", lambda e, ysl=ysl, bank=bank: e.mul(
                     ysl, bank[:, 0:TW], scalb[:, 1:2]),
                 reads=[bkey, "b_scalb"], writes=[xbk + "#%d" % o],
                 tag="b_deq%d_%d" % (it, o))
        for o in range(NCH):
            ysl = xb[:, o * TW:(o + 1) * TW]
            b.op("dve", lambda e, ysl=ysl, o=o: e.tensor_scalar(
                     ysl, ysl, bias32[:, o:o + 1], None, OP.add),
                 reads=[xbk + "#%d" % o, "b_bias32"],
                 writes=[xbk + "#%d" % o],
                 tag="b_bias%d_%d" % (it, o))
        emit_ln(p, xb, xbk, TW, banks, stat, "stat", x_bf16=True,
                tag="b_ln2_%d" % it)
        ln_apply_residual(p, xb, xbk, x_t, xk, xo_t, xok, TW, banks,
                          tag="b_res%d" % it)
        store_fm(p, "pool", dst_hbm, dst_key, xo_t, xok, t0_, TW,
                 tag="b_st%d" % it)




def emit_phase_c(p, src, dst):
    src_hbm, src_key = src
    dst_hbm, dst_key = dst
    b = p.b
    ps = [p.sb[f"ps{i}"] for i in range(8)]
    t = [p.sb[f"t{i}"] for i in range(3)]
    stat = p.sb["stat"]
    wbig = p.sb["wbig"]
    ident = p.sb["ident"]
    tri = p.sb["tri"]
    ones_m = p.sb["ones_m"]
    ones_t = p.sb["ones_t"]
    QS = p.dram_tmp("qs", [D, NTOK], F32)
    KS = p.dram_tmp("ks", [D, NTOOK := NTOK], F32)
    VS = p.dram_tmp("vs", [D, NTOK], F32)
    FS = p.dram_tmp("fs", [D, NTOK], F32)
    OS = p.dram_tmp("os", [D, NTOK], F32)
    wq_h = p.dram_in("wq_t", [D, D], F32)
    wk_h = p.dram_in("wk_t", [D, D], F32)
    wv_h = p.dram_in("wv_t", [D, D], F32)
    wf_h = p.dram_in("wf_t", [D, D], F32)
    wo_h = p.dram_in("wo_t", [D, D], F32)
    gw = p.sbuf("c_gw", [128, NCH], F32)
    load_vec(p, "sp", gw[:], "c_gw", p.dram_in("g_w", [128, NCH], F32))
    z128 = p.cval(0.0, 128)
    n1_128 = p.cval(-1.0, 128)
    one128 = p.cval(1.0, 128)
    eps128 = p.cval(EPS, 128)

    TW = min(256, NTOK)
    nt = NTOK // TW
    banks = std_banks(p, TW)
    b.barrier("c1a_bar")
    load_w(p, "sp", wbig[:, 0:NCH * D], "wbig", wq_h, D, D, tag="c_ldwq")
    load_w(p, "sp", wbig[:, NCH * D:2 * NCH * D], "wbig", wk_h, D, D,
           tag="c_ldwk")

    def c1tile(i, it):
        par = it % 2
        return t[i][:, par * NCH * TW:(par + 1) * NCH * TW], "c1_t%d_%d" % (i, par)

    def proj_pass(pref, w_off, act_kind, out_scratch, it, ln_t, lnk, oslot):
        o_t = t[2][:, oslot * NCH * TW:(oslot + 1) * NCH * TW]
        ok_ = "c1_o%d" % oslot
        for o in range(NCH):
            bank = ps[o % 2]
            bkey = "ps%d" % (o % 2)

            def mm(eng, o=o, bank=bank, w_off=w_off, ln_t=ln_t, TW=TW):
                ins = None
                for c in range(NCH):
                    col = w_off + c * D + o * 128
                    ins = eng.matmul(bank[:, 0:TW], wbig[:, col:col + 128],
                                     ln_t[:, c * TW:(c + 1) * TW],
                                     start=(c == 0), stop=(c == NCH - 1))
                return ins

            b.op("pe", mm, reads=["wbig"] + ck(lnk), writes=[bkey],
                 tag="%s_mm%d_%d" % (pref, it, o))
            osl = o_t[:, o * TW:(o + 1) * TW]
            if act_kind == "silu":
                b.op("act", lambda e, osl=osl, bank=bank, TW=TW: e.activation(
                         osl, bank[:, 0:TW], ACT.Silu, bias=z128),
                     reads=[bkey, "cvals"], writes=[ok_ + "#%d" % o],
                     tag="%s_silu%d_%d" % (pref, it, o))
            else:
                ssl = p.sb["sq_t"][:, 2048 + o * TW:2048 + (o + 1) * TW]
                b.op("act", lambda e, ssl=ssl, bank=bank, TW=TW: e.activation(
                         ssl, bank[:, 0:TW], ACT.Exp, bias=z128,
                         scale=n1_128),
                     reads=[bkey, "cvals"], writes=["sq_hi#%d" % o],
                     tag="%s_e%d_%d" % (pref, it, o))
                b.op("act", lambda e, osl=osl, ssl=ssl: e.activation(
                         osl, ssl, ACT.Ln, bias=one128),
                     reads=["sq_hi#%d" % o, "cvals"],
                     writes=[ok_ + "#%d" % o],
                     tag="%s_l%d_%d" % (pref, it, o))
                b.op("act", lambda e, osl=osl: e.mul(osl, osl, -0.125),
                     reads=[ok_ + "#%d" % o], writes=[ok_ + "#%d" % o],
                     tag="%s_s%d_%d" % (pref, it, o))
        store_fm(p, "pool", out_scratch, pref + "_scr", o_t, ok_, it * TW, TW,
                 tag="%s_st%d" % (pref, it))

    for it in range(nt):
        x_t, xk = c1tile(0, it)
        ln_t, lnk = c1tile(1, it)
        load_fm(p, "sp", x_t, xk, src_hbm, it * TW, TW, tag="c1a_ldx%d" % it)
        emit_ln(p, x_t, xk, TW, banks, stat, "stat", tag="c1a_ln%d" % it)
        ln_apply(p, x_t, xk, ln_t, lnk, TW, banks, tag="c1a_lnap%d" % it)
        proj_pass("cq", 0, "silu", QS, it, ln_t, lnk, 0)
        proj_pass("ck", NCH * D, "silu", KS, it, ln_t, lnk, 1)
    b.barrier("c1b_bar")
    load_w(p, "sp", wbig[:, 0:NCH * D], "wbig", wv_h, D, D, tag="c_ldwv")
    load_w(p, "sp", wbig[:, NCH * D:2 * NCH * D], "wbig", wf_h, D, D,
           tag="c_ldwf")
    for it in range(nt):
        x_t, xk = c1tile(0, it)
        ln_t, lnk = c1tile(1, it)
        load_fm(p, "sp", x_t, xk, src_hbm, it * TW, TW, tag="c1b_ldx%d" % it)
        emit_ln(p, x_t, xk, TW, banks, stat, "stat", tag="c1b_ln%d" % it)
        ln_apply(p, x_t, xk, ln_t, lnk, TW, banks, tag="c1b_lnap%d" % it)
        proj_pass("cv", 0, "silu", VS, it, ln_t, lnk, 0)
        proj_pass("cf", NCH * D, "logsig", FS, it, ln_t, lnk, 1)

    b.barrier("c2_bar")
    C = 128

    def sub(i, lo, n):
        return t[i][:, lo:lo + n]

    q2 = sub(0, 0, 256); k2 = sub(0, 256, 256); v2 = sub(0, 512, 256)
    f2 = sub(0, 768, 256); gam = sub(0, 1024, 256); expg = sub(0, 1280, 256)
    expng = sub(0, 1536, 256); s2v = sub(0, 1792, 256)
    ufm = sub(0, 2048, 256); ktm = sub(0, 2304, 256); utm = sub(0, 2560, 256)
    egtm = sub(0, 2816, 256); qvtm = sub(0, 3072, 256); qvfm = sub(0, 3328, 256)
    qtfm = sub(0, 3584, 256); vtm = sub(0, 3840, 256)
    ovtm = sub(1, 0, 256)
    asb = p.sb["c_asb"][:, 0:128]
    a2sb = p.sb["c_asb"][:, 128:256]
    ofm = sub(1, 512, 1024)
    osil = sub(1, 1536, 1024)
    osq = sub(1, 2560, 1024)
    K0 = {"q2": "c_t0", "k2": "c_t0", "v2": "c_t0", "f2": "c_t0"}
    t0k, t1k, t2k = "t0", "t1", "t2"

    def hload(scr, scrk, dstap, dk, h, tt0, tag):
        src2 = scr.rearrange("(c p) t -> p c t", p=128)[:, 2 * h:2 * h + 2,
                                                        tt0:tt0 + C]
        d2 = dstap.rearrange("p (c t) -> p c t", c=2)
        b.dma("sp", d2, src2, reads=[scrk], writes=[dk], tag=tag)

    for s_ in range(2):
        b.op("dve", lambda e: e.memset(t[2][:], 0.0),
             writes=["c_s1_%d" % hh for hh in range(4)]
             + ["c_s2_%d" % hh for hh in range(4)],
             tag="c2_z%d" % s_)
        for c_ in range(SEQ // C):
            tt0 = s_ * SEQ + c_ * C
            for h in range(4):
                S1 = t[2][:, h * 512: h * 512 + 512]
                S2 = t[2][:, 2048 + h * 512: 2048 + h * 512 + 512]
                tg = "c2_%d_%d_%d" % (s_, c_, h)
                hload(QS, "cq_scr", q2, "c_q2", h, tt0, tg + "_lq")
                hload(KS, "ck_scr", k2, "c_k2", h, tt0, tg + "_lk")
                hload(VS, "cv_scr", v2, "c_v2", h, tt0, tg + "_lv")
                hload(FS, "cf_scr", f2, "c_f2", h, tt0, tg + "_lf")
                def scan(eng):
                    ins = None
                    for blk in range(2):
                        ins = eng.tensor_tensor_scan(
                            gam[:, blk * C:(blk + 1) * C],
                            ones_t[:, 0:C], f2[:, blk * C:(blk + 1) * C],
                            0.0, OP.mult, OP.add)
                    return ins
                b.op("dve", scan, reads=["c_f2", "ones_t"],
                     writes=["c_gam"], tag=tg + "_scan")
                b.op("act", lambda e: e.activation(expg[:], gam[:], ACT.Exp,
                                                   bias=z128),
                     reads=["c_gam", "cvals"], writes=["c_eg"],
                     tag=tg + "_eg")
                b.op("act", lambda e: e.activation(expng[:], gam[:], ACT.Exp,
                                                   bias=z128, scale=n1_128),
                     reads=["c_gam", "cvals"], writes=["c_eng"],
                     tag=tg + "_eng")
                b.op("act", lambda e: e.activation(s2v[:], f2[:], ACT.Exp,
                                                   bias=z128),
                     reads=["c_f2", "cvals"], writes=["c_s2v"],
                     tag=tg + "_ef")
                b.op("dve", lambda e: e.scalar_tensor_tensor(
                         s2v[:], s2v[:], -1.0, ones_t[:, 0:256],
                         OP.mult, OP.add),
                     reads=["c_s2v", "ones_t"], writes=["c_s2v"],
                     tag=tg + "_s")
                b.op("dve", lambda e: e.tensor_mul(ufm[:], s2v[:], expng[:]),
                     reads=["c_s2v", "c_eng"], writes=["c_ufm"],
                     tag=tg + "_u")
                def tr1(eng):
                    ins = None
                    for blk in range(2):
                        ins = eng.matmul(ps[2][:, blk * 128:(blk + 1) * 128],
                                         k2[:, blk * 128:(blk + 1) * 128],
                                         ident[:], is_transpose=True,
                                         start=(blk == 0), stop=False)
                    for blk in range(2):
                        ins = eng.matmul(ps[2][:, 256 + blk * 128:
                                               256 + (blk + 1) * 128],
                                         ufm[:, blk * 128:(blk + 1) * 128],
                                         ident[:], is_transpose=True,
                                         start=False, stop=(blk == 1))
                    return ins
                b.op("pe", tr1, reads=["c_k2", "c_ufm", "ident"],
                     writes=["ps2"], tag=tg + "_tr1")
                b.op("dve", lambda e: e.tensor_copy(ktm[:], ps[2][:, 0:256]),
                     reads=["ps2"], writes=["c_ktm"], tag=tg + "_ktmc")
                b.op("dve", lambda e: e.tensor_copy(utm[:], ps[2][:, 256:512]),
                     reads=["ps2"], writes=["c_utm"], tag=tg + "_utmc")
                def tr2(eng):
                    ins = None
                    for blk in range(2):
                        ins = eng.matmul(ps[3][:, blk * 128:(blk + 1) * 128],
                                         expg[:, blk * 128:(blk + 1) * 128],
                                         ident[:], is_transpose=True,
                                         start=(blk == 0), stop=False)
                    for blk in range(2):
                        ins = eng.matmul(ps[3][:, 256 + blk * 128:
                                               256 + (blk + 1) * 128],
                                         v2[:, blk * 128:(blk + 1) * 128],
                                         ident[:], is_transpose=True,
                                         start=False, stop=(blk == 1))
                    return ins
                b.op("pe", tr2, reads=["c_eg", "c_v2", "ident"],
                     writes=["ps3"], tag=tg + "_tr2")
                b.op("dve", lambda e: e.tensor_copy(egtm[:], ps[3][:, 0:256]),
                     reads=["ps3"], writes=["c_egtm"], tag=tg + "_egtmc")
                b.op("dve", lambda e: e.tensor_copy(vtm[:], ps[3][:, 256:512]),
                     reads=["ps3"], writes=["c_vtm"], tag=tg + "_vtmc")
                def mmA(eng):
                    ins = None
                    for blk in range(2):
                        ins = eng.matmul(ps[0][:, 0:128],
                                         k2[:, blk * 128:(blk + 1) * 128],
                                         q2[:, blk * 128:(blk + 1) * 128],
                                         start=(blk == 0), stop=(blk == 1))
                    return ins
                b.op("pe", mmA, reads=["c_k2", "c_q2"], writes=["ps0a"],
                     tag=tg + "_A")
                b.op("dve", lambda e: e.tensor_mul(asb[:], ps[0][:, 0:128],
                                                   tri[:]),
                     reads=["ps0a", "tri"], writes=["c_as"],
                     tag=tg + "_Am")
                def mmok(eng):
                    return eng.matmul(ps[1][:, 0:256], asb[:], utm[:],
                                      start=True, stop=True)
                b.op("pe", mmok, reads=["c_as", "c_utm"],
                     writes=["ps1a"], tag=tg + "_ok")
                def mmoki(eng, S1=S1):
                    ins = None
                    for blk in range(2):
                        ins = eng.matmul(ps[6][:, 0:256],
                                         q2[:, blk * 128:(blk + 1) * 128],
                                         S1[:, blk * 256:(blk + 1) * 256],
                                         start=(blk == 0), stop=(blk == 1))
                    return ins
                b.op("pe", mmoki, reads=["c_q2", "c_s1_%d" % h],
                     writes=["ps6a"], tag=tg + "_oki")
                b.op("dve", lambda e: e.tensor_mul(qvtm[:], ps[1][:, 0:256],
                                                   egtm[:]),
                     reads=["ps1a", "c_egtm"], writes=["c_qvtm"],
                     tag=tg + "_oksc")
                b.op("dve", lambda e: e.tensor_mul(p.sb["c_sc"][:, 0:256],
                                                   ps[6][:, 0:256], egtm[:]),
                     reads=["ps6a", "c_egtm"], writes=["c_sc"],
                     tag=tg + "_oksc2")
                b.op("dve", lambda e: e.tensor_add(qvtm[:], qvtm[:],
                                                   p.sb["c_sc"][:, 0:256]),
                     reads=["c_qvtm", "c_sc"], writes=["c_qvtm"],
                     tag=tg + "_okadd")
                b.op("dve", lambda e: e.tensor_reduce(
                         p.sb["c_mx"][:, 0:1], qvtm[:], AX, OP.max,
                         negate=True),
                     reads=["c_qvtm"], writes=["c_mx"], tag=tg + "_mx")
                b.op("act", lambda e: e.activation(
                         qvtm[:], qvtm[:], ACT.Exp,
                         bias=p.sb["c_mx"][:, 0:1],
                         accum_out=p.sb["c_mx"][:, 1:2]),
                     reads=["c_qvtm", "c_mx"], writes=["c_qvtm", "c_mx"],
                     tag=tg + "_sme")
                b.op("dve", lambda e: e.reciprocal(p.sb["c_mx"][:, 2:3],
                                                   p.sb["c_mx"][:, 1:2]),
                     reads=["c_mx"], writes=["c_mx"], tag=tg + "_smr")
                b.op("dve", lambda e: e.tensor_scalar(
                         qvtm[:], qvtm[:], p.sb["c_mx"][:, 2:3], None,
                         OP.mult),
                     reads=["c_qvtm", "c_mx"], writes=["c_qvtm"],
                     tag=tg + "_smn")
                def tr3(eng):
                    ins = None
                    for blk in range(2):
                        ins = eng.matmul(ps[7][:, blk * 128:(blk + 1) * 128],
                                         qvtm[:, blk * 128:(blk + 1) * 128],
                                         ident[:], is_transpose=True,
                                         start=(blk == 0), stop=(blk == 1))
                    return ins
                b.op("pe", tr3, reads=["c_qvtm", "ident"], writes=["ps7a"],
                     tag=tg + "_tr3")
                b.op("dve", lambda e: e.tensor_copy(qvfm[:], ps[7][:, 0:256]),
                     reads=["ps7a"], writes=["c_qvfm"], tag=tg + "_qvfmc")
                b.op("dve", lambda e: e.tensor_mul(qtfm[:], qvfm[:], expg[:]),
                     reads=["c_qvfm", "c_eg"], writes=["c_qtfm"],
                     tag=tg + "_qt")
                def mmA2(eng):
                    ins = None
                    for blk in range(2):
                        ins = eng.matmul(ps[0][:, 128:256],
                                         ufm[:, blk * 128:(blk + 1) * 128],
                                         qtfm[:, blk * 128:(blk + 1) * 128],
                                         start=(blk == 0), stop=(blk == 1))
                    return ins
                b.op("pe", mmA2, reads=["c_ufm", "c_qtfm"], writes=["ps0b"],
                     tag=tg + "_A2")
                b.op("dve", lambda e: e.tensor_mul(a2sb[:], ps[0][:, 128:256],
                                                   tri[:]),
                     reads=["ps0b", "tri"], writes=["c_a2s"],
                     tag=tg + "_A2m")
                def mmov(eng):
                    return eng.matmul(ps[1][:, 256:512], a2sb[:], vtm[:],
                                      start=True, stop=True)
                b.op("pe", mmov, reads=["c_a2s", "c_vtm"], writes=["ps1b"],
                     tag=tg + "_ov")
                def mmovi(eng, S2=S2):
                    ins = None
                    for blk in range(2):
                        ins = eng.matmul(ps[6][:, 256:512],
                                         qtfm[:, blk * 128:(blk + 1) * 128],
                                         S2[:, blk * 256:(blk + 1) * 256],
                                         start=(blk == 0), stop=(blk == 1))
                    return ins
                b.op("pe", mmovi, reads=["c_qtfm", "c_s2_%d" % h],
                     writes=["ps6b"], tag=tg + "_ovi")
                b.op("dve", lambda e: e.tensor_copy(ovtm[:], ps[1][:, 256:512]),
                     reads=["ps1b"], writes=["c_ovtm"], tag=tg + "_ovc")
                b.op("dve", lambda e: e.tensor_add(ovtm[:], ovtm[:],
                                                   ps[6][:, 256:512]),
                     reads=["ps6b", "c_ovtm"], writes=["c_ovtm"],
                     tag=tg + "_ovc2")
                def mmku(eng):
                    ins = None
                    for blk in range(2):
                        ins = eng.matmul(ps[5][:, blk * 256:(blk + 1) * 256],
                                         ktm[:, blk * 128:(blk + 1) * 128],
                                         utm[:], start=True, stop=True)
                    return ins
                b.op("pe", mmku, reads=["c_ktm", "c_utm"], writes=["ps5"],
                     tag=tg + "_ku")
                b.dma("sp", p.sb["c_fr"][0:1, 0:256], egtm[127:128, 0:256],
                      reads=["c_egtm"], writes=["c_fr"], tag=tg + "_frd")
                b.op("pe", lambda e: e.matmul(ps[0][:, 256:512], ones_m[:],
                                              p.sb["c_fr"][0:1, 0:256],
                                              start=True, stop=True),
                     reads=["ones_m", "c_fr"], writes=["ps0f"],
                     tag=tg + "_fb")
                for blk in range(2):
                    S1b = S1[:, blk * 256:(blk + 1) * 256]
                    b.op("dve", lambda e, S1b=S1b, blk=blk: e.tensor_add(
                             S1b, S1b, ps[5][:, blk * 256:(blk + 1) * 256]),
                         reads=["c_s1_%d" % h, "ps5"],
                         writes=["c_s1_%d" % h],
                         tag=tg + "_s1a%d" % blk)
                    b.op("dve", lambda e, S1b=S1b: e.tensor_mul(
                             S1b, S1b, ps[0][:, 256:512]),
                         reads=["c_s1_%d" % h, "ps0f"],
                         writes=["c_s1_%d" % h],
                         tag=tg + "_s1m%d" % blk)
                def mmku2(eng):
                    ins = None
                    for blk in range(2):
                        ins = eng.matmul(ps[4][:, blk * 256:(blk + 1) * 256],
                                         utm[:, blk * 128:(blk + 1) * 128],
                                         vtm[:], start=True, stop=True)
                    return ins
                b.op("pe", mmku2, reads=["c_utm", "c_vtm"], writes=["ps4"],
                     tag=tg + "_ku2")
                for blk in range(2):
                    S2b = S2[:, blk * 256:(blk + 1) * 256]
                    egcol = expg[:, blk * 128 + 127: blk * 128 + 128]
                    b.op("dve", lambda e, S2b=S2b, blk=blk: e.tensor_add(
                             S2b, S2b, ps[4][:, blk * 256:(blk + 1) * 256]),
                         reads=["c_s2_%d" % h, "ps4"],
                         writes=["c_s2_%d" % h],
                         tag=tg + "_s2a%d" % blk)
                    b.op("dve", lambda e, S2b=S2b, egcol=egcol: e.tensor_scalar(
                             S2b, S2b, egcol, None, OP.mult),
                         reads=["c_s2_%d" % h, "c_eg"],
                         writes=["c_s2_%d" % h],
                         tag=tg + "_s2m%d" % blk)
                def tr4(eng):
                    ins = None
                    for blk in range(2):
                        ins = eng.matmul(ps[7][:, 256 + blk * 128:
                                               256 + (blk + 1) * 128],
                                         ovtm[:, blk * 128:(blk + 1) * 128],
                                         ident[:], is_transpose=True,
                                         start=(blk == 0), stop=(blk == 1))
                    return ins
                b.op("pe", tr4, reads=["c_ovtm", "ident"], writes=["ps7b"],
                     tag=tg + "_tr4")
                b.op("dve", lambda e, h=h: e.tensor_copy(
                         ofm[:, h * 256:(h + 1) * 256], ps[7][:, 256:512]),
                     reads=["ps7b"], writes=["c_ofm#%d" % h],
                     tag=tg + "_ofmc")
            tgc = "c2r_%d_%d" % (s_, c_)
            b.op("act", lambda e: e.activation(osil[:], ofm[:], ACT.Silu,
                                               bias=z128),
                 reads=["c_ofm#%d" % hh for hh in range(4)] + ["cvals"],
                 writes=["c_osil"], tag=tgc + "_sil")
            b.op("act", lambda e: e.square(osq[:], osil[:]),
                 reads=["c_osil"], writes=["c_osq"], tag=tgc + "_sq")
            def mmrs(eng):
                ins = None
                for c2 in range(8):
                    ins = eng.matmul(ps[6][0:1, 0:128], p.sb["ones_k"][:],
                                     osq[:, c2 * 128:(c2 + 1) * 128],
                                     start=(c2 == 0), stop=(c2 == 7))
                return ins
            b.op("pe", mmrs, reads=["c_osq", "ones_k"], writes=["ps6a"],
                 tag=tgc + "_ss")
            rr = stat[0:1, 0:128]
            b.op("act", lambda e: e.mul(rr, ps[6][0:1, 0:128], 1.0 / D),
                 reads=["ps6a"], writes=["stat"], tag=tgc + "_m")
            b.op("act", lambda e: e.activation(rr, rr, ACT.Sqrt,
                                               bias=p.cval(EPS)),
                 reads=["stat", "cvals"], writes=["stat"], tag=tgc + "_sd")
            b.op("dve", lambda e: e.reciprocal(rr, rr),
                 reads=["stat"], writes=["stat"], tag=tgc + "_r")
            b.op("pe", lambda e: e.matmul(ps[7][:, 0:128], ones_m[:], rr,
                                          start=True, stop=True),
                 reads=["stat", "ones_m"], writes=["ps7a"],
                 tag=tgc + "_bc")
            for c2 in range(8):
                sl = osil[:, c2 * 128:(c2 + 1) * 128]
                b.op("dve", lambda e, sl=sl: e.tensor_mul(sl, sl,
                                                          ps[7][:, 0:128]),
                     reads=["c_osil", "ps7a"], writes=["c_osil"],
                     tag=tgc + "_rm%d" % c2)
                b.op("dve", lambda e, sl=sl, c2=c2: e.tensor_scalar(
                         sl, sl, gw[:, c2:c2 + 1], None, OP.mult),
                     reads=["c_osil", "c_gw"], writes=["c_osil"],
                     tag=tgc + "_gw%d" % c2)
            store_fm(p, "pool", OS, "o_scr", osil, "c_osil", tt0, C,
                     tag=tgc + "_st", chunked=False)

    b.barrier("c3_bar")
    TW3 = min(256, NTOK)
    banks3 = std_banks(p, TW3)
    load_w(p, "sp", wbig[:, 0:NCH * D], "wbig", wo_h, D, D, tag="c_ldwo")
    for it in range(NTOK // TW3):
        t0_ = it * TW3
        par = it % 2
        x_t = t[0][:, par * NCH * TW3:(par + 1) * NCH * TW3]
        xk = "c3_x%d" % par
        o_t = t[1][:, par * NCH * TW3:(par + 1) * NCH * TW3]
        ok_ = "c3_o%d" % par
        y_t = t[2][:, par * NCH * TW3:(par + 1) * NCH * TW3]
        yk = "c3_y%d" % par
        xo_t, xok = o_t, ok_
        load_fm(p, "sp", x_t, xk, src_hbm, t0_, TW3, tag="c3_ldx%d" % it)
        load_fm(p, "sp", o_t, ok_, OS, t0_, TW3, tag="c3_ldo%d" % it)
        gb = [ps[2 * par], ps[2 * par + 1]]
        gbk = ["ps%d" % (2 * par), "ps%d" % (2 * par + 1)]
        for o in range(NCH):
            bank = gb[o % 2]
            bkey = gbk[o % 2]

            def mm(eng, o=o, bank=bank, o_t=o_t, TW3=TW3):
                ins = None
                for c in range(NCH):
                    col = c * D + o * 128
                    ins = eng.matmul(bank[:, 0:TW3], wbig[:, col:col + 128],
                                     o_t[:, c * TW3:(c + 1) * TW3],
                                     start=(c == 0), stop=(c == NCH - 1))
                return ins

            b.op("pe", mm, reads=["wbig"] + ck(ok_), writes=[bkey],
                 tag="c3_mm%d_%d" % (it, o))
            b.op("act", lambda e, o=o, bank=bank, y_t=y_t, TW3=TW3: e.copy(
                     y_t[:, o * TW3:(o + 1) * TW3], bank[:, 0:TW3]),
                 reads=[bkey], writes=[yk + "#%d" % o],
                 tag="c3_cp%d_%d" % (it, o))
        emit_ln(p, y_t, yk, TW3, banks3, stat, "stat", tag="c3_ln%d" % it)
        ln_apply_residual(p, y_t, yk, x_t, xk, xo_t, xok, TW3, banks3,
                          tag="c3_res%d" % it)
        store_fm(p, "pool", dst_hbm, dst_key, xo_t, xok, t0_, TW3,
                 tag="c3_st%d" % it)

def emit_phase_d(p, src, dst):
    src_hbm, src_key = src
    dst_hbm, dst_key = dst
    b = p.b
    ps = [p.sb[f"ps{i}"] for i in range(8)]
    t = [p.sb[f"t{i}"] for i in range(3)]
    stat = p.sb["stat"]
    wbig = p.sb["wbig"]
    ones_m = p.sb["ones_m"]
    ident = p.sb["ident"]
    hb = p.sb["hb"]
    lb = p.sb["lb"]
    mb = p.sb["mb"]
    u0 = p.sb["u0"]
    wq16 = p.sb["wq16"]
    HS = p.dram_tmp("hs", [2 * 4096, NTOK], BF16)
    WQF = p.dram_tmp("wqf", [D, 4096], BF16)
    WQP = p.dram_tmp("wqp", [4096, D], BF16)
    wfc_h = p.dram_in("wfc_t", [D, 4096], F32)
    wpr_h = p.dram_in("wpr_t", [4096, D], F32)
    bfc_v = p.sbuf("d_bfcv", [128, 32], F32)
    load_vec(p, "sp", bfc_v[:], "d_bfcv", p.dram_in("b_fc", [128, 32], F32))
    bpr_v = p.sbuf("d_bprv", [128, 8], F32)
    load_vec(p, "sp", bpr_v[:], "d_bprv", p.dram_in("b_pr", [128, 8], F32))
    sqt = p.sb["sq_t"]
    afc = t[1][:, 1152:1152 + NCH * 16]
    bfc_h = p.dram_in("bfc_t", [16, 4096], F32)
    bfc_sl = sqt[0:16, 2560:3072]
    apr = t[1][:, 640:640 + 32 * 16]
    bpr_h = p.dram_in("bpr_t", [16, D], F32)
    bpr_sl = p.sbuf("d_bprs", [16, 256], F32)[:]
    z_sb = p.sbuf("d_z2", [16, 512], F32)[:]
    dsc = p.sbuf("d_sc", [1, 8], F32)
    dscb = p.sbuf("d_scb", [128, 4], F32)
    mx1 = p.sbuf("d_mx", [128, 2], F32)
    rowt = sqt[0:1, 3328:3456]
    MAGIC = float(3 * 2 ** 22)
    cmag = p.cval(MAGIC, 128)
    c127 = p.cval(127.0, 128)
    cn128 = p.cval(-128.0, 128)

    b.barrier("d0_bar")
    for wi, (wh, wq_scr, wq_key, rows, cols, sc_i) in enumerate([
            (wfc_h, WQF, "wqf", D, 4096, 0),
            (wpr_h, WQP, "wqp", 4096, D, 2)]):
        nld = (rows // 128) * cols // 16384
        b.op("dve", lambda e: e.memset(mx1[:, 0:1], 0.0), writes=["d_mx"],
             tag="d0_z%d" % wi)
        ncol_per = 16384 // (rows // 128)
        for j in range(nld):
            load_w(p, "sp", wbig[:], "wbig", wh, rows, cols,
                   col0=j * ncol_per, ncols=ncol_per, tag="d0_ld%d_%d" % (wi, j))
            b.op("dve", lambda e: e.tensor_reduce(mx1[:, 1:2], wbig[:], AX,
                                                  OP.max,
                                                  apply_absolute_value=True),
                 reads=["wbig"], writes=["d_mx"], tag="d0_mx%d_%d" % (wi, j))
            b.op("dve", lambda e: e.tensor_max(mx1[:, 0:1], mx1[:, 0:1],
                                               mx1[:, 1:2]),
                 reads=["d_mx"], writes=["d_mx"], tag="d0_acc%d_%d" % (wi, j))
        b.op("pe", lambda e: e.matmul(ps[3][0:1, 0:128], mx1[:, 0:1], ident[:],
                                      is_transpose=True, start=True, stop=True),
             reads=["d_mx", "ident"], writes=["ps3"], tag="d0_tr%d" % wi)
        b.op("dve", lambda e: e.tensor_copy(rowt, ps[3][0:1, 0:128]),
             reads=["ps3"], writes=["d_row"], tag="d0_rc%d" % wi)
        b.op("dve", lambda e, sc_i=sc_i: e.tensor_reduce(
                 dsc[0:1, sc_i:sc_i + 1], rowt, AX, OP.max,
                 apply_absolute_value=True),
             reads=["d_row"], writes=["d_sc"], tag="d0_rm%d" % wi)
        b.op("act", lambda e, sc_i=sc_i: e.mul(dsc[0:1, sc_i:sc_i + 1],
                                               dsc[0:1, sc_i:sc_i + 1],
                                               1.0 / 127.0),
             reads=["d_sc"], writes=["d_sc"], tag="d0_s%d" % wi)
        b.op("dve", lambda e, sc_i=sc_i: e.reciprocal(
                 dsc[0:1, sc_i + 1:sc_i + 2], dsc[0:1, sc_i:sc_i + 1]),
             reads=["d_sc"], writes=["d_sc"], tag="d0_r%d" % wi)
    b.op("pe", lambda e: e.matmul(ps[3][:, 0:4], ones_m[:], dsc[0:1, 0:4],
                                  start=True, stop=True),
         reads=["d_sc", "ones_m"], writes=["ps3"], tag="d0_bc")
    b.op("dve", lambda e: e.tensor_copy(dscb[:], ps[3][:, 0:4]),
         reads=["ps3"], writes=["d_scb"], tag="d0_bcc")
    for wi, (wh, wq_scr, wq_key, rows, cols, sc_i) in enumerate([
            (wfc_h, WQF, "wqf", D, 4096, 0),
            (wpr_h, WQP, "wqp", 4096, D, 2)]):
        nld = (rows // 128) * cols // 16384
        ncol_per = 16384 // (rows // 128)
        for j in range(nld):
            load_w(p, "sp", wbig[:], "wbig", wh, rows, cols,
                   col0=j * ncol_per, ncols=ncol_per, tag="d0q_ld%d_%d" % (wi, j))
            b.op("dve", lambda e, sc_i=sc_i: e.tensor_scalar(
                     wbig[:], wbig[:], dscb[:, sc_i + 1:sc_i + 2], None,
                     OP.mult),
                 reads=["wbig", "d_scb"], writes=["wbig"],
                 tag="d0q_m%d_%d" % (wi, j))
            b.op("dve", lambda e: e.tensor_scalar(wbig[:], wbig[:], cmag,
                                                  None, OP.add),
                 reads=["wbig", "cvals"], writes=["wbig"],
                 tag="d0q_a%d_%d" % (wi, j))
            b.op("dve", lambda e: e.tensor_scalar(wbig[:], wbig[:], cmag,
                                                  None, OP.subtract),
                 reads=["wbig", "cvals"], writes=["wbig"],
                 tag="d0q_s%d_%d" % (wi, j))
            b.op("dve", lambda e: e.tensor_scalar(wbig[:], wbig[:], c127,
                                                  None, OP.min),
                 reads=["wbig", "cvals"], writes=["wbig"],
                 tag="d0q_c1%d_%d" % (wi, j))
            b.op("dve", lambda e: e.tensor_scalar(wbig[:], wbig[:], cn128,
                                                  None, OP.max),
                 reads=["wbig", "cvals"], writes=["wbig"],
                 tag="d0q_c2%d_%d" % (wi, j))
            nchj = rows // 256
            for hf in range(2):
                sl = wbig[:, hf * 8192:(hf + 1) * 8192]
                w16 = p.sb["d0w16"]
                b.op("dve", lambda e, sl=sl, w16=w16: e.tensor_copy(w16[:], sl),
                     reads=["wbig"], writes=["d0_w16"],
                     tag="d0q_cv%d_%d_%d" % (wi, j, hf))
                dstw = wq_scr.rearrange("(c p) o -> p c o", p=128)[
                    :, hf * nchj:(hf + 1) * nchj,
                    j * ncol_per:(j + 1) * ncol_per]
                srcw = w16[:].rearrange("p (c o) -> p c o", c=nchj)
                b.dma("pool", dstw, srcw, reads=["d0_w16"],
                      writes=["%s#%d_%d" % (wq_key, j, hf)],
                      tag="d0q_st%d_%d_%d" % (wi, j, hf))
    b.barrier("d0e_bar")
    load_w(p, "sp", afc, "d_afc", p.dram_in("afc_t", [D, 16], F32), D, 16)
    load_w(p, "sp", apr, "d_apr", p.dram_in("apr_t", [4096, 16], F32),
           4096, 16)
    afcs = t[1][:, 0:NCH * 16]
    afch = mb[:, 0:NCH * 16]
    afcl = mb[:, NCH * 16:2 * NCH * 16]
    b.op("dve", lambda e: e.tensor_scalar(afcs, afc, dscb[:, 1:2],
                                          None, OP.mult),
         reads=["d_afc", "d_scb"], writes=["d0_afcs"], tag="d0_afs")
    b.op("dve", lambda e: e.tensor_copy(afch, afcs),
         reads=["d0_afcs"], writes=["d_afch"], tag="d0_afh")
    b.op("dve", lambda e: e.tensor_tensor(afcl, afcs, afch,
                                          OP.subtract),
         reads=["d0_afcs", "d_afch"], writes=["d_afcl"], tag="d0_afl")
    aprs = t[1][:, 128:128 + 32 * 16]
    aprh = p.sbuf("d_aprh", [128, 32 * 16], BF16)
    aprl = p.sbuf("d_aprl", [128, 32 * 16], BF16)
    
    b.op("dve", lambda e: e.tensor_scalar(aprs, apr, dscb[:, 3:4],
                                          None, OP.mult),
         reads=["d_apr", "d_scb"], writes=["d0_aprs"], tag="d0_aps")
    b.op("dve", lambda e: e.tensor_copy(aprh[:], aprs),
         reads=["d0_aprs"], writes=["d_aprh"], tag="d0_aph")
    b.op("dve", lambda e: e.tensor_tensor(aprl[:], aprs, aprh[:],
                                          OP.subtract),
         reads=["d0_aprs", "d_aprh"], writes=["d_aprl"], tag="d0_apl")

    b.barrier("d1_bar")
    TW = 512
    nt = NTOK // TW
    banks = std_banks(p, TW)
    x_t, xk = t[0][:, :NCH * TW], "d1_x"
    HSr = HS.rearrange("(c p) t -> p c t", p=128)
    hbs = [hb, p.sb["hbB"]]
    lbs = [lb, p.sb["lbB"]]
    for it in range(nt):
        t0_ = it * TW
        itp = it % 2
        hbt, lbt = hbs[itp], lbs[itp]
        hikeys = ["d1_hi%d#%d" % (itp, c) for c in range(NCH)]
        lokeys = ["d1_lo%d#%d" % (itp, c) for c in range(NCH)]
        load_fm(p, "sp", x_t, xk, src_hbm, t0_, TW, tag="d1_ldx%d" % it)
        emit_ln(p, x_t, xk, TW, banks, stat, "stat", tag="d1_ln%d" % it,
                scale_ap=dsc[0:1, 0:1], scale_key="d_sc",
                sq=t[1][:], sqk="d1_t32")
        for c in range(NCH):
            xs_ = x_t[:, c * TW:(c + 1) * TW]
            tmp = t[1][:, c * TW:(c + 1) * TW]
            hic = hbt[:, c * TW:(c + 1) * TW]
            loc = lbt[:, c * TW:(c + 1) * TW]
            tk = "d1_t32#%d" % c
            b.op("dve", lambda e, tmp=tmp, xs_=xs_: e.tensor_mul(
                     tmp, xs_, banks["br"]),
                 reads=[xk + "#%d" % c, banks["kr"]], writes=[tk],
                 tag="d1_sp%d_m%d" % (it, c))
            b.op("dve", lambda e, tmp=tmp: e.tensor_sub(tmp, tmp,
                                                        banks["bm"]),
                 reads=[tk, banks["km"]], writes=[tk],
                 tag="d1_sp%d_s%d" % (it, c))
            b.op("dve", lambda e, hic=hic, tmp=tmp: e.tensor_copy(hic, tmp),
                 reads=[tk], writes=[hikeys[c]],
                 tag="d1_sp%d_h%d" % (it, c))
            b.op("dve", lambda e, loc=loc, tmp=tmp, hic=hic: e.tensor_tensor(
                     loc, tmp, hic, OP.subtract),
                 reads=[tk, hikeys[c]], writes=[lokeys[c]],
                 tag="d1_sp%d_l%d" % (it, c))
        def mmz(eng, TW=TW, hbt=hbt, lbt=lbt):
            ins = None
            for c in range(NCH):
                ins = eng.matmul(ps[3][0:16, 0:TW], afch[:, c * 16:(c + 1) * 16],
                                 hbt[:, c * TW:(c + 1) * TW],
                                 start=(c == 0), stop=False)
            for c in range(NCH):
                ins = eng.matmul(ps[3][0:16, 0:TW], afch[:, c * 16:(c + 1) * 16],
                                 lbt[:, c * TW:(c + 1) * TW],
                                 start=False, stop=False)
            for c in range(NCH):
                ins = eng.matmul(ps[3][0:16, 0:TW],
                                 afcl[:, c * 16:(c + 1) * 16],
                                 hbt[:, c * TW:(c + 1) * TW],
                                 start=False, stop=(c == NCH - 1))
            return ins
        b.op("pe", mmz, reads=["d_afch", "d_afcl"] + hikeys + lokeys,
             writes=["ps3"], tag="d1_z%d" % it)
        b.op("act", lambda e: e.mul(z_sb[:, 0:TW], ps[3][0:16, 0:TW], 2.0),
             reads=["ps3"], writes=["d_z"], tag="d1_zc%d" % it)
        for o in range(32):
            slot = o % 8
            wreg = wq16[:, slot * 1024:slot * 1024 + 1024]
            wkey = "wq16#%d" % slot
            load_w(p, "sp", wreg, wkey, WQF, D, 4096, col0=o * 128, ncols=128,
                   tag="d1_lw%d_%d" % (it, o))
            s4 = o % 4
            b.dma("sp", bfc_sl[:, s4 * 128:(s4 + 1) * 128],
                  bfc_h[:, o * 128:(o + 1) * 128], reads=["d_bfc#%d" % s4],
                  writes=["d_bfc#%d" % s4], tag="d1_lb%d_%d" % (it, o))
            bank = ps[o % 3]
            bkey = "ps%d" % (o % 3)

            def mm(eng, o=o, bank=bank, wreg=wreg, TW=TW, hbt=hbt, lbt=lbt):
                ins = None
                for c in range(NCH):
                    ins = eng.matmul(bank[:, 0:TW],
                                     wreg[:, c * 128:(c + 1) * 128],
                                     hbt[:, c * TW:(c + 1) * TW],
                                     start=(c == 0), stop=False)
                    ins = eng.matmul(bank[:, 0:TW],
                                     wreg[:, c * 128:(c + 1) * 128],
                                     lbt[:, c * TW:(c + 1) * TW],
                                     start=False, stop=False)
                s4 = o % 4
                ins = eng.matmul(bank[:, 0:TW],
                                 bfc_sl[:, s4 * 128:(s4 + 1) * 128],
                                 z_sb[0:16, 0:TW], start=False, stop=True)
                return ins

            b.op("pe", mm,
                 reads=[wkey, "d_bfc#%d" % (o % 4), "d_z"] + hikeys + lokeys,
                 writes=[bkey], tag="d1_mm%d_%d" % (it, o))
            par = o % 4
            h_t = t[2][:, par * 512:par * 512 + 512]
            hk = "d1_h%d" % par
            b.op("act", lambda e, bank=bank, h_t=h_t, o=o: e.activation(
                     h_t, bank[:, 0:TW], ACT.Gelu,
                     bias=bfc_v[:, o:o + 1]),
                 reads=[bkey, "d_bfcv"], writes=[hk], tag="d1_g%d_%d" % (it, o))
            hs32 = t[2][:, 2048 + par * 512:2048 + par * 512 + 512]
            hsk = "d1_hs%d" % par
            b.op("dve", lambda e, hs32=hs32, h_t=h_t: e.tensor_scalar(
                     hs32, h_t, dscb[:, 2:3], None, OP.mult),
                 reads=[hk, "d_scb"], writes=[hsk],
                 tag="d1_hm%d_%d" % (it, o))
            hi8 = u0[:, par * 1024:par * 1024 + 512]
            lo8 = u0[:, par * 1024 + 512:par * 1024 + 1024]
            h8k = "d1_h8%d" % par
            l8k = "d1_l8%d" % par
            b.op("dve", lambda e, hi8=hi8, hs32=hs32: e.tensor_copy(hi8, hs32),
                 reads=[hsk], writes=[h8k], tag="d1_hh%d_%d" % (it, o))
            b.op("dve", lambda e, lo8=lo8, hs32=hs32, hi8=hi8: e.tensor_tensor(
                     lo8, hs32, hi8, OP.subtract),
                 reads=[hsk, h8k], writes=[l8k], tag="d1_hl%d_%d" % (it, o))
            b.dma("pool", HSr[:, o:o + 1, t0_:t0_ + TW],
                  hi8.rearrange("p (c t) -> p c t", c=1),
                  reads=[h8k], writes=["hs_%d_%d_h" % (it, o)],
                  tag="d1_sth%d_%d" % (it, o))
            b.dma("pool", HSr[:, 32 + o:33 + o, t0_:t0_ + TW],
                  lo8.rearrange("p (c t) -> p c t", c=1),
                  reads=[l8k], writes=["hs_%d_%d_l" % (it, o)],
                  tag="d1_stl%d_%d" % (it, o))

    b.barrier("d2_bar")
    TW2 = 256
    nt2 = NTOK // TW2
    banks2 = std_banks(p, TW2)
    wq16d = p.sb["wq16d"]
    d2hs = [p.sb["d2h0"], p.sb["d2h1"]]
    d2ls = [p.sb["d2l0"], p.sb["d2l1"]]
    for it in range(nt2):
        t0_ = it * TW2
        itp = it % 2
        d2h, d2l = d2hs[itp], d2ls[itp]
        x2_t = t[2][:, itp * NCH * TW2:(itp + 1) * NCH * TW2]
        x2k = "d2_x%d" % itp
        y2_t = t[1][:, itp * NCH * TW2:(itp + 1) * NCH * TW2]
        y2k = "d2_y%d" % itp
        xo2_t, xo2k = y2_t, y2k
        hikeys2 = ["d2_hi%d#%d" % (itp, c) for c in range(32)]
        lokeys2 = ["d2_lo%d#%d" % (itp, c) for c in range(32)]
        b.dma("sp", d2h[:, 0:32 * TW2].rearrange("p (c t) -> p c t", c=32),
              HSr[:, 0:32, t0_:t0_ + TW2],
              reads=[], writes=hikeys2, tag="d2_lh_%d" % it)
        b.dma("sp", d2l[:, 0:32 * TW2].rearrange("p (c t) -> p c t", c=32),
              HSr[:, 32:64, t0_:t0_ + TW2],
              reads=[], writes=lokeys2, tag="d2_ll_%d" % it)
        load_fm(p, "sp", x2_t, x2k, src_hbm, t0_, TW2, tag="d2_ldx%d" % it)
        def mmz2(eng, TW2=TW2, d2h=d2h, d2l=d2l):
            ins = None
            for c in range(32):
                ins = eng.matmul(ps[3][0:16, 0:TW2],
                                 aprh[:, c * 16:(c + 1) * 16],
                                 d2h[:, c * TW2:(c + 1) * TW2],
                                 start=(c == 0), stop=False)
            for c in range(32):
                ins = eng.matmul(ps[3][0:16, 0:TW2],
                                 aprh[:, c * 16:(c + 1) * 16],
                                 d2l[:, c * TW2:(c + 1) * TW2],
                                 start=False, stop=False)
            for c in range(32):
                ins = eng.matmul(ps[3][0:16, 0:TW2],
                                 aprl[:, c * 16:(c + 1) * 16],
                                 d2h[:, c * TW2:(c + 1) * TW2],
                                 start=False, stop=(c == 31))
            return ins
        b.op("pe", mmz2, reads=["d_aprh", "d_aprl"] + hikeys2 + lokeys2,
             writes=["ps3"], tag="d2_z%d" % it)
        b.op("act", lambda e: e.mul(z_sb[:, 0:TW2], ps[3][0:16, 0:TW2], 2.0),
             reads=["ps3"], writes=["d_z"], tag="d2_zc%d" % it)
        for o in range(NCH):
            slot = o % 2
            wreg = wq16d[:, slot * 4096:(slot + 1) * 4096]
            wkey2 = "wq16d#%d" % slot
            load_w(p, "sp", wreg, wkey2, WQP, 4096, D, col0=o * 128,
                   ncols=128, tag="d2_lw%d_%d" % (it, o))
            s2_ = o % 2
            b.dma("sp", bpr_sl[:, s2_ * 128:(s2_ + 1) * 128],
                  bpr_h[:, o * 128:(o + 1) * 128], reads=["d_bpr#%d" % s2_],
                  writes=["d_bpr#%d" % s2_], tag="d2_lb%d_%d" % (it, o))
            bank = ps[o % 3]
            bkey = "ps%d" % (o % 3)

            def mm(eng, o=o, bank=bank, wreg=wreg, TW2=TW2, d2h=d2h, d2l=d2l):
                ins = None
                for c in range(32):
                    ins = eng.matmul(bank[:, 0:TW2],
                                     wreg[:, c * 128:(c + 1) * 128],
                                     d2h[:, c * TW2:(c + 1) * TW2],
                                     start=(c == 0), stop=False)
                    ins = eng.matmul(bank[:, 0:TW2],
                                     wreg[:, c * 128:(c + 1) * 128],
                                     d2l[:, c * TW2:(c + 1) * TW2],
                                     start=False, stop=False)
                s2_ = o % 2
                ins = eng.matmul(bank[:, 0:TW2],
                                 bpr_sl[:, s2_ * 128:(s2_ + 1) * 128],
                                 z_sb[0:16, 0:TW2], start=False, stop=True)
                return ins

            b.op("pe", mm,
                 reads=[wkey2, "d_bpr#%d" % (o % 2), "d_z"]
                 + hikeys2 + lokeys2,
                 writes=[bkey], tag="d2_mm%d_%d" % (it, o))
            b.op("act", lambda e, o=o, bank=bank, y2_t=y2_t, TW2=TW2:
                 e.activation(
                     y2_t[:, o * TW2:(o + 1) * TW2], bank[:, 0:TW2],
                     ACT.Identity, bias=bpr_v[:, o:o + 1]),
                 reads=[bkey, "d_bprv"], writes=[y2k + "#%d" % o],
                 tag="d2_y%d_%d" % (it, o))
        emit_ln(p, y2_t, y2k, TW2, banks2, stat, "stat",
                tag="d2_ln%d" % it)
        ln_apply_residual(p, y2_t, y2k, x2_t, x2k, xo2_t, xo2k, TW2, banks2,
                          tag="d2_res%d" % it)
        store_fm(p, "pool", dst_hbm, dst_key + "#st%d" % it, xo2_t, xo2k, t0_,
                 TW2, tag="d2_st%d" % it)


# version 14 (final)
# speedup vs baseline: 1.5389x; 1.0079x over previous
import numpy as np
from contextlib import ExitStack

import concourse.bass as bass
import concourse.mybir as mybir
from concourse.bass_utils import run_bass_kernel_spmd

F32 = mybir.dt.float32
BF16 = mybir.dt.bfloat16
FP8 = mybir.dt.float8e4
AX = mybir.AxisListType.X
OP = mybir.AluOpType
ACT = mybir.ActivationFunctionType

D = 1024
NTOK = 8192
SEQ = 4096
NCH = 8
EPS = 1e-5

_ENG_ATTR = {"sp": "sync", "pe": "tensor", "act": "scalar", "dve": "vector",
             "pool": "gpsimd"}


class _Op:
    __slots__ = ("eng", "fn", "reads", "writes", "dma", "deps", "need_inc",
                 "inc_idx", "idx", "tag", "slot", "use")

    def __init__(self, idx, eng, fn, reads, writes, dma, tag):
        self.idx = idx
        self.eng = eng
        self.fn = fn
        self.reads = reads
        self.writes = writes
        self.dma = dma
        self.deps = set()
        self.need_inc = dma
        self.inc_idx = None
        self.tag = tag


class Build:
    def __init__(self, nc, stack):
        self.nc = nc
        self.stack = stack
        self.ops = []
        self._last_w = {}
        self._readers = {}
        self._bar = None

    def op(self, eng, fn, reads=(), writes=(), dma=False, tag=""):
        o = _Op(len(self.ops), eng, fn, tuple(reads), tuple(writes), dma, tag)
        if self._bar is not None:
            o.deps.add(self._bar)
        for buf in o.reads:
            wi = self._last_w.get(buf)
            if wi is not None:
                o.deps.add(wi)
            self._readers.setdefault(buf, []).append(o.idx)
        for buf in o.writes:
            wi = self._last_w.get(buf)
            if wi is not None:
                o.deps.add(wi)
            for r in self._readers.get(buf, ()):
                o.deps.add(r)
            self._readers[buf] = []
            self._last_w[buf] = o.idx
        o.deps.discard(o.idx)
        self.ops.append(o)
        return o

    def dma(self, queue, out, in_, reads, writes, tag=""):
        def fn(eng):
            return eng.dma_start(out=out, in_=in_)
        return self.op(queue, fn, reads=reads, writes=writes, dma=True, tag=tag)

    def barrier(self, tag="bar"):
        keys = sorted(set(self._last_w) | set(self._readers))
        o = self.op("dve", lambda e: e.engine_nop(), reads=keys,
                    writes=keys, tag=tag)
        self._bar = o.idx
        return o

    def finalize(self):
        nc = self.nc
        ops = self.ops
        for o in ops:
            for dd in o.deps:
                a = ops[dd]
                if a.dma:
                    continue
                if a.eng != o.eng:
                    a.need_inc = True
        NSLOT = {"sp": 12, "pool": 8, "act": 4, "dve": 4, "pe": 4}
        dcount = {e: 0 for e in _ENG_ATTR}
        csem_n = {e: 0 for e in _ENG_ATTR}
        slot_use = {}
        for o in ops:
            if o.dma:
                i = dcount[o.eng]
                dcount[o.eng] += 1
                o.slot = (o.eng, i % NSLOT[o.eng])
                o.use = i // NSLOT[o.eng] + 1
                slot_use[o.slot] = o.use
            elif o.need_inc:
                csem_n[o.eng] += 1
                o.inc_idx = csem_n[o.eng]
        csem, dsem = {}, {}
        for e in _ENG_ATTR:
            if csem_n[e]:
                csem[e] = self.stack.enter_context(nc.semaphore(f"c_{e}"))
        for sl in slot_use:
            dsem[sl] = self.stack.enter_context(
                nc.semaphore(f"d_{sl[0]}{sl[1]}"))
        per_eng = {e: [] for e in _ENG_ATTR}
        for o in ops:
            per_eng[o.eng].append(o)
        block = self.stack.enter_context(nc.Block())

        def emit(eng_name, eng_obj):
            water = {}
            covered = -1
            prev_idx = -1

            def wait(key, val):
                if water.get(key, 0) >= val:
                    return
                sem = dsem[key[1]] if key[0] == "d" else csem[key[1]]
                eng_obj.wait_ge(sem, val)
                water[key] = val

            for o in per_eng[eng_name]:
                req = {}
                need_drain = False
                for dd in sorted(o.deps):
                    a = ops[dd]
                    if a.eng == o.eng and not a.dma:
                        if dd > covered and o.eng != "pe":
                            need_drain = True
                        continue
                    key = ("d", a.slot) if a.dma else ("c", a.eng)
                    val = 16 * a.use if a.dma else a.inc_idx
                    if req.get(key, 0) < val:
                        req[key] = val
                if o.dma and o.use >= 2:
                    k = ("d", o.slot)
                    if req.get(k, 0) < 16 * (o.use - 1):
                        req[k] = max(req.get(k, 0), 16 * (o.use - 1))
                if need_drain:
                    eng_obj.drain()
                    covered = prev_idx
                for key, val in sorted(req.items(), key=str):
                    wait(key, val)
                ins = o.fn(eng_obj)
                prev_idx = o.idx
                if o.dma:
                    ins.then_inc(dsem[o.slot], 16)
                elif o.need_inc:
                    assert ins is not None, f"op {o.tag} needs inc, returned None"
                    ins.then_inc(csem[o.eng], 1)
            return water

        @block.sync
        def _(sync):
            water = emit("sp", sync)
            for sl, use in sorted(slot_use.items(), key=str):
                if water.get(("d", sl), 0) < 16 * use:
                    sync.wait_ge(dsem[sl], 16 * use)
            for e in sorted(_ENG_ATTR):
                n = csem_n[e]
                if e != "sp" and n and water.get(("c", e), 0) < n:
                    sync.wait_ge(csem[e], n)

        @block.tensor
        def _(tensor):
            emit("pe", tensor)

        @block.scalar
        def _(scalar):
            emit("act", scalar)

        @block.vector
        def _(vector):
            emit("dve", vector)

        @block.gpsimd
        def _(gpsimd):
            emit("pool", gpsimd)



class Prog:
    def __init__(self):
        self.nc = bass.Bass("TRN2", target_bir_lowering=False, debug=False)
        self.stack = ExitStack()
        self.b = Build(self.nc, self.stack)
        self.sb = {}

    def sbuf(self, name, shape, dtype=F32):
        t = self.stack.enter_context(self.nc.sbuf_tensor(name, list(shape), dtype))
        self.sb[name] = t
        return t

    def sbuf_alias(self, name, shape, dtype, offset):
        t = self.nc.alloc_sbuf_tensor_at(name, list(shape), dtype,
                                         offset=offset)
        self.sb[name] = t
        return t

    def psum(self, name, shape=(128, 512)):
        t = self.stack.enter_context(self.nc.psum_tensor(name, list(shape), F32))
        self.sb[name] = t
        return t

    def dram_in(self, name, shape, dtype=F32):
        return self.nc.dram_tensor(name, list(shape), dtype, kind="ExternalInput").ap()

    def dram_out(self, name, shape, dtype=F32):
        return self.nc.dram_tensor(name, list(shape), dtype, kind="ExternalOutput").ap()

    def dram_tmp(self, name, shape, dtype=F32):
        return self.nc.dram_tensor(name, list(shape), dtype, kind="Internal").ap()

    def cval(self, value, parts=1):
        if not hasattr(self, "_cvals"):
            self._cvals = {}
            self._cval_tile = self.sbuf("cvals", [128, 32], F32)
        if value not in self._cvals:
            j = len(self._cvals)
            assert j < 32
            self._cvals[value] = j
            sl = self._cval_tile[:, j:j + 1]
            self.b.op("dve", lambda e, sl=sl, v=value: e.memset(sl, v),
                      writes=["cvals"], tag="cval")
        j = self._cvals[value]
        return self._cval_tile[0:parts, j:j + 1]


def setup_consts(p):
    b = p.b
    ones_k = p.sbuf("ones_k", [128, 1], F32)
    ones_k16 = p.sbuf("ones_k16", [128, 1], BF16)
    ones_m = p.sbuf("ones_m", [1, 128], F32)
    ident = p.sbuf("ident", [128, 128], F32)
    tri = p.sbuf("tri", [128, 128], F32)
    iot = p.sbuf("iot", [128, 128], F32)
    b.op("dve", lambda e: e.memset(ones_k[:], 1.0), writes=["ones_k"])
    b.op("dve", lambda e: e.memset(ones_k16[:], 1.0), writes=["ones_k16"])
    b.op("dve", lambda e: e.memset(ones_m[:], 1.0), writes=["ones_m"])
    b.op("dve", lambda e: e.memset(p.sb["ones_t"][:], 1.0), writes=["ones_t"])
    b.op("pool", lambda e: e.iota(iot[:], pattern=[[-1, 128]], base=0,
                                  channel_multiplier=1,
                                  allow_small_or_imprecise_dtypes=True),
         writes=["iot"], tag="iota_ident")
    b.op("dve", lambda e: e.tensor_single_scalar(ident[:], iot[:], 0.0,
                                                 OP.is_equal),
         reads=["iot"], writes=["ident"], tag="ident")
    b.op("pool", lambda e: e.iota(iot[:], pattern=[[1, 128]], base=0,
                                  channel_multiplier=-1,
                                  allow_small_or_imprecise_dtypes=True),
         reads=["ident"], writes=["iot"], tag="iota_tri")
    b.op("dve", lambda e: e.tensor_single_scalar(tri[:], iot[:], 0.0,
                                                 OP.is_ge),
         reads=["iot"], writes=["tri"], tag="tri")



def ck(key, nch=NCH):
    return [key + "#%d" % c for c in range(nch)]


def load_fm(p, queue, dst, dst_key, src_hbm, t0, tw, nch=NCH, tag="ld",
            chunked=True):
    src = src_hbm.rearrange("(c p) t -> p c t", p=128)[:, :, t0:t0 + tw]
    d2 = dst.rearrange("p (c t) -> p c t", c=nch)
    wk = ck(dst_key, nch) if chunked else [dst_key]
    p.b.dma(queue, d2, src, reads=[], writes=wk, tag=tag)


def store_fm(p, queue, dst_hbm, dst_key, src, src_key, t0, tw, nch=NCH,
             tag="st", chunked=True):
    dst = dst_hbm.rearrange("(c p) t -> p c t", p=128)[:, :, t0:t0 + tw]
    s2 = src.rearrange("p (c t) -> p c t", c=nch)
    rk = ck(src_key, nch) if chunked else [src_key]
    p.b.dma(queue, dst, s2, reads=rk, writes=[dst_key], tag=tag)


def load_vec(p, queue, dst, dst_key, src_hbm, tag="ldv"):
    p.b.dma(queue, dst, src_hbm[:], reads=[], writes=[dst_key], tag=tag)


def load_w(p, queue, dst, dst_key, src_hbm, rows, cols, col0=0, ncols=None,
           tag="ldw"):
    if ncols is None:
        ncols = cols
    src = src_hbm[:, col0:col0 + ncols].rearrange("(c p) o -> p c o", p=128)
    d2 = dst.rearrange("p (c o) -> p c o", c=rows // 128)
    p.b.dma(queue, d2, src, reads=[], writes=[dst_key], tag=tag)



def emit_ln(p, x, xk, tw, banks, stat, statk, *, nch=NCH, nfeat=D,
            x_bf16=False, scale_ap=None, scale_key=None, sq=None, sqk=None,
            sq_slots=None, tag="ln"):
    b = p.b
    ones_k = p.sb["ones_k16"] if x_bf16 else p.sb["ones_k"]
    ones_kk = "ones_k16" if x_bf16 else "ones_k"
    ones_f = p.sb["ones_k"]
    ones_m = p.sb["ones_m"]
    if sq is None:
        sq = p.sb["sq_t"][:]
        sqk = "sq_t"

    def mm_s(eng):
        ins = None
        for c in range(nch):
            ins = eng.matmul(banks["sa"], ones_k[:], x[:, c * tw:(c + 1) * tw],
                             start=(c == 0), stop=(c == nch - 1))
        return ins

    b.op("pe", mm_s, reads=ck(xk, nch) + [ones_kk], writes=[banks["ka"]],
         tag=tag + "_ss")
    if sq_slots is None:
        sqks = [sqk] if sqk == "sq_t" else ck(sqk, nch)
        b.op("act", lambda e: e.square(sq[:, :nch * tw], x[:, :nch * tw]),
             reads=ck(xk, nch), writes=sqks, tag=tag + "_sq")

        def mm_q(eng):
            ins = None
            for c in range(nch):
                ins = eng.matmul(banks["sb"], ones_f[:],
                                 sq[:, c * tw:(c + 1) * tw],
                                 start=(c == 0), stop=(c == nch - 1))
            return ins

        b.op("pe", mm_q, reads=sqks + ["ones_k"], writes=[banks["kb"]],
             tag=tag + "_sq2")
    else:
        nsl = sq_slots
        for c in range(nch):
            sl = sq[:, (c % nsl) * tw:(c % nsl + 1) * tw]
            b.op("act", lambda e, sl=sl, c=c: e.square(
                     sl, x[:, c * tw:(c + 1) * tw]),
                 reads=[xk + "#%d" % c], writes=[sqk + "#%d" % (c % nsl)],
                 tag=tag + "_sq%d" % c)
        for c in range(nch):
            sl = sq[:, (c % nsl) * tw:(c % nsl + 1) * tw]
            b.op("pe", lambda e, sl=sl, c=c: e.matmul(
                     banks["sb"], ones_f[:], sl,
                     start=(c == 0), stop=(c == nch - 1)),
                 reads=[sqk + "#%d" % (c % nsl), "ones_k"],
                 writes=[banks["kb"]], tag=tag + "_sq2_%d" % c)
    r = stat[0:1, 0:tw]
    mur = stat[0:1, tw:2 * tw]
    mu = stat[0:1, 2 * tw:3 * tw]
    v = stat[0:1, 3 * tw:4 * tw]
    m2 = stat[0:1, 4 * tw:5 * tw]
    b.op("act", lambda e: e.mul(mu, banks["sa"], 1.0 / nfeat),
         reads=[banks["ka"]], writes=[statk], tag=tag + "_mu")
    b.op("act", lambda e: e.mul(v, banks["sb"], 1.0 / nfeat),
         reads=[banks["kb"]], writes=[statk], tag=tag + "_ex2")
    b.op("dve", lambda e: e.tensor_mul(m2, mu, mu), reads=[statk],
         writes=[statk], tag=tag + "_mu2")
    b.op("dve", lambda e: e.tensor_sub(v, v, m2), reads=[statk],
         writes=[statk], tag=tag + "_var")
    epsap = p.cval(EPS)
    b.op("act", lambda e: e.activation(m2, v, ACT.Sqrt, bias=epsap),
         reads=[statk, "cvals"], writes=[statk], tag=tag + "_sd")
    b.op("dve", lambda e: e.reciprocal(r, m2),
         reads=[statk], writes=[statk], tag=tag + "_r")
    b.op("dve", lambda e: e.tensor_mul(mur, mu, r), reads=[statk],
         writes=[statk], tag=tag + "_mur")
    if scale_ap is not None:
        rm2 = stat[0:1, 0:2 * tw]
        b.op("dve", lambda e: e.tensor_scalar(rm2, rm2, scale_ap, None,
                                              OP.mult),
             reads=[statk, scale_key], writes=[statk], tag=tag + "_fs")
    b.op("pe", lambda e: e.matmul(banks["br"], ones_m[:], r, start=True,
                                  stop=True),
         reads=[statk, "ones_m"], writes=[banks["kr"]], tag=tag + "_bcr")
    b.op("pe", lambda e: e.matmul(banks["bm"], ones_m[:], mur, start=True,
                                  stop=True),
         reads=[statk, "ones_m"], writes=[banks["km"]], tag=tag + "_bcm")


def ln_apply(p, x, xk, ln, lnk, tw, banks, nch=NCH, tag="lnap"):
    b = p.b
    for c in range(nch):
        xs = x[:, c * tw:(c + 1) * tw]
        ls = ln[:, c * tw:(c + 1) * tw]
        b.op("dve", lambda e, xs=xs, ls=ls: e.tensor_mul(ls, xs, banks["br"]),
             reads=[xk + "#%d" % c, banks["kr"]], writes=[lnk + "#%d" % c],
             tag=tag + "_m%d" % c)
    for c in range(nch):
        ls = ln[:, c * tw:(c + 1) * tw]
        b.op("dve", lambda e, ls=ls: e.tensor_sub(ls, ls, banks["bm"]),
             reads=[lnk + "#%d" % c, banks["km"]], writes=[lnk + "#%d" % c],
             tag=tag + "_s%d" % c)


def ln_apply_residual(p, y, yk, x, xk, out, outk, tw, banks, nch=NCH,
                      tag="lnres"):
    b = p.b
    for c in range(nch):
        ys = y[:, c * tw:(c + 1) * tw]
        os_ = out[:, c * tw:(c + 1) * tw]
        b.op("dve", lambda e, ys=ys, os_=os_: e.tensor_mul(os_, ys, banks["br"]),
             reads=[yk + "#%d" % c, banks["kr"]], writes=[outk + "#%d" % c],
             tag=tag + "_m%d" % c)
    for c in range(nch):
        os_ = out[:, c * tw:(c + 1) * tw]
        b.op("dve", lambda e, os_=os_: e.tensor_sub(os_, os_, banks["bm"]),
             reads=[outk + "#%d" % c, banks["km"]], writes=[outk + "#%d" % c],
             tag=tag + "_s%d" % c)
    for c in range(nch):
        xs = x[:, c * tw:(c + 1) * tw]
        os_ = out[:, c * tw:(c + 1) * tw]
        b.op("dve", lambda e, os_=os_, xs=xs: e.tensor_add(os_, os_, xs),
             reads=[outk + "#%d" % c, xk + "#%d" % c],
             writes=[outk + "#%d" % c], tag=tag + "_a%d" % c)


def std_banks(p, tw):
    ps = [p.sb[f"ps{i}"] for i in range(8)]
    return dict(sa=ps[6][0:1, 0:tw], sb=ps[7][0:1, 0:tw],
                br=ps[4][:, 0:tw], bm=ps[5][:, 0:tw],
                ka="ps6", kb="ps7", kr="ps4", km="ps5")



def _prep_weights(inputs):
    w = {}
    T = lambda a: np.ascontiguousarray(np.asarray(a).T)
    V = lambda v: np.ascontiguousarray(np.asarray(v).reshape(-1, 128).T)
    w["w_qkv_t"] = T(inputs["L0_attn_w_qkv"])
    w["b_qkv"] = V(inputs["L0_attn_b_qkv"])
    w["b_v_row"] = np.ascontiguousarray(
        np.asarray(inputs["L0_attn_b_qkv"])[2048:].reshape(1, 1024))
    w["w_g1_t"] = T(inputs["L0_attn_w_g1"])
    w["b_g1"] = np.ascontiguousarray(
        np.asarray(inputs["L0_attn_b_g1"]).reshape(64, 1))
    w["w_g2_t"] = T(inputs["L0_attn_w_g2"])
    w["b_g2"] = V(inputs["L0_attn_b_g2"])
    w["w_out_t"] = T(inputs["L0_attn_w_out"])
    w["b_out"] = V(inputs["L0_attn_b_out"])
    w["ln_g"] = V(inputs["L0_attn_ln_g"])
    w["ln_b"] = V(inputs["L0_attn_ln_b"])
    w["ffn_k"] = np.asarray(inputs["L0_ffn_kernel"])
    w["ffn_sin"] = np.asarray(inputs["L0_ffn_input_scale"])
    w["ffn_sk"] = np.asarray(inputs["L0_ffn_kernel_scale"])
    w["ffn_bias"] = V(inputs["L0_ffn_bias"])
    w["wq_t"] = T(inputs["L1_attn_Wq"])
    w["wk_t"] = T(inputs["L1_attn_Wk"])
    w["wv_t"] = T(inputs["L1_attn_Wv"])
    w["wf_t"] = T(inputs["L1_attn_Wf"])
    w["g_w"] = V(inputs["L1_attn_g_w"])
    w["wo_t"] = T(inputs["L1_attn_Wo"])
    w["wfc_t"] = T(inputs["L1_ffn_W_fc"])
    w["b_fc"] = V(inputs["L1_ffn_b_fc"])
    w["afc_t"] = T(inputs["L1_ffn_A_fc"])
    w["bfc_t"] = T(inputs["L1_ffn_B_fc"])
    w["wpr_t"] = T(inputs["L1_ffn_W_proj"])
    w["b_pr"] = V(inputs["L1_ffn_b_proj"])
    w["apr_t"] = T(inputs["L1_ffn_A_proj"])
    w["bpr_t"] = T(inputs["L1_ffn_B_proj"])
    return w


def _prep_x(inputs):
    x = np.asarray(inputs["L0_attn_x"])
    xs = []
    for c in range(8):
        xc = np.concatenate([x[2 * c], x[2 * c + 1]], axis=0)
        xs.append(np.ascontiguousarray(xc.T))
    return xs


def kernel(**inputs):
    w = _prep_weights(inputs)
    xs = _prep_x(inputs)
    p = build_program()
    in_maps = []
    for c in range(8):
        m = dict(w)
        m["x_fm"] = xs[c]
        in_maps.append(m)
    with p.stack:
        res = run_bass_kernel_spmd(p.nc, in_maps, core_ids=list(range(8)))
    outs = []
    for c in range(8):
        y = res.results[c]["y_fm"]
        outs.append(y[:, :SEQ].T)
        outs.append(y[:, SEQ:].T)
    return np.ascontiguousarray(np.stack(outs, axis=0).astype(np.float32))



def alloc_globals(p):
    for i in range(8):
        p.psum(f"ps{i}")
    p.off = {}
    rup = lambda a: (a + 31) & ~31
    for i in range(3):
        p.off[f"t{i}"] = rup(p.nc._sbuf_addr_for_side(None))
        p.sbuf(f"t{i}", [128, 4096], F32)
    p.sbuf("sq_t", [128, 4096], F32)
    p.off["wbig"] = rup(p.nc._sbuf_addr_for_side(None))
    p.sbuf("wbig", [128, 16384], F32)
    p.sbuf("u0", [128, 4096], BF16)
    p.sbuf_alias("hb", [128, 4096], BF16, p.off["wbig"])
    p.sbuf_alias("lb", [128, 4096], BF16, p.off["wbig"] + 8192)
    p.sbuf_alias("mb", [128, 4096], BF16, p.off["wbig"] + 16384)
    p.sbuf_alias("wq16", [128, 8192], BF16, p.off["wbig"] + 24576)
    p.sbuf_alias("d0w16", [128, 8192], BF16, p.off["t1"])
    p.sbuf_alias("hbB", [128, 4096], BF16, p.off["wbig"] + 40960)
    p.sbuf_alias("lbB", [128, 4096], BF16, p.off["wbig"] + 49152)
    p.sbuf_alias("d2h0", [128, 8192], BF16, p.off["wbig"])
    p.sbuf_alias("d2l0", [128, 8192], BF16, p.off["wbig"] + 16384)
    p.sbuf_alias("d2h1", [128, 8192], BF16, p.off["wbig"] + 32768)
    p.sbuf_alias("d2l1", [128, 8192], BF16, p.off["wbig"] + 49152)
    p.sbuf_alias("wq16d", [128, 8192], BF16, p.off["t0"])
    p.sbuf("kq8", [128, 8192], FP8)
    p.sbuf("q8t", [128, 4096], FP8)
    p.sbuf("stat", [1, 5 * 512], F32)
    p.sbuf("ones_t", [128, 256], F32)
    p.sbuf("c_mx", [128, 4], F32)
    p.sbuf("c_fr", [1, 256], F32)
    p.sbuf("c_sc", [128, 256], F32)
    p.sbuf("c_asb", [128, 256], F32)
    p.sbuf("c_utm", [128, 256], F32)
    setup_consts(p)


def build_program(phases=("A", "B", "C", "D")):
    p = Prog()
    alloc_globals(p)
    x_in = p.dram_in("x_fm", [D, NTOK], F32)
    y_out = p.dram_out("y_fm", [D, NTOK], F32)
    s1 = p.dram_tmp("s1", [D, NTOK], F32)
    s2 = p.dram_tmp("s2", [D, NTOK], F32)
    scr = [(s1, "s1"), (s2, "s2")]
    chain = [ph for ph in ("A", "B", "C", "D") if ph in phases]
    emits = {"A": emit_phase_a, "B": emit_phase_b, "C": emit_phase_c,
             "D": emit_phase_d}
    cur = (x_in, "x_fm")
    for i, ph in enumerate(chain):
        dst = (y_out, "y_fm") if i == len(chain) - 1 else scr[i % 2]
        emits[ph](p, cur, dst)
        cur = dst
    p.b.finalize()
    return p



def emit_phase_a(p, src, dst):
    src_hbm, _ = src
    dst_hbm, dst_key = dst
    b = p.b
    TW = 256
    nt = SEQ // TW
    nsub = TW // 128
    ps = [p.sb[f"ps{i}"] for i in range(8)]
    banks = std_banks(p, TW)
    t = [p.sb[f"t{i}"] for i in range(3)]
    sq_hi = p.sb["sq_t"][:, 2048:4096]
    ek_t, ekk = t[2][:, 0:2048], "a_ek"
    ektm, ekmk = t[2][:, 2048:4096], "a_ektm"
    v_tm, vtmk = sq_hi, "sq_hi"
    q_t, qk = t[2][:, 0:2048], "a_q"
    ao_t, aok = t[2][:, 2048:4096], "a_ao"
    lno_t, lnok = sq_hi, "sq_hi"
    stat = p.sb["stat"]
    wbig = p.sb["wbig"]
    ident = p.sb["ident"]
    ones_m = p.sb["ones_m"]

    bqkv = p.sbuf("a_bqkv", [128, 24], F32)
    load_vec(p, "sp", bqkv[:], "a_bqkv", p.dram_in("b_qkv", [128, 24], F32))
    bvrow = p.sbuf("a_bvrow", [1, D], F32)
    load_vec(p, "sp", bvrow[:], "a_bvrow", p.dram_in("b_v_row", [1, D], F32))
    wg1 = p.sbuf("a_wg1", [128, NCH * 64], F32)
    load_w(p, "sp", wg1[:], "a_wg1", p.dram_in("w_g1_t", [D, 64], F32), D, 64)
    bg1 = p.sbuf("a_bg1", [64, 1], F32)
    load_vec(p, "sp", bg1[:], "a_bg1", p.dram_in("b_g1", [64, 1], F32))
    wg2 = p.sbuf("a_wg2", [64, D], F32)
    load_vec(p, "sp", wg2[:], "a_wg2", p.dram_in("w_g2_t", [64, D], F32))
    bg2 = p.sbuf("a_bg2", [128, NCH], F32)
    load_vec(p, "sp", bg2[:], "a_bg2", p.dram_in("b_g2", [128, NCH], F32))
    bout = p.sbuf("a_bout", [128, NCH], F32)
    load_vec(p, "sp", bout[:], "a_bout", p.dram_in("b_out", [128, NCH], F32))
    lng = p.sbuf("a_lng", [128, NCH], F32)
    load_vec(p, "sp", lng[:], "a_lng", p.dram_in("ln_g", [128, NCH], F32))
    lnb = p.sbuf("a_lnb", [128, NCH], F32)
    load_vec(p, "sp", lnb[:], "a_lnb", p.dram_in("ln_b", [128, NCH], F32))
    sacc = p.sbuf("a_sacc", [128, NCH], F32)
    tacc = p.sbuf("a_tacc", [128, NCH], F32)
    kv_sb = p.sbuf("a_kv", [128, NCH * 64], F32)
    srec = p.sbuf("a_srec", [128, NCH], F32)
    gh_t = p.sbuf("a_gh", [64, TW], F32)
    gtmp = p.sbuf("a_gtmp", [128, TW], F32)
    wqkv_h = p.dram_in("w_qkv_t", [D, 3072], F32)
    w_out_h = p.dram_in("w_out_t", [D, D], F32)

    def ptile(i, it):
        par = it % 2
        return t[i][:, par * 2048:(par + 1) * 2048], "a_t%d_%d" % (i, par)

    for sq_ in range(2):
        base = sq_ * SEQ
        b.op("dve", lambda e: e.memset(sacc[:], 0.0), writes=["a_sacc"],
             tag="a_z")
        load_w(p, "sp", wbig[:, 0:NCH * D], "wbig", wqkv_h, D, 3072,
               col0=1024, ncols=D, tag="a_ldk")
        load_w(p, "sp", wbig[:, NCH * D:2 * NCH * D], "wbig", wqkv_h, D, 3072,
               col0=2048, ncols=D, tag="a_ldv")

        for it in range(nt):
            t0 = base + it * TW
            x_t, xk = ptile(0, it)
            ln_t, lnk = ptile(1, it)
            load_fm(p, "sp", x_t, xk, src_hbm, t0, TW, tag="a1_ldx%d" % it)
            emit_ln(p, x_t, xk, TW, banks, stat, "stat", tag="a1_ln%d" % it)
            ln_apply(p, x_t, xk, ln_t, lnk, TW, banks, tag="a1_lnap%d" % it)
            for o in range(NCH):
                bank = ps[o % 2]
                bkey = "ps%d" % (o % 2)

                def mm(eng, o=o, bank=bank, ln_t=ln_t):
                    ins = None
                    for c in range(NCH):
                        col = c * D + o * 128
                        ins = eng.matmul(bank[:, 0:TW], wbig[:, col:col + 128],
                                         ln_t[:, c * TW:(c + 1) * TW],
                                         start=(c == 0), stop=(c == NCH - 1))
                    return ins

                b.op("pe", mm, reads=["wbig"] + ck(lnk), writes=[bkey],
                     tag="a1_kmm%d_%d" % (it, o))
                eksl = ek_t[:, o * TW:(o + 1) * TW]
                b.op("act", lambda e, eksl=eksl, bank=bank, o=o: e.activation(
                         eksl, bank[:, 0:TW], ACT.Exp,
                         bias=bqkv[:, 8 + o:9 + o],
                         accum_out=tacc[:, o:o + 1]),
                     reads=[bkey, "a_bqkv"],
                     writes=[ekk + "#%d" % o, "a_tacc"],
                     tag="a1_exp%d_%d" % (it, o))
                b.op("dve", lambda e, o=o: e.tensor_add(
                         sacc[:, o:o + 1], sacc[:, o:o + 1], tacc[:, o:o + 1]),
                     reads=["a_tacc", "a_sacc"], writes=["a_sacc"],
                     tag="a1_sacc%d_%d" % (it, o))
            for half_ in range(2):
                for ts in range(nsub):
                    bank = ps[half_]
                    bkey = "ps%d" % half_

                    def mmv(eng, half_=half_, ts=ts, bank=bank, ln_t=ln_t):
                        ins = None
                        for c in range(NCH):
                            ins = eng.matmul(
                                bank[:, 0:512],
                                ln_t[:, c * TW + ts * 128:
                                     c * TW + ts * 128 + 128],
                                wbig[:, NCH * D + c * D + half_ * 512:
                                     NCH * D + c * D + (half_ + 1) * 512],
                                start=(c == 0), stop=False)
                        ins = eng.matmul(
                            bank[:, 0:512], ones_m[:],
                            bvrow[0:1, half_ * 512:(half_ + 1) * 512],
                            start=False, stop=True)
                        return ins

                    b.op("pe", mmv,
                         reads=["wbig", "ones_m", "a_bvrow"] + ck(lnk),
                         writes=[bkey], tag="a1_vmm%d_%d_%d" % (it, half_, ts))
                    b.op("act", lambda e, half_=half_, ts=ts, bank=bank: e.copy(
                             v_tm[:, ts * D + half_ * 512:
                                  ts * D + (half_ + 1) * 512],
                             bank[:, 0:512]),
                         reads=[bkey], writes=[vtmk + "#%d" % (ts * 2 + half_)],
                         tag="a1_vcp%d_%d_%d" % (it, half_, ts))
            for o in range(NCH):
                bank = ps[o % 2]
                bkey = "ps%d" % (o % 2)

                def mmt(eng, o=o, bank=bank):
                    ins = None
                    for ts in range(nsub):
                        ins = eng.matmul(bank[:, ts * 128:(ts + 1) * 128],
                                         ek_t[:, o * TW + ts * 128:
                                              o * TW + (ts + 1) * 128],
                                         ident[:], is_transpose=True,
                                         start=(ts == 0), stop=(ts == nsub - 1))
                    return ins

                b.op("pe", mmt, reads=[ekk + "#%d" % o, "ident"],
                     writes=[bkey], tag="a1_ekT%d_%d" % (it, o))

                def cpt(eng, o=o, bank=bank):
                    ins = None
                    for ts in range(nsub):
                        ins = eng.tensor_copy(
                            ektm[:, ts * D + o * 128: ts * D + o * 128 + 128],
                            bank[:, ts * 128:(ts + 1) * 128])
                    return ins

                b.op("dve", cpt, reads=[bkey], writes=[ekmk + "#%d" % o],
                     tag="a1_ekTc%d_%d" % (it, o))
            for hh in range(NCH):
                bankkv = ps[2 + hh // 4]
                bkey = "ps%d" % (2 + hh // 4)
                off = (hh % 4) * 128

                def mkv(eng, hh=hh, bankkv=bankkv, off=off, it=it):
                    ins = None
                    for ts in range(nsub):
                        first = (it == 0 and hh % 4 == 0 and ts == 0)
                        last = (it == nt - 1 and hh % 4 == 3 and ts == nsub - 1)
                        ins = eng.matmul(
                            bankkv[:, off:off + 128],
                            ektm[:, ts * D + hh * 128: ts * D + hh * 128 + 128],
                            v_tm[:, ts * D + hh * 128: ts * D + hh * 128 + 128],
                            start=first, stop=last)
                    return ins

                vu = [vtmk + "#%d" % (ts * 2 + hh // 4) for ts in range(2)]
                b.op("pe", mkv, reads=[ekmk + "#%d" % hh] + vu, writes=[bkey],
                     tag="a1_kv%d_%d" % (it, hh))
        b.op("dve", lambda e: e.reciprocal(srec[:], sacc[:]),
             reads=["a_sacc"], writes=["a_srec"], tag="a_srec")
        for hh in range(NCH):
            bankkv = ps[2 + hh // 4]
            bkey = "ps%d" % (2 + hh // 4)
            off = (hh % 4) * 128
            b.op("dve", lambda e, hh=hh, bankkv=bankkv, off=off: e.tensor_scalar(
                     kv_sb[0:64, hh * 64:(hh + 1) * 64],
                     bankkv[0:64, off:off + 64], srec[0:64, hh:hh + 1], None,
                     OP.mult),
                 reads=[bkey, "a_srec"], writes=["a_kv"], tag="a_kvn%d_0" % hh)
            b.op("dve", lambda e, hh=hh, bankkv=bankkv, off=off: e.tensor_scalar(
                     kv_sb[64:128, hh * 64:(hh + 1) * 64],
                     bankkv[64:128, off + 64:off + 128], srec[64:128, hh:hh + 1],
                     None, OP.mult),
                 reads=[bkey, "a_srec"], writes=["a_kv"], tag="a_kvn%d_1" % hh)

        load_w(p, "sp", wbig[:, 0:NCH * D], "wbig", wqkv_h, D, 3072, col0=0,
               ncols=D, tag="a_ldq")
        load_w(p, "sp", wbig[:, NCH * D:2 * NCH * D], "wbig", w_out_h, D, D,
               tag="a_ldwo")

        for it in range(nt):
            t0 = base + it * TW
            x_t, xk = ptile(0, it)
            ln_t, lnk = ptile(1, it)
            xo_t, xok = ln_t, lnk
            load_fm(p, "sp", x_t, xk, src_hbm, t0, TW, tag="a2_ldx%d" % it)
            emit_ln(p, x_t, xk, TW, banks, stat, "stat", tag="a2_ln%d" % it)
            ln_apply(p, x_t, xk, ln_t, lnk, TW, banks, tag="a2_lnap%d" % it)
            for o in range(NCH):
                bank = ps[o % 2]
                bkey = "ps%d" % (o % 2)

                def mm(eng, o=o, bank=bank, ln_t=ln_t):
                    ins = None
                    for c in range(NCH):
                        col = c * D + o * 128
                        ins = eng.matmul(bank[:, 0:TW], wbig[:, col:col + 128],
                                         ln_t[:, c * TW:(c + 1) * TW],
                                         start=(c == 0), stop=(c == NCH - 1))
                    return ins

                b.op("pe", mm, reads=["wbig"] + ck(lnk), writes=[bkey],
                     tag="a2_qmm%d_%d" % (it, o))
                qsl = q_t[:, o * TW:(o + 1) * TW]
                b.op("act", lambda e, qsl=qsl, bank=bank, o=o: e.activation(
                         qsl, bank[:, 0:TW], ACT.Silu, bias=bqkv[:, o:o + 1]),
                     reads=[bkey, "a_bqkv"], writes=[qk + "#%d" % o],
                     tag="a2_silu%d_%d" % (it, o))
            for hh in range(NCH):
                bank = ps[2 + (hh % 2)]
                bkey = "ps%d" % (2 + (hh % 2))

                def mma(eng, hh=hh, bank=bank):
                    ins = None
                    for sub in range(2):
                        rb = sub * 64
                        ins = eng.matmul(
                            bank[rb:rb + 64, 0:TW],
                            kv_sb[rb:rb + 64, hh * 64:(hh + 1) * 64],
                            q_t[rb:rb + 64, hh * TW:(hh + 1) * TW],
                            start=True, stop=True)
                    return ins

                b.op("pe", mma, reads=["a_kv", qk + "#%d" % hh],
                     writes=[bkey], tag="a2_ao%d_%d" % (it, hh))
                b.op("act", lambda e, hh=hh, bank=bank: e.copy(
                         ao_t[:, hh * TW:(hh + 1) * TW], bank[:, 0:TW]),
                     reads=[bkey], writes=[aok + "#%d" % hh],
                     tag="a2_aocp%d_%d" % (it, hh))
            emit_ln(p, ao_t, aok, TW, banks, stat, "stat", tag="a2_lno%d" % it)
            for c in range(NCH):
                asl = ao_t[:, c * TW:(c + 1) * TW]
                lsl = lno_t[:, c * TW:(c + 1) * TW]
                b.op("dve", lambda e, asl=asl, lsl=lsl: e.tensor_mul(
                         lsl, asl, banks["br"]),
                     reads=[aok + "#%d" % c, banks["kr"]],
                     writes=[lnok + "#%d" % c],
                     tag="a2_lnoap%d_m%d" % (it, c))
            for c in range(NCH):
                lsl = lno_t[:, c * TW:(c + 1) * TW]
                b.op("dve", lambda e, lsl=lsl: e.tensor_sub(lsl, lsl,
                                                            banks["bm"]),
                     reads=[lnok + "#%d" % c, banks["km"]],
                     writes=[lnok + "#%d" % c],
                     tag="a2_lnoap%d_s%d" % (it, c))
            for c in range(NCH):
                lsl = lno_t[:, c * TW:(c + 1) * TW]
                b.op("dve", lambda e, lsl=lsl, c=c: e.tensor_scalar(
                         lsl, lsl, lng[:, c:c + 1], lnb[:, c:c + 1],
                         OP.mult, OP.add),
                     reads=[lnok + "#%d" % c, "a_lng", "a_lnb"],
                     writes=[lnok + "#%d" % c],
                     tag="a2_lnogb%d_%d" % (it, c))

            def mmg1(eng, ln_t=ln_t):
                ins = None
                for c in range(NCH):
                    ins = eng.matmul(ps[3][0:64, 0:TW],
                                     wg1[:, c * 64:(c + 1) * 64],
                                     ln_t[:, c * TW:(c + 1) * TW],
                                     start=(c == 0), stop=(c == NCH - 1))
                return ins

            b.op("pe", mmg1, reads=["a_wg1"] + ck(lnk), writes=["ps3"],
                 tag="a2_g1%d" % it)
            b.op("act", lambda e: e.activation(gh_t[:], ps[3][0:64, 0:TW],
                                               ACT.Identity, bias=bg1[:, 0:1]),
                 reads=["ps3", "a_bg1"], writes=["a_gh"], tag="a2_g1b%d" % it)
            for o in range(NCH):
                bank = ps[o % 2]
                bkey = "ps%d" % (o % 2)
                b.op("pe", lambda e, o=o, bank=bank: e.matmul(
                         bank[:, 0:TW], wg2[:, o * 128:(o + 1) * 128], gh_t[:],
                         start=True, stop=True),
                     reads=["a_wg2", "a_gh"], writes=[bkey],
                     tag="a2_g2%d_%d" % (it, o))
                b.op("act", lambda e, bank=bank, o=o: e.activation(
                         gtmp[:], bank[:, 0:TW], ACT.Sigmoid,
                         bias=bg2[:, o:o + 1]),
                     reads=[bkey, "a_bg2"], writes=["a_gtmp"],
                     tag="a2_sig%d_%d" % (it, o))
                lsl = lno_t[:, o * TW:(o + 1) * TW]
                b.op("dve", lambda e, lsl=lsl: e.tensor_mul(lsl, gtmp[:], lsl),
                     reads=["a_gtmp", lnok + "#%d" % o],
                     writes=[lnok + "#%d" % o],
                     tag="a2_gmul%d_%d" % (it, o))
            for o in range(NCH):
                bank = ps[2 + (o % 2)]
                bkey = "ps%d" % (2 + (o % 2))

                def mmo(eng, o=o, bank=bank):
                    ins = None
                    for c in range(NCH):
                        col = NCH * D + c * D + o * 128
                        ins = eng.matmul(bank[:, 0:TW], wbig[:, col:col + 128],
                                         lno_t[:, c * TW:(c + 1) * TW],
                                         start=(c == 0), stop=(c == NCH - 1))
                    return ins

                b.op("pe", mmo, reads=["wbig"] + ck(lnok), writes=[bkey],
                     tag="a2_wo%d_%d" % (it, o))
                b.op("act", lambda e, o=o, bank=bank: e.activation(
                         ao_t[:, o * TW:(o + 1) * TW], bank[:, 0:TW],
                         ACT.Identity, bias=bout[:, o:o + 1]),
                     reads=[bkey, "a_bout"], writes=[aok + "#%d" % o],
                     tag="a2_wob%d_%d" % (it, o))
            emit_ln(p, ao_t, aok, TW, banks, stat, "stat",
                    tag="a2_ln2_%d" % it)
            ln_apply_residual(p, ao_t, aok, x_t, xk, xo_t, xok, TW, banks,
                              tag="a2_res%d" % it)
            store_fm(p, "pool", dst_hbm, dst_key, xo_t, xok, t0, TW,
                     tag="a2_st%d" % it)



def emit_phase_b(p, src, dst):
    src_hbm, _ = src
    dst_hbm, dst_key = dst
    b = p.b
    TW = 256
    nt = NTOK // TW
    ps = [p.sb[f"ps{i}"] for i in range(8)]
    banks = std_banks(p, TW)
    t = [p.sb[f"t{i}"] for i in range(3)]
    stat = p.sb["stat"]
    u0 = p.sb["u0"]
    kq8 = p.sb["kq8"]
    q8t = p.sb["q8t"]
    ones_m = p.sb["ones_m"]
    b.barrier("b_bar")

    bias16 = p.sbuf("b_bias16", [128, NCH], BF16)
    load_vec(p, "sp", bias16[:], "b_bias16",
             p.dram_in("ffn_bias", [128, D // 128], BF16))
    bias32 = p.sbuf("b_bias32", [128, NCH], F32)
    b.op("dve", lambda e: e.tensor_copy(bias32[:], bias16[:]),
         reads=["b_bias16"], writes=["b_bias32"], tag="b_b32")
    scal = p.sbuf("b_scal", [1, 4], F32)
    b.op("dve", lambda e: e.memset(scal[:], 0.0), writes=["b_scal"], tag="b_z")
    sin_h = p.dram_in("ffn_sin", [1], F32)
    sk_h = p.dram_in("ffn_sk", [1], F32)
    b.dma("sp", scal[0:1, 0:1], sin_h[:], reads=["b_scal"], writes=["b_scal"])
    b.dma("sp", scal[0:1, 3:4], sk_h[:], reads=["b_scal"], writes=["b_scal"])
    b.op("dve", lambda e: e.tensor_mul(scal[0:1, 1:2], scal[0:1, 0:1],
                                       scal[0:1, 3:4]),
         reads=["b_scal"], writes=["b_scal"], tag="sprod")
    scalb = p.sbuf("b_scalb", [128, 2], F32)
    skb = p.sbuf("b_skb", [128, 1], F32)
    b.op("pe", lambda e: e.matmul(ps[3][:, 0:4], ones_m[:], scal[0:1, 0:4],
                                  start=True, stop=True),
         reads=["b_scal", "ones_m"], writes=["ps3"], tag="bc_scal_mm")
    b.op("dve", lambda e: e.tensor_copy(scalb[:], ps[3][:, 0:2]),
         reads=["ps3"], writes=["b_scalb"], tag="bc_scal_cp")
    b.op("dve", lambda e: e.tensor_copy(skb[:], ps[3][:, 3:4]),
         reads=["ps3"], writes=["b_skb"], tag="bc_sk_cp")
    rsin = p.sbuf("b_rsin", [128, 1], F32)
    rsk = p.sbuf("b_rsk", [128, 1], F32)
    b.op("dve", lambda e: e.reciprocal(rsin[:], scalb[:, 0:1]),
         reads=["b_scalb"], writes=["b_rsin"], tag="rsin")
    b.op("dve", lambda e: e.reciprocal(rsk[:], skb[:]),
         reads=["b_skb"], writes=["b_rsk"], tag="rsk")
    ffnk_h = p.dram_in("ffn_k", [D, D], BF16)
    for hf in range(2):
        load_w(p, "sp", u0[:, 0:NCH * 512], "u0", ffnk_h, D, D,
               col0=hf * 512, ncols=512, tag="b_ldk%d" % hf)
        for c in range(NCH):
            sl_in = u0[:, c * 512:(c + 1) * 512]
            sl_out = kq8[:, c * D + hf * 512: c * D + (hf + 1) * 512]
            b.op("dve", lambda e, a=sl_out, x=sl_in: e.tensor_scalar(
                     a, x, rsk[:, 0:1], None, OP.mult),
                 reads=["u0", "b_rsk"], writes=["kq8#%d_%d" % (hf, c)],
                 tag="b_wq8_%d_%d" % (hf, c))

    for it in range(nt):
        t0_ = it * TW
        par = it % 2
        off = par * NCH * TW
        x_t, xk = t[0][:, off:off + NCH * TW], "b_x%d" % par
        ln_t, lnk = t[1][:, off:off + NCH * TW], "b_ln%d" % par
        xo_t, xok = t[2][:, off:off + NCH * TW], "b_xo%d" % par
        xb = u0[:, off:off + NCH * TW]
        xbk = "b_u%d" % par
        q8 = q8t[:, off:off + NCH * TW]
        q8k = "b_q8%d" % par
        gb = [ps[2 * par], ps[2 * par + 1]]
        gbk = ["ps%d" % (2 * par), "ps%d" % (2 * par + 1)]
        load_fm(p, "sp", x_t, xk, src_hbm, t0_, TW, tag="b_ldx%d" % it)
        emit_ln(p, x_t, xk, TW, banks, stat, "stat", tag="b_ln%d" % it)
        ln_apply(p, x_t, xk, ln_t, lnk, TW, banks, tag="b_lnap%d" % it)
        b.op("dve", lambda e, xb=xb, ln_t=ln_t: e.tensor_copy(xb, ln_t),
             reads=ck(lnk), writes=ck(xbk), tag="b_bf%d" % it)
        b.op("dve", lambda e, q8=q8, xb=xb: e.tensor_scalar(
                 q8, xb, rsin[:, 0:1], None, OP.mult),
             reads=ck(xbk) + ["b_rsin"], writes=ck(q8k),
             tag="b_q8_%d" % it)
        for o in range(NCH):
            bank = gb[o % 2]
            bkey = gbk[o % 2]

            def mm(eng, o=o, bank=bank, q8=q8):
                ins = None
                for c in range(NCH):
                    ins = eng.matmul(bank[:, 0:TW],
                                     kq8[:, c * D + o * 128: c * D + o * 128 + 128],
                                     q8[:, c * TW:(c + 1) * TW],
                                     start=(c == 0), stop=(c == NCH - 1))
                return ins

            b.op("pe", mm,
                 reads=["kq8#%d_%d" % (o // 4, c) for c in range(NCH)]
                 + ck(q8k), writes=[bkey], tag="b_mm%d_%d" % (it, o))
            ysl = xb[:, o * TW:(o + 1) * TW]
            b.op("act", lambda e, ysl=ysl, bank=bank: e.mul(
                     ysl, bank[:, 0:TW], scalb[:, 1:2]),
                 reads=[bkey, "b_scalb"], writes=[xbk + "#%d" % o],
                 tag="b_deq%d_%d" % (it, o))
        for o in range(NCH):
            ysl = xb[:, o * TW:(o + 1) * TW]
            b.op("dve", lambda e, ysl=ysl, o=o: e.tensor_scalar(
                     ysl, ysl, bias32[:, o:o + 1], None, OP.add),
                 reads=[xbk + "#%d" % o, "b_bias32"],
                 writes=[xbk + "#%d" % o],
                 tag="b_bias%d_%d" % (it, o))
        emit_ln(p, xb, xbk, TW, banks, stat, "stat", x_bf16=True,
                tag="b_ln2_%d" % it)
        ln_apply_residual(p, xb, xbk, x_t, xk, xo_t, xok, TW, banks,
                          tag="b_res%d" % it)
        store_fm(p, "pool", dst_hbm, dst_key, xo_t, xok, t0_, TW,
                 tag="b_st%d" % it)




ZIP = False
DBG_SAMEBUF = False
DBG_SPLIT_EVAC = False
DBG_SPLIT_SADD = False
DBG_PS0F = False
DBG_FRBC_DED = False
DBG_FBC_ONLY = False


def emit_phase_c(p, src, dst):
    src_hbm, src_key = src
    dst_hbm, dst_key = dst
    b = p.b
    ps = [p.sb[f"ps{i}"] for i in range(8)]
    t = [p.sb[f"t{i}"] for i in range(3)]
    stat = p.sb["stat"]
    wbig = p.sb["wbig"]
    ident = p.sb["ident"]
    tri = p.sb["tri"]
    ones_m = p.sb["ones_m"]
    ones_t = p.sb["ones_t"]
    QS = p.dram_tmp("qs", [D, NTOK], F32)
    KS = p.dram_tmp("ks", [D, NTOOK := NTOK], F32)
    VS = p.dram_tmp("vs", [D, NTOK], F32)
    FS = p.dram_tmp("fs", [D, NTOK], F32)
    OS = p.dram_tmp("os", [D, NTOK], F32)
    wq_h = p.dram_in("wq_t", [D, D], F32)
    wk_h = p.dram_in("wk_t", [D, D], F32)
    wv_h = p.dram_in("wv_t", [D, D], F32)
    wf_h = p.dram_in("wf_t", [D, D], F32)
    wo_h = p.dram_in("wo_t", [D, D], F32)
    gw = p.sbuf("c_gw", [128, NCH], F32)
    load_vec(p, "sp", gw[:], "c_gw", p.dram_in("g_w", [128, NCH], F32))
    z128 = p.cval(0.0, 128)
    n1_128 = p.cval(-1.0, 128)
    one128 = p.cval(1.0, 128)
    eps128 = p.cval(EPS, 128)

    TW = min(256, NTOK)
    nt = NTOK // TW
    banks = std_banks(p, TW)
    b.barrier("c1a_bar")
    load_w(p, "sp", wbig[:, 0:NCH * D], "wbig", wq_h, D, D, tag="c_ldwq")
    load_w(p, "sp", wbig[:, NCH * D:2 * NCH * D], "wbig", wk_h, D, D,
           tag="c_ldwk")

    def c1tile(i, it):
        par = it % 2
        return t[i][:, par * NCH * TW:(par + 1) * NCH * TW], "c1_t%d_%d" % (i, par)

    def proj_pass(pref, w_off, act_kind, out_scratch, it, ln_t, lnk, oslot):
        o_t = t[2][:, oslot * NCH * TW:(oslot + 1) * NCH * TW]
        ok_ = "c1_o%d" % oslot
        for o in range(NCH):
            bank = ps[o % 2]
            bkey = "ps%d" % (o % 2)

            def mm(eng, o=o, bank=bank, w_off=w_off, ln_t=ln_t, TW=TW):
                ins = None
                for c in range(NCH):
                    col = w_off + c * D + o * 128
                    ins = eng.matmul(bank[:, 0:TW], wbig[:, col:col + 128],
                                     ln_t[:, c * TW:(c + 1) * TW],
                                     start=(c == 0), stop=(c == NCH - 1))
                return ins

            b.op("pe", mm, reads=["wbig"] + ck(lnk), writes=[bkey],
                 tag="%s_mm%d_%d" % (pref, it, o))
            osl = o_t[:, o * TW:(o + 1) * TW]
            if act_kind == "silu":
                b.op("act", lambda e, osl=osl, bank=bank, TW=TW: e.activation(
                         osl, bank[:, 0:TW], ACT.Silu, bias=z128),
                     reads=[bkey, "cvals"], writes=[ok_ + "#%d" % o],
                     tag="%s_silu%d_%d" % (pref, it, o))
            else:
                ssl = p.sb["sq_t"][:, 2048 + o * TW:2048 + (o + 1) * TW]
                b.op("act", lambda e, ssl=ssl, bank=bank, TW=TW: e.activation(
                         ssl, bank[:, 0:TW], ACT.Exp, bias=z128,
                         scale=n1_128),
                     reads=[bkey, "cvals"], writes=["sq_hi#%d" % o],
                     tag="%s_e%d_%d" % (pref, it, o))
                b.op("act", lambda e, osl=osl, ssl=ssl: e.activation(
                         osl, ssl, ACT.Ln, bias=one128),
                     reads=["sq_hi#%d" % o, "cvals"],
                     writes=[ok_ + "#%d" % o],
                     tag="%s_l%d_%d" % (pref, it, o))
                b.op("act", lambda e, osl=osl: e.mul(osl, osl, -0.125),
                     reads=[ok_ + "#%d" % o], writes=[ok_ + "#%d" % o],
                     tag="%s_s%d_%d" % (pref, it, o))
        store_fm(p, "pool", out_scratch, pref + "_scr", o_t, ok_, it * TW, TW,
                 tag="%s_st%d" % (pref, it))

    for it in range(nt):
        x_t, xk = c1tile(0, it)
        ln_t, lnk = c1tile(1, it)
        load_fm(p, "sp", x_t, xk, src_hbm, it * TW, TW, tag="c1a_ldx%d" % it)
        emit_ln(p, x_t, xk, TW, banks, stat, "stat", tag="c1a_ln%d" % it)
        ln_apply(p, x_t, xk, ln_t, lnk, TW, banks, tag="c1a_lnap%d" % it)
        proj_pass("cq", 0, "silu", QS, it, ln_t, lnk, 0)
        proj_pass("ck", NCH * D, "silu", KS, it, ln_t, lnk, 1)
    b.barrier("c1b_bar")
    load_w(p, "sp", wbig[:, 0:NCH * D], "wbig", wv_h, D, D, tag="c_ldwv")
    load_w(p, "sp", wbig[:, NCH * D:2 * NCH * D], "wbig", wf_h, D, D,
           tag="c_ldwf")
    for it in range(nt):
        x_t, xk = c1tile(0, it)
        ln_t, lnk = c1tile(1, it)
        load_fm(p, "sp", x_t, xk, src_hbm, it * TW, TW, tag="c1b_ldx%d" % it)
        emit_ln(p, x_t, xk, TW, banks, stat, "stat", tag="c1b_ln%d" % it)
        ln_apply(p, x_t, xk, ln_t, lnk, TW, banks, tag="c1b_lnap%d" % it)
        proj_pass("cv", 0, "silu", VS, it, ln_t, lnk, 0)
        proj_pass("cf", NCH * D, "logsig", FS, it, ln_t, lnk, 1)

    b.barrier("c2_bar")
    C = 128
    ofm = t[1][:, 512:1536]
    osil = t[1][:, 1536:2560]
    osq = t[1][:, 2560:3584]
    sqt = p.sb["sq_t"]
    WSN = ["q2", "k2", "v2", "f2", "gam", "expg", "expng", "s2v",
           "ufm", "ktm", "utm", "egtm", "vtm", "qvtm", "qvfm", "qtfm"]

    def mkws(par):
        W, K = {}, {}
        base = t[0] if par == 0 else sqt
        for i, nm in enumerate(WSN):
            W[nm] = base[:, i * 256:(i + 1) * 256]
        W["ktu"] = base[:, 9 * 256:11 * 256]
        W["egv"] = base[:, 11 * 256:13 * 256]
        if par == 0:
            W["ovtm"] = t[1][:, 0:256]
            W["asb"] = p.sb["c_asb"][:, 0:128]
            W["a2sb"] = p.sb["c_asb"][:, 128:256]
            W["sc"] = p.sb["c_sc"][:, 0:256]
            W["mx0"] = p.sb["c_mx"][:, 0:1]
            W["mx1"] = p.sb["c_mx"][:, 1:2]
            W["mx2"] = p.sb["c_mx"][:, 2:3]
            W["fr"] = p.sb["c_fr"][0:1, 0:256]
            W["frbc"] = (p.sbuf("c_frbc0", [128, 256], F32)[:]
                         if DBG_FRBC_DED else wbig[:, 800:1056])
        else:
            W["ovtm"] = wbig[:, 0:256]
            W["asb"] = wbig[:, 256:384]
            W["a2sb"] = wbig[:, 384:512]
            W["sc"] = wbig[:, 512:768]
            W["mx0"] = wbig[:, 768:769]
            W["mx1"] = wbig[:, 769:770]
            W["mx2"] = wbig[:, 770:771]
            W["fr"] = wbig[0:1, 1312:1568]
            W["frbc"] = (p.sbuf("c_frbc1", [128, 256], F32)[:]
                         if DBG_FRBC_DED else wbig[:, 1056:1312])
        for nm in list(W):
            K[nm] = "c2%s_p%d" % (nm, par)
        for nm in ("mx0", "mx1", "mx2"):
            K[nm] = "c2mx_p%d" % par
        K["ktm"] = K["utm"] = K["ktu"]
        K["egtm"] = K["vtm"] = K["egv"]
        return W, K

    WS = [mkws(0), mkws(0) if DBG_SAMEBUF else mkws(1)]

    def head_ops(s_, c_, h):
        W, K = WS[h % 2]
        tt0 = s_ * SEQ + c_ * C
        tg = "c2_%d_%d_%d" % (s_, c_, h)
        S1 = t[2][:, h * 512: h * 512 + 512]
        S2 = t[2][:, 2048 + h * 512: 2048 + h * 512 + 512]
        s1k, s2k = "c_s1_%d" % h, "c_s2_%d" % h
        L = []

        def O(eng, fn, reads, writes, tag):
            L.append(lambda: b.op(eng, fn, reads=reads, writes=writes,
                                  tag=tag))

        def DM(dst, src, reads, writes, tag):
            L.append(lambda: b.dma("sp", dst, src, reads=reads,
                                   writes=writes, tag=tag))

        for scr, scrk, nm in ((QS, "cq_scr", "q2"), (KS, "ck_scr", "k2"),
                              (VS, "cv_scr", "v2"), (FS, "cf_scr", "f2")):
            src2 = scr.rearrange("(c p) t -> p c t", p=128)[
                :, 2 * h:2 * h + 2, tt0:tt0 + C]
            d2 = W[nm].rearrange("p (c t) -> p c t", c=2)
            DM(d2, src2, [scrk], [K[nm]], tg + "_l" + nm[0])

        def scan(eng, W=W):
            ins = None
            for blk in range(2):
                ins = eng.tensor_tensor_scan(
                    W["gam"][:, blk * C:(blk + 1) * C],
                    ones_t[:, 0:C], W["f2"][:, blk * C:(blk + 1) * C],
                    0.0, OP.mult, OP.add)
            return ins
        O("dve", scan, [K["f2"], "ones_t"], [K["gam"]], tg + "_scan")
        O("act", lambda e, W=W: e.activation(W["expg"][:], W["gam"][:],
                                             ACT.Exp, bias=z128),
          [K["gam"], "cvals"], [K["expg"]], tg + "_eg")
        O("act", lambda e, W=W: e.activation(W["expng"][:], W["gam"][:],
                                             ACT.Exp, bias=z128,
                                             scale=n1_128),
          [K["gam"], "cvals"], [K["expng"]], tg + "_eng")
        O("act", lambda e, W=W: e.activation(W["s2v"][:], W["f2"][:],
                                             ACT.Exp, bias=z128),
          [K["f2"], "cvals"], [K["s2v"]], tg + "_ef")
        O("dve", lambda e, W=W: e.scalar_tensor_tensor(
              W["s2v"][:], W["s2v"][:], -1.0, ones_t[:, 0:256],
              OP.mult, OP.add),
          [K["s2v"], "ones_t"], [K["s2v"]], tg + "_s")
        O("dve", lambda e, W=W: e.tensor_mul(W["ufm"][:], W["s2v"][:],
                                             W["expng"][:]),
          [K["s2v"], K["expng"]], [K["ufm"]], tg + "_u")

        def tr1(eng, W=W):
            ins = None
            for blk in range(2):
                ins = eng.matmul(ps[2][:, blk * 128:(blk + 1) * 128],
                                 W["k2"][:, blk * 128:(blk + 1) * 128],
                                 ident[:], is_transpose=True,
                                 start=(blk == 0), stop=False)
            for blk in range(2):
                ins = eng.matmul(ps[2][:, 256 + blk * 128:
                                       256 + (blk + 1) * 128],
                                 W["ufm"][:, blk * 128:(blk + 1) * 128],
                                 ident[:], is_transpose=True,
                                 start=False, stop=(blk == 1))
            return ins
        O("pe", tr1, [K["k2"], K["ufm"], "ident"], ["ps2"], tg + "_tr1")
        if DBG_SPLIT_EVAC:
            O("dve", lambda e, W=W: e.tensor_copy(W["ktm"][:],
                                                  ps[2][:, 0:256]),
              ["ps2"], [K["ktu"]], tg + "_ktuc")
            O("dve", lambda e, W=W: e.tensor_copy(W["utm"][:],
                                                  ps[2][:, 256:512]),
              ["ps2"], [K["ktu"]], tg + "_ktuc2")
        else:
            O("dve", lambda e, W=W: e.tensor_copy(W["ktu"][:],
                                                  ps[2][:, 0:512]),
              ["ps2"], [K["ktu"]], tg + "_ktuc")

        def tr2(eng, W=W):
            ins = None
            for blk in range(2):
                ins = eng.matmul(ps[3][:, blk * 128:(blk + 1) * 128],
                                 W["expg"][:, blk * 128:(blk + 1) * 128],
                                 ident[:], is_transpose=True,
                                 start=(blk == 0), stop=False)
            for blk in range(2):
                ins = eng.matmul(ps[3][:, 256 + blk * 128:
                                       256 + (blk + 1) * 128],
                                 W["v2"][:, blk * 128:(blk + 1) * 128],
                                 ident[:], is_transpose=True,
                                 start=False, stop=(blk == 1))
            return ins
        O("pe", tr2, [K["expg"], K["v2"], "ident"], ["ps3"], tg + "_tr2")
        if DBG_SPLIT_EVAC:
            O("dve", lambda e, W=W: e.tensor_copy(W["egtm"][:],
                                                  ps[3][:, 0:256]),
              ["ps3"], [K["egv"]], tg + "_egvc")
            O("dve", lambda e, W=W: e.tensor_copy(W["vtm"][:],
                                                  ps[3][:, 256:512]),
              ["ps3"], [K["egv"]], tg + "_egvc2")
        else:
            O("dve", lambda e, W=W: e.tensor_copy(W["egv"][:],
                                                  ps[3][:, 0:512]),
              ["ps3"], [K["egv"]], tg + "_egvc")
        def mmA(eng, W=W):
            ins = None
            for blk in range(2):
                ins = eng.matmul(ps[0][:, 0:128],
                                 W["k2"][:, blk * 128:(blk + 1) * 128],
                                 W["q2"][:, blk * 128:(blk + 1) * 128],
                                 start=(blk == 0), stop=(blk == 1))
            return ins
        O("pe", mmA, [K["k2"], K["q2"]], ["ps0a"], tg + "_A")
        O("dve", lambda e, W=W: e.tensor_mul(W["asb"][:], ps[0][:, 0:128],
                                             tri[:]),
          ["ps0a", "tri"], [K["asb"]], tg + "_Am")
        O("pe", lambda e, W=W: e.matmul(ps[1][:, 0:256], W["asb"][:],
                                        W["utm"][:], start=True, stop=True),
          [K["asb"], K["ktu"]], ["ps1a"], tg + "_ok")

        def mmoki(eng, W=W, S1=S1):
            ins = None
            for blk in range(2):
                ins = eng.matmul(ps[6][:, 0:256],
                                 W["q2"][:, blk * 128:(blk + 1) * 128],
                                 S1[:, blk * 256:(blk + 1) * 256],
                                 start=(blk == 0), stop=(blk == 1))
            return ins
        O("pe", mmoki, [K["q2"], s1k], ["ps6a"], tg + "_oki")
        O("dve", lambda e, W=W: e.tensor_mul(W["qvtm"][:], ps[1][:, 0:256],
                                             W["egtm"][:]),
          ["ps1a", K["egv"]], [K["qvtm"]], tg + "_oksc")
        O("dve", lambda e, W=W: e.tensor_mul(W["sc"][:], ps[6][:, 0:256],
                                             W["egtm"][:]),
          ["ps6a", K["egv"]], [K["sc"]], tg + "_oksc2")
        O("dve", lambda e, W=W: e.tensor_add(W["qvtm"][:], W["qvtm"][:],
                                             W["sc"][:]),
          [K["qvtm"], K["sc"]], [K["qvtm"]], tg + "_okadd")
        O("dve", lambda e, W=W: e.tensor_reduce(W["mx0"], W["qvtm"][:], AX,
                                                OP.max, negate=True),
          [K["qvtm"]], [K["mx0"]], tg + "_mx")
        O("act", lambda e, W=W: e.activation(W["qvtm"][:], W["qvtm"][:],
                                             ACT.Exp, bias=W["mx0"],
                                             accum_out=W["mx1"]),
          [K["qvtm"], K["mx0"]], [K["qvtm"], K["mx1"]], tg + "_sme")
        O("dve", lambda e, W=W: e.reciprocal(W["mx2"], W["mx1"]),
          [K["mx1"]], [K["mx2"]], tg + "_smr")
        O("dve", lambda e, W=W: e.tensor_scalar(W["qvtm"][:], W["qvtm"][:],
                                                W["mx2"], None, OP.mult),
          [K["qvtm"], K["mx2"]], [K["qvtm"]], tg + "_smn")

        def tr3(eng, W=W):
            ins = None
            for blk in range(2):
                ins = eng.matmul(ps[7][:, blk * 128:(blk + 1) * 128],
                                 W["qvtm"][:, blk * 128:(blk + 1) * 128],
                                 ident[:], is_transpose=True,
                                 start=(blk == 0), stop=(blk == 1))
            return ins
        O("pe", tr3, [K["qvtm"], "ident"], ["ps7a"], tg + "_tr3")
        O("dve", lambda e, W=W: e.tensor_copy(W["qvfm"][:], ps[7][:, 0:256]),
          ["ps7a"], [K["qvfm"]], tg + "_qvfmc")
        O("dve", lambda e, W=W: e.tensor_mul(W["qtfm"][:], W["qvfm"][:],
                                             W["expg"][:]),
          [K["qvfm"], K["expg"]], [K["qtfm"]], tg + "_qt")

        def mmA2(eng, W=W):
            ins = None
            for blk in range(2):
                ins = eng.matmul(ps[0][:, 128:256],
                                 W["ufm"][:, blk * 128:(blk + 1) * 128],
                                 W["qtfm"][:, blk * 128:(blk + 1) * 128],
                                 start=(blk == 0), stop=(blk == 1))
            return ins
        O("pe", mmA2, [K["ufm"], K["qtfm"]], ["ps0b"], tg + "_A2")
        O("dve", lambda e, W=W: e.tensor_mul(W["a2sb"][:], ps[0][:, 128:256],
                                             tri[:]),
          ["ps0b", "tri"], [K["a2sb"]], tg + "_A2m")
        O("pe", lambda e, W=W: e.matmul(ps[1][:, 256:512], W["a2sb"][:],
                                        W["vtm"][:], start=True, stop=True),
          [K["a2sb"], K["egv"]], ["ps1b"], tg + "_ov")

        def mmovi(eng, W=W, S2=S2):
            ins = None
            for blk in range(2):
                ins = eng.matmul(ps[6][:, 256:512],
                                 W["qtfm"][:, blk * 128:(blk + 1) * 128],
                                 S2[:, blk * 256:(blk + 1) * 256],
                                 start=(blk == 0), stop=(blk == 1))
            return ins
        O("pe", mmovi, [K["qtfm"], s2k], ["ps6b"], tg + "_ovi")
        O("dve", lambda e, W=W: e.tensor_copy(W["ovtm"][:],
                                              ps[1][:, 256:512]),
          ["ps1b"], [K["ovtm"]], tg + "_ovc")
        O("dve", lambda e, W=W: e.tensor_add(W["ovtm"][:], W["ovtm"][:],
                                             ps[6][:, 256:512]),
          ["ps6b", K["ovtm"]], [K["ovtm"]], tg + "_ovc2")

        def mmku(eng, W=W):
            ins = None
            for blk in range(2):
                ins = eng.matmul(ps[5][:, blk * 256:(blk + 1) * 256],
                                 W["ktm"][:, blk * 128:(blk + 1) * 128],
                                 W["utm"][:], start=True, stop=True)
            return ins
        O("pe", mmku, [K["ktu"]], ["ps5"], tg + "_ku")
        DM(W["fr"], W["egtm"][127:128, 0:256], [K["egv"]], [K["fr"]],
           tg + "_frd")
        O("pe", lambda e, W=W: e.matmul(ps[0][:, 256:512], ones_m[:],
                                        W["fr"], start=True, stop=True),
          ["ones_m", K["fr"]], ["ps0f"], tg + "_fb")
        if DBG_SPLIT_SADD:
            for blk in range(2):
                O("dve", lambda e, S1=S1, blk=blk: e.tensor_add(
                      S1[:, blk * 256:(blk + 1) * 256],
                      S1[:, blk * 256:(blk + 1) * 256],
                      ps[5][:, blk * 256:(blk + 1) * 256]),
                  [s1k, "ps5"], [s1k], tg + "_s1a%d" % blk)
        else:
            O("dve", lambda e, S1=S1: e.tensor_add(S1, S1, ps[5][:, 0:512]),
              [s1k, "ps5"], [s1k], tg + "_s1a")
        for blk in range(2):
            O("dve", lambda e, S1=S1, blk=blk: e.tensor_mul(
                  S1[:, blk * 256:(blk + 1) * 256],
                  S1[:, blk * 256:(blk + 1) * 256], ps[0][:, 256:512]),
              [s1k, "ps0f"], [s1k], tg + "_s1m%d" % blk)

        def mmku2(eng, W=W):
            ins = None
            for blk in range(2):
                ins = eng.matmul(ps[4][:, blk * 256:(blk + 1) * 256],
                                 W["utm"][:, blk * 128:(blk + 1) * 128],
                                 W["vtm"][:], start=True, stop=True)
            return ins
        O("pe", mmku2, [K["ktu"], K["egv"]], ["ps4"], tg + "_ku2")
        if DBG_SPLIT_SADD:
            for blk in range(2):
                O("dve", lambda e, S2=S2, blk=blk: e.tensor_add(
                      S2[:, blk * 256:(blk + 1) * 256],
                      S2[:, blk * 256:(blk + 1) * 256],
                      ps[4][:, blk * 256:(blk + 1) * 256]),
                  [s2k, "ps4"], [s2k], tg + "_s2a%d" % blk)
        else:
            O("dve", lambda e, S2=S2: e.tensor_add(S2, S2, ps[4][:, 0:512]),
              [s2k, "ps4"], [s2k], tg + "_s2a")
        for blk in range(2):
            egcol = W["expg"][:, blk * 128 + 127: blk * 128 + 128]
            O("dve", lambda e, S2=S2, blk=blk, egcol=egcol: e.tensor_scalar(
                  S2[:, blk * 256:(blk + 1) * 256],
                  S2[:, blk * 256:(blk + 1) * 256], egcol, None, OP.mult),
              [s2k, K["expg"]], [s2k], tg + "_s2m%d" % blk)

        def tr4(eng, W=W):
            ins = None
            for blk in range(2):
                ins = eng.matmul(ps[7][:, 256 + blk * 128:
                                       256 + (blk + 1) * 128],
                                 W["ovtm"][:, blk * 128:(blk + 1) * 128],
                                 ident[:], is_transpose=True,
                                 start=(blk == 0), stop=(blk == 1))
            return ins
        O("pe", tr4, [K["ovtm"], "ident"], ["ps7b"], tg + "_tr4")
        O("dve", lambda e, h=h: e.tensor_copy(ofm[:, h * 256:(h + 1) * 256],
                                              ps[7][:, 256:512]),
          ["ps7b"], ["c_ofm#%d" % h], tg + "_ofmc")
        return L

    for s_ in range(2):
        b.op("dve", lambda e: e.memset(t[2][:], 0.0),
             writes=["c_s1_%d" % hh for hh in range(4)]
             + ["c_s2_%d" % hh for hh in range(4)],
             tag="c2_z%d" % s_)
        for c_ in range(SEQ // C):
            tt0 = s_ * SEQ + c_ * C
            for hpair in (0, 2):
                L0 = head_ops(s_, c_, hpair)
                L1 = head_ops(s_, c_, hpair + 1)
                if ZIP:
                    for i in range(max(len(L0), len(L1))):
                        if i < len(L0):
                            L0[i]()
                        if i < len(L1):
                            L1[i]()
                else:
                    for f in L0 + L1:
                        f()
            tgc = "c2r_%d_%d" % (s_, c_)
            b.op("act", lambda e: e.activation(osil[:], ofm[:], ACT.Silu,
                                               bias=z128),
                 reads=["c_ofm#%d" % hh for hh in range(4)] + ["cvals"],
                 writes=["c_osil"], tag=tgc + "_sil")
            b.op("act", lambda e: e.square(osq[:], osil[:]),
                 reads=["c_osil"], writes=["c_osq"], tag=tgc + "_sq")
            def mmrs(eng):
                ins = None
                for c2 in range(8):
                    ins = eng.matmul(ps[6][0:1, 0:128], p.sb["ones_k"][:],
                                     osq[:, c2 * 128:(c2 + 1) * 128],
                                     start=(c2 == 0), stop=(c2 == 7))
                return ins
            b.op("pe", mmrs, reads=["c_osq", "ones_k"], writes=["ps6a"],
                 tag=tgc + "_ss")
            rr = stat[0:1, 0:128]
            b.op("act", lambda e: e.mul(rr, ps[6][0:1, 0:128], 1.0 / D),
                 reads=["ps6a"], writes=["stat"], tag=tgc + "_m")
            b.op("act", lambda e: e.activation(rr, rr, ACT.Sqrt,
                                               bias=p.cval(EPS)),
                 reads=["stat", "cvals"], writes=["stat"], tag=tgc + "_sd")
            b.op("dve", lambda e: e.reciprocal(rr, rr),
                 reads=["stat"], writes=["stat"], tag=tgc + "_r")
            b.op("pe", lambda e: e.matmul(ps[7][:, 0:128], ones_m[:], rr,
                                          start=True, stop=True),
                 reads=["stat", "ones_m"], writes=["ps7a"],
                 tag=tgc + "_bc")
            for c2 in range(8):
                sl = osil[:, c2 * 128:(c2 + 1) * 128]
                b.op("dve", lambda e, sl=sl: e.tensor_mul(sl, sl,
                                                          ps[7][:, 0:128]),
                     reads=["c_osil", "ps7a"], writes=["c_osil"],
                     tag=tgc + "_rm%d" % c2)
                b.op("dve", lambda e, sl=sl, c2=c2: e.tensor_scalar(
                         sl, sl, gw[:, c2:c2 + 1], None, OP.mult),
                     reads=["c_osil", "c_gw"], writes=["c_osil"],
                     tag=tgc + "_gw%d" % c2)
            store_fm(p, "pool", OS, "o_scr", osil, "c_osil", tt0, C,
                     tag=tgc + "_st", chunked=False)

    b.barrier("c3_bar")
    TW3 = min(256, NTOK)
    banks3 = std_banks(p, TW3)
    load_w(p, "sp", wbig[:, 0:NCH * D], "wbig", wo_h, D, D, tag="c_ldwo")
    for it in range(NTOK // TW3):
        t0_ = it * TW3
        par = it % 2
        x_t = t[0][:, par * NCH * TW3:(par + 1) * NCH * TW3]
        xk = "c3_x%d" % par
        o_t = t[1][:, par * NCH * TW3:(par + 1) * NCH * TW3]
        ok_ = "c3_o%d" % par
        y_t = t[2][:, par * NCH * TW3:(par + 1) * NCH * TW3]
        yk = "c3_y%d" % par
        xo_t, xok = o_t, ok_
        load_fm(p, "sp", x_t, xk, src_hbm, t0_, TW3, tag="c3_ldx%d" % it)
        load_fm(p, "sp", o_t, ok_, OS, t0_, TW3, tag="c3_ldo%d" % it)
        gb = [ps[2 * par], ps[2 * par + 1]]
        gbk = ["ps%d" % (2 * par), "ps%d" % (2 * par + 1)]
        for o in range(NCH):
            bank = gb[o % 2]
            bkey = gbk[o % 2]

            def mm(eng, o=o, bank=bank, o_t=o_t, TW3=TW3):
                ins = None
                for c in range(NCH):
                    col = c * D + o * 128
                    ins = eng.matmul(bank[:, 0:TW3], wbig[:, col:col + 128],
                                     o_t[:, c * TW3:(c + 1) * TW3],
                                     start=(c == 0), stop=(c == NCH - 1))
                return ins

            b.op("pe", mm, reads=["wbig"] + ck(ok_), writes=[bkey],
                 tag="c3_mm%d_%d" % (it, o))
            b.op("act", lambda e, o=o, bank=bank, y_t=y_t, TW3=TW3: e.copy(
                     y_t[:, o * TW3:(o + 1) * TW3], bank[:, 0:TW3]),
                 reads=[bkey], writes=[yk + "#%d" % o],
                 tag="c3_cp%d_%d" % (it, o))
        emit_ln(p, y_t, yk, TW3, banks3, stat, "stat", tag="c3_ln%d" % it)
        ln_apply_residual(p, y_t, yk, x_t, xk, xo_t, xok, TW3, banks3,
                          tag="c3_res%d" % it)
        store_fm(p, "pool", dst_hbm, dst_key, xo_t, xok, t0_, TW3,
                 tag="c3_st%d" % it)

def emit_phase_d(p, src, dst):
    src_hbm, src_key = src
    dst_hbm, dst_key = dst
    b = p.b
    ps = [p.sb[f"ps{i}"] for i in range(8)]
    t = [p.sb[f"t{i}"] for i in range(3)]
    stat = p.sb["stat"]
    wbig = p.sb["wbig"]
    ones_m = p.sb["ones_m"]
    ident = p.sb["ident"]
    hb = p.sb["hb"]
    lb = p.sb["lb"]
    mb = p.sb["mb"]
    u0 = p.sb["u0"]
    wq16 = p.sb["wq16"]
    HS = p.dram_tmp("hs", [2 * 4096, NTOK], BF16)
    WQF = p.dram_tmp("wqf", [D, 4096], BF16)
    WQP = p.dram_tmp("wqp", [4096, D], BF16)
    wfc_h = p.dram_in("wfc_t", [D, 4096], F32)
    wpr_h = p.dram_in("wpr_t", [4096, D], F32)
    bfc_v = p.sbuf("d_bfcv", [128, 32], F32)
    load_vec(p, "sp", bfc_v[:], "d_bfcv", p.dram_in("b_fc", [128, 32], F32))
    bpr_v = p.sbuf("d_bprv", [128, 8], F32)
    load_vec(p, "sp", bpr_v[:], "d_bprv", p.dram_in("b_pr", [128, 8], F32))
    sqt = p.sb["sq_t"]
    afc = t[1][:, 1152:1152 + NCH * 16]
    bfc_h = p.dram_in("bfc_t", [16, 4096], F32)
    bfc_sl = sqt[0:16, 2560:3072]
    apr = t[1][:, 640:640 + 32 * 16]
    bpr_h = p.dram_in("bpr_t", [16, D], F32)
    bpr_sl = p.sbuf("d_bprs", [16, 256], F32)[:]
    z_sb = p.sbuf("d_z2", [16, 512], F32)[:]
    dsc = p.sbuf("d_sc", [1, 8], F32)
    dscb = p.sbuf("d_scb", [128, 4], F32)
    mx1 = p.sbuf("d_mx", [128, 2], F32)
    rowt = sqt[0:1, 3328:3456]
    MAGIC = float(3 * 2 ** 22)
    cmag = p.cval(MAGIC, 128)
    c127 = p.cval(127.0, 128)
    cn128 = p.cval(-128.0, 128)

    b.barrier("d0_bar")
    for wi, (wh, wq_scr, wq_key, rows, cols, sc_i) in enumerate([
            (wfc_h, WQF, "wqf", D, 4096, 0),
            (wpr_h, WQP, "wqp", 4096, D, 2)]):
        nld = (rows // 128) * cols // 16384
        b.op("dve", lambda e: e.memset(mx1[:, 0:1], 0.0), writes=["d_mx"],
             tag="d0_z%d" % wi)
        ncol_per = 16384 // (rows // 128)
        for j in range(nld):
            load_w(p, "sp", wbig[:], "wbig", wh, rows, cols,
                   col0=j * ncol_per, ncols=ncol_per, tag="d0_ld%d_%d" % (wi, j))
            b.op("dve", lambda e: e.tensor_reduce(mx1[:, 1:2], wbig[:], AX,
                                                  OP.max,
                                                  apply_absolute_value=True),
                 reads=["wbig"], writes=["d_mx"], tag="d0_mx%d_%d" % (wi, j))
            b.op("dve", lambda e: e.tensor_max(mx1[:, 0:1], mx1[:, 0:1],
                                               mx1[:, 1:2]),
                 reads=["d_mx"], writes=["d_mx"], tag="d0_acc%d_%d" % (wi, j))
        b.op("pe", lambda e: e.matmul(ps[3][0:1, 0:128], mx1[:, 0:1], ident[:],
                                      is_transpose=True, start=True, stop=True),
             reads=["d_mx", "ident"], writes=["ps3"], tag="d0_tr%d" % wi)
        b.op("dve", lambda e: e.tensor_copy(rowt, ps[3][0:1, 0:128]),
             reads=["ps3"], writes=["d_row"], tag="d0_rc%d" % wi)
        b.op("dve", lambda e, sc_i=sc_i: e.tensor_reduce(
                 dsc[0:1, sc_i:sc_i + 1], rowt, AX, OP.max,
                 apply_absolute_value=True),
             reads=["d_row"], writes=["d_sc"], tag="d0_rm%d" % wi)
        b.op("act", lambda e, sc_i=sc_i: e.mul(dsc[0:1, sc_i:sc_i + 1],
                                               dsc[0:1, sc_i:sc_i + 1],
                                               1.0 / 127.0),
             reads=["d_sc"], writes=["d_sc"], tag="d0_s%d" % wi)
        b.op("dve", lambda e, sc_i=sc_i: e.reciprocal(
                 dsc[0:1, sc_i + 1:sc_i + 2], dsc[0:1, sc_i:sc_i + 1]),
             reads=["d_sc"], writes=["d_sc"], tag="d0_r%d" % wi)
    b.op("pe", lambda e: e.matmul(ps[3][:, 0:4], ones_m[:], dsc[0:1, 0:4],
                                  start=True, stop=True),
         reads=["d_sc", "ones_m"], writes=["ps3"], tag="d0_bc")
    b.op("dve", lambda e: e.tensor_copy(dscb[:], ps[3][:, 0:4]),
         reads=["ps3"], writes=["d_scb"], tag="d0_bcc")
    for wi, (wh, wq_scr, wq_key, rows, cols, sc_i) in enumerate([
            (wfc_h, WQF, "wqf", D, 4096, 0),
            (wpr_h, WQP, "wqp", 4096, D, 2)]):
        nld = (rows // 128) * cols // 16384
        ncol_per = 16384 // (rows // 128)
        for j in range(nld):
            load_w(p, "sp", wbig[:], "wbig", wh, rows, cols,
                   col0=j * ncol_per, ncols=ncol_per, tag="d0q_ld%d_%d" % (wi, j))
            b.op("dve", lambda e, sc_i=sc_i: e.tensor_scalar(
                     wbig[:], wbig[:], dscb[:, sc_i + 1:sc_i + 2], None,
                     OP.mult),
                 reads=["wbig", "d_scb"], writes=["wbig"],
                 tag="d0q_m%d_%d" % (wi, j))
            b.op("dve", lambda e: e.tensor_scalar(wbig[:], wbig[:], cmag,
                                                  None, OP.add),
                 reads=["wbig", "cvals"], writes=["wbig"],
                 tag="d0q_a%d_%d" % (wi, j))
            b.op("dve", lambda e: e.tensor_scalar(wbig[:], wbig[:], cmag,
                                                  None, OP.subtract),
                 reads=["wbig", "cvals"], writes=["wbig"],
                 tag="d0q_s%d_%d" % (wi, j))
            b.op("dve", lambda e: e.tensor_scalar(wbig[:], wbig[:], c127,
                                                  None, OP.min),
                 reads=["wbig", "cvals"], writes=["wbig"],
                 tag="d0q_c1%d_%d" % (wi, j))
            b.op("dve", lambda e: e.tensor_scalar(wbig[:], wbig[:], cn128,
                                                  None, OP.max),
                 reads=["wbig", "cvals"], writes=["wbig"],
                 tag="d0q_c2%d_%d" % (wi, j))
            nchj = rows // 256
            for hf in range(2):
                sl = wbig[:, hf * 8192:(hf + 1) * 8192]
                w16 = p.sb["d0w16"]
                b.op("dve", lambda e, sl=sl, w16=w16: e.tensor_copy(w16[:], sl),
                     reads=["wbig"], writes=["d0_w16"],
                     tag="d0q_cv%d_%d_%d" % (wi, j, hf))
                dstw = wq_scr.rearrange("(c p) o -> p c o", p=128)[
                    :, hf * nchj:(hf + 1) * nchj,
                    j * ncol_per:(j + 1) * ncol_per]
                srcw = w16[:].rearrange("p (c o) -> p c o", c=nchj)
                b.dma("pool", dstw, srcw, reads=["d0_w16"],
                      writes=["%s#%d_%d" % (wq_key, j, hf)],
                      tag="d0q_st%d_%d_%d" % (wi, j, hf))
    b.barrier("d0e_bar")
    load_w(p, "sp", afc, "d_afc", p.dram_in("afc_t", [D, 16], F32), D, 16)
    load_w(p, "sp", apr, "d_apr", p.dram_in("apr_t", [4096, 16], F32),
           4096, 16)
    afcs = t[1][:, 0:NCH * 16]
    afch = mb[:, 0:NCH * 16]
    afcl = mb[:, NCH * 16:2 * NCH * 16]
    b.op("dve", lambda e: e.tensor_scalar(afcs, afc, dscb[:, 1:2],
                                          None, OP.mult),
         reads=["d_afc", "d_scb"], writes=["d0_afcs"], tag="d0_afs")
    b.op("dve", lambda e: e.tensor_copy(afch, afcs),
         reads=["d0_afcs"], writes=["d_afch"], tag="d0_afh")
    b.op("dve", lambda e: e.tensor_tensor(afcl, afcs, afch,
                                          OP.subtract),
         reads=["d0_afcs", "d_afch"], writes=["d_afcl"], tag="d0_afl")
    aprs = t[1][:, 128:128 + 32 * 16]
    aprh = p.sbuf("d_aprh", [128, 32 * 16], BF16)
    aprl = p.sbuf("d_aprl", [128, 32 * 16], BF16)
    
    b.op("dve", lambda e: e.tensor_scalar(aprs, apr, dscb[:, 3:4],
                                          None, OP.mult),
         reads=["d_apr", "d_scb"], writes=["d0_aprs"], tag="d0_aps")
    b.op("dve", lambda e: e.tensor_copy(aprh[:], aprs),
         reads=["d0_aprs"], writes=["d_aprh"], tag="d0_aph")
    b.op("dve", lambda e: e.tensor_tensor(aprl[:], aprs, aprh[:],
                                          OP.subtract),
         reads=["d0_aprs", "d_aprh"], writes=["d_aprl"], tag="d0_apl")

    b.barrier("d1_bar")
    TW = 512
    nt = NTOK // TW
    banks = std_banks(p, TW)
    x_t, xk = t[0][:, :NCH * TW], "d1_x"
    HSr = HS.rearrange("(c p) t -> p c t", p=128)
    hbs = [hb, p.sb["hbB"]]
    lbs = [lb, p.sb["lbB"]]
    for it in range(nt):
        t0_ = it * TW
        itp = it % 2
        hbt, lbt = hbs[itp], lbs[itp]
        hikeys = ["d1_hi%d#%d" % (itp, c) for c in range(NCH)]
        lokeys = ["d1_lo%d#%d" % (itp, c) for c in range(NCH)]
        load_fm(p, "sp", x_t, xk, src_hbm, t0_, TW, tag="d1_ldx%d" % it)
        emit_ln(p, x_t, xk, TW, banks, stat, "stat", tag="d1_ln%d" % it,
                scale_ap=dsc[0:1, 0:1], scale_key="d_sc",
                sq=t[1][:], sqk="d1_t32")
        for c in range(NCH):
            xs_ = x_t[:, c * TW:(c + 1) * TW]
            tmp = t[1][:, c * TW:(c + 1) * TW]
            hic = hbt[:, c * TW:(c + 1) * TW]
            loc = lbt[:, c * TW:(c + 1) * TW]
            tk = "d1_t32#%d" % c
            b.op("dve", lambda e, tmp=tmp, xs_=xs_: e.tensor_mul(
                     tmp, xs_, banks["br"]),
                 reads=[xk + "#%d" % c, banks["kr"]], writes=[tk],
                 tag="d1_sp%d_m%d" % (it, c))
            b.op("dve", lambda e, tmp=tmp: e.tensor_sub(tmp, tmp,
                                                        banks["bm"]),
                 reads=[tk, banks["km"]], writes=[tk],
                 tag="d1_sp%d_s%d" % (it, c))
            b.op("dve", lambda e, hic=hic, tmp=tmp: e.tensor_copy(hic, tmp),
                 reads=[tk], writes=[hikeys[c]],
                 tag="d1_sp%d_h%d" % (it, c))
            b.op("dve", lambda e, loc=loc, tmp=tmp, hic=hic: e.tensor_tensor(
                     loc, tmp, hic, OP.subtract),
                 reads=[tk, hikeys[c]], writes=[lokeys[c]],
                 tag="d1_sp%d_l%d" % (it, c))
        def mmz(eng, TW=TW, hbt=hbt, lbt=lbt):
            ins = None
            for c in range(NCH):
                ins = eng.matmul(ps[3][0:16, 0:TW], afch[:, c * 16:(c + 1) * 16],
                                 hbt[:, c * TW:(c + 1) * TW],
                                 start=(c == 0), stop=False)
            for c in range(NCH):
                ins = eng.matmul(ps[3][0:16, 0:TW], afch[:, c * 16:(c + 1) * 16],
                                 lbt[:, c * TW:(c + 1) * TW],
                                 start=False, stop=False)
            for c in range(NCH):
                ins = eng.matmul(ps[3][0:16, 0:TW],
                                 afcl[:, c * 16:(c + 1) * 16],
                                 hbt[:, c * TW:(c + 1) * TW],
                                 start=False, stop=(c == NCH - 1))
            return ins
        b.op("pe", mmz, reads=["d_afch", "d_afcl"] + hikeys + lokeys,
             writes=["ps3"], tag="d1_z%d" % it)
        b.op("act", lambda e: e.mul(z_sb[:, 0:TW], ps[3][0:16, 0:TW], 2.0),
             reads=["ps3"], writes=["d_z"], tag="d1_zc%d" % it)
        for o in range(32):
            slot = o % 8
            wreg = wq16[:, slot * 1024:slot * 1024 + 1024]
            wkey = "wq16#%d" % slot
            load_w(p, "sp", wreg, wkey, WQF, D, 4096, col0=o * 128, ncols=128,
                   tag="d1_lw%d_%d" % (it, o))
            s4 = o % 4
            b.dma("sp", bfc_sl[:, s4 * 128:(s4 + 1) * 128],
                  bfc_h[:, o * 128:(o + 1) * 128], reads=["d_bfc#%d" % s4],
                  writes=["d_bfc#%d" % s4], tag="d1_lb%d_%d" % (it, o))
            bank = ps[o % 3]
            bkey = "ps%d" % (o % 3)

            def mm(eng, o=o, bank=bank, wreg=wreg, TW=TW, hbt=hbt, lbt=lbt):
                ins = None
                for c in range(NCH):
                    ins = eng.matmul(bank[:, 0:TW],
                                     wreg[:, c * 128:(c + 1) * 128],
                                     hbt[:, c * TW:(c + 1) * TW],
                                     start=(c == 0), stop=False)
                    ins = eng.matmul(bank[:, 0:TW],
                                     wreg[:, c * 128:(c + 1) * 128],
                                     lbt[:, c * TW:(c + 1) * TW],
                                     start=False, stop=False)
                s4 = o % 4
                ins = eng.matmul(bank[:, 0:TW],
                                 bfc_sl[:, s4 * 128:(s4 + 1) * 128],
                                 z_sb[0:16, 0:TW], start=False, stop=True)
                return ins

            b.op("pe", mm,
                 reads=[wkey, "d_bfc#%d" % (o % 4), "d_z"] + hikeys + lokeys,
                 writes=[bkey], tag="d1_mm%d_%d" % (it, o))
            par = o % 4
            h_t = t[2][:, par * 512:par * 512 + 512]
            hk = "d1_h%d" % par
            b.op("act", lambda e, bank=bank, h_t=h_t, o=o: e.activation(
                     h_t, bank[:, 0:TW], ACT.Gelu,
                     bias=bfc_v[:, o:o + 1]),
                 reads=[bkey, "d_bfcv"], writes=[hk], tag="d1_g%d_%d" % (it, o))
            hs32 = t[2][:, 2048 + par * 512:2048 + par * 512 + 512]
            hsk = "d1_hs%d" % par
            b.op("dve", lambda e, hs32=hs32, h_t=h_t: e.tensor_scalar(
                     hs32, h_t, dscb[:, 2:3], None, OP.mult),
                 reads=[hk, "d_scb"], writes=[hsk],
                 tag="d1_hm%d_%d" % (it, o))
            hi8 = u0[:, par * 1024:par * 1024 + 512]
            lo8 = u0[:, par * 1024 + 512:par * 1024 + 1024]
            h8k = "d1_h8%d" % par
            l8k = "d1_l8%d" % par
            b.op("dve", lambda e, hi8=hi8, hs32=hs32: e.tensor_copy(hi8, hs32),
                 reads=[hsk], writes=[h8k], tag="d1_hh%d_%d" % (it, o))
            b.op("dve", lambda e, lo8=lo8, hs32=hs32, hi8=hi8: e.tensor_tensor(
                     lo8, hs32, hi8, OP.subtract),
                 reads=[hsk, h8k], writes=[l8k], tag="d1_hl%d_%d" % (it, o))
            b.dma("pool", HSr[:, o:o + 1, t0_:t0_ + TW],
                  hi8.rearrange("p (c t) -> p c t", c=1),
                  reads=[h8k], writes=["hs_%d_%d_h" % (it, o)],
                  tag="d1_sth%d_%d" % (it, o))
            b.dma("pool", HSr[:, 32 + o:33 + o, t0_:t0_ + TW],
                  lo8.rearrange("p (c t) -> p c t", c=1),
                  reads=[l8k], writes=["hs_%d_%d_l" % (it, o)],
                  tag="d1_stl%d_%d" % (it, o))

    b.barrier("d2_bar")
    TW2 = 256
    nt2 = NTOK // TW2
    banks2 = std_banks(p, TW2)
    wq16d = p.sb["wq16d"]
    d2hs = [p.sb["d2h0"], p.sb["d2h1"]]
    d2ls = [p.sb["d2l0"], p.sb["d2l1"]]
    for it in range(nt2):
        t0_ = it * TW2
        itp = it % 2
        d2h, d2l = d2hs[itp], d2ls[itp]
        x2_t = t[2][:, itp * NCH * TW2:(itp + 1) * NCH * TW2]
        x2k = "d2_x%d" % itp
        y2_t = t[1][:, itp * NCH * TW2:(itp + 1) * NCH * TW2]
        y2k = "d2_y%d" % itp
        xo2_t, xo2k = y2_t, y2k
        hikeys2 = ["d2_hi%d#%d" % (itp, c) for c in range(32)]
        lokeys2 = ["d2_lo%d#%d" % (itp, c) for c in range(32)]
        b.dma("sp", d2h[:, 0:32 * TW2].rearrange("p (c t) -> p c t", c=32),
              HSr[:, 0:32, t0_:t0_ + TW2],
              reads=[], writes=hikeys2, tag="d2_lh_%d" % it)
        b.dma("sp", d2l[:, 0:32 * TW2].rearrange("p (c t) -> p c t", c=32),
              HSr[:, 32:64, t0_:t0_ + TW2],
              reads=[], writes=lokeys2, tag="d2_ll_%d" % it)
        load_fm(p, "sp", x2_t, x2k, src_hbm, t0_, TW2, tag="d2_ldx%d" % it)
        def mmz2(eng, TW2=TW2, d2h=d2h, d2l=d2l):
            ins = None
            for c in range(32):
                ins = eng.matmul(ps[3][0:16, 0:TW2],
                                 aprh[:, c * 16:(c + 1) * 16],
                                 d2h[:, c * TW2:(c + 1) * TW2],
                                 start=(c == 0), stop=False)
            for c in range(32):
                ins = eng.matmul(ps[3][0:16, 0:TW2],
                                 aprh[:, c * 16:(c + 1) * 16],
                                 d2l[:, c * TW2:(c + 1) * TW2],
                                 start=False, stop=False)
            for c in range(32):
                ins = eng.matmul(ps[3][0:16, 0:TW2],
                                 aprl[:, c * 16:(c + 1) * 16],
                                 d2h[:, c * TW2:(c + 1) * TW2],
                                 start=False, stop=(c == 31))
            return ins
        b.op("pe", mmz2, reads=["d_aprh", "d_aprl"] + hikeys2 + lokeys2,
             writes=["ps3"], tag="d2_z%d" % it)
        b.op("act", lambda e: e.mul(z_sb[:, 0:TW2], ps[3][0:16, 0:TW2], 2.0),
             reads=["ps3"], writes=["d_z"], tag="d2_zc%d" % it)
        for o in range(NCH):
            slot = o % 2
            wreg = wq16d[:, slot * 4096:(slot + 1) * 4096]
            wkey2 = "wq16d#%d" % slot
            load_w(p, "sp", wreg, wkey2, WQP, 4096, D, col0=o * 128,
                   ncols=128, tag="d2_lw%d_%d" % (it, o))
            s2_ = o % 2
            b.dma("sp", bpr_sl[:, s2_ * 128:(s2_ + 1) * 128],
                  bpr_h[:, o * 128:(o + 1) * 128], reads=["d_bpr#%d" % s2_],
                  writes=["d_bpr#%d" % s2_], tag="d2_lb%d_%d" % (it, o))
            bank = ps[o % 3]
            bkey = "ps%d" % (o % 3)

            def mm(eng, o=o, bank=bank, wreg=wreg, TW2=TW2, d2h=d2h, d2l=d2l):
                ins = None
                for c in range(32):
                    ins = eng.matmul(bank[:, 0:TW2],
                                     wreg[:, c * 128:(c + 1) * 128],
                                     d2h[:, c * TW2:(c + 1) * TW2],
                                     start=(c == 0), stop=False)
                    ins = eng.matmul(bank[:, 0:TW2],
                                     wreg[:, c * 128:(c + 1) * 128],
                                     d2l[:, c * TW2:(c + 1) * TW2],
                                     start=False, stop=False)
                s2_ = o % 2
                ins = eng.matmul(bank[:, 0:TW2],
                                 bpr_sl[:, s2_ * 128:(s2_ + 1) * 128],
                                 z_sb[0:16, 0:TW2], start=False, stop=True)
                return ins

            b.op("pe", mm,
                 reads=[wkey2, "d_bpr#%d" % (o % 2), "d_z"]
                 + hikeys2 + lokeys2,
                 writes=[bkey], tag="d2_mm%d_%d" % (it, o))
            b.op("act", lambda e, o=o, bank=bank, y2_t=y2_t, TW2=TW2:
                 e.activation(
                     y2_t[:, o * TW2:(o + 1) * TW2], bank[:, 0:TW2],
                     ACT.Identity, bias=bpr_v[:, o:o + 1]),
                 reads=[bkey, "d_bprv"], writes=[y2k + "#%d" % o],
                 tag="d2_y%d_%d" % (it, o))
        emit_ln(p, y2_t, y2k, TW2, banks2, stat, "stat",
                tag="d2_ln%d" % it)
        ln_apply_residual(p, y2_t, y2k, x2_t, x2k, xo2_t, xo2k, TW2, banks2,
                          tag="d2_res%d" % it)
        store_fm(p, "pool", dst_hbm, dst_key + "#st%d" % it, xo2_t, xo2k, t0_,
                 TW2, tag="d2_st%d" % it)
